# version 13
# speedup vs baseline: 1.4013x; 1.0138x over previous
import math
import numpy as np
import ml_dtypes
import concourse.bass as bass
import concourse.mybir as mybir
from concourse.bass_utils import run_bass_kernel_spmd

F32 = mybir.dt.float32
I32 = mybir.dt.int32
BF16 = mybir.dt.bfloat16
Alu = mybir.AluOpType
Act = mybir.ActivationFunctionType
AX = mybir.AxisListType.X
BF = ml_dtypes.bfloat16

NCORES = 8


def make_cfg(full=True):
    c = {}
    if full:
        c.update(S=4096, D=2048, H=16, HKV=4, INTER=5632)
    else:
        c.update(S=1024, D=512, H=4, HKV=1, INTER=1408)
    c["B"] = 2
    c["DH"] = 128
    c["CHUNK"] = 64
    c["TPC"] = c["B"] * c["S"] // NCORES
    c["TT"] = c["TPC"] // 128
    c["KT"] = c["D"] // 128
    c["IT"] = c["INTER"] // 128
    c["L1H"] = c["D"] // 128
    c["NCH"] = c["TPC"] // c["CHUNK"]
    c["sublayers"] = ["ffn0a", "attn0", "ffn0b", "ffn1a", "attn1", "ffn1b"]
    c["taps"] = []
    return c



def _lhsT_tiles(w_T):
    K, N = w_T.shape
    a = w_T.reshape(K // 128, 128, N // 128, 128)
    return np.ascontiguousarray(a.transpose(2, 1, 0, 3))


def _moving_tiles(w_T):
    K, N = w_T.shape
    return np.ascontiguousarray(w_T.reshape(K // 128, 128, N))


def _part_vec(v, tiles):
    return np.ascontiguousarray(np.asarray(v, np.float32).reshape(tiles, 128).T)


def host_prepare(cfg, I):
    B, S, D, H, HKV = cfg["B"], cfg["S"], cfg["D"], cfg["H"], cfg["HKV"]
    DH, CHUNK, TPC, TT, KT, IT = (cfg["DH"], cfg["CHUNK"], cfg["TPC"],
                                  cfg["TT"], cfg["KT"], cfg["IT"])
    NCH = cfg["NCH"]
    f32 = np.float32

    shared = {}
    gW = _lhsT_tiles(np.ascontiguousarray(np.asarray(I["L0_ffn_gate_w"], f32).T))
    uW = _lhsT_tiles(np.ascontiguousarray(np.asarray(I["L0_ffn_up_w"], f32).T))
    shared["w_gu"] = np.ascontiguousarray(np.stack([gW, uW], axis=2))
    shared["w_down"] = _moving_tiles(
        np.ascontiguousarray(np.asarray(I["L0_ffn_down_w"], f32).T))
    shared["gate_s"] = _part_vec(I["L0_ffn_gate_s"], IT)
    shared["up_s"] = _part_vec(I["L0_ffn_up_s"], IT)
    shared["down_s"] = np.asarray(I["L0_ffn_down_s"], f32).reshape(1, D)

    shared["w_q0"] = _lhsT_tiles(np.ascontiguousarray(np.asarray(I["L0_attn_wq"], f32).T))
    shared["w_k0"] = _lhsT_tiles(np.ascontiguousarray(np.asarray(I["L0_attn_wk"], f32).T))
    shared["w_v0"] = _lhsT_tiles(np.ascontiguousarray(np.asarray(I["L0_attn_wv"], f32).T))
    shared["w_o0"] = _moving_tiles(np.ascontiguousarray(np.asarray(I["L0_attn_wo"], f32).T))

    for nm in ("wq", "wk", "wv"):
        cd = np.asarray(I[f"L1_attn_{nm}_codes"], f32)
        shared[f"c_{nm}1"] = _lhsT_tiles(cd)
        sc = np.repeat(np.asarray(I[f"L1_attn_{nm}_scales"], f32), 32, axis=0)
        shared[f"s_{nm}1"] = _lhsT_tiles(sc)
    cd = np.asarray(I["L1_attn_wo_codes"], f32)
    shared["c_wo1"] = _moving_tiles(cd)
    sc = np.repeat(np.asarray(I["L1_attn_wo_scales"], f32), 32, axis=0)
    shared["s_wo1"] = _moving_tiles(sc)
    for nm in ("bq", "bk", "bv", "bo"):
        shared[f"{nm}1"] = np.asarray(I[f"L1_attn_{nm}"], f32).reshape(1, -1)

    shared["w_f1"] = np.ascontiguousarray(
        np.asarray(I["L1_ffn_weight"], f32).reshape(D // 128, 128, D))
    shared["b_f1"] = np.asarray(I["L1_ffn_bias"], f32).reshape(1, D)
    shared["ts_f1"] = np.asarray(I["L1_ffn_x_tensor_scale"], f32).reshape(1, 1)

    shared["ident"] = np.eye(128, dtype=BF)
    shared["ident32"] = np.eye(128, dtype=f32)
    shared["ones_row"] = np.ones((1, 128), dtype=BF)
    shared["ones32"] = np.ones((1, 128), f32)
    shared["onec_b"] = np.ones((128, 1), BF)
    shared["onec_f"] = np.ones((128, 1), f32)
    shared["cvals"] = np.broadcast_to(
        np.array([0.0, 1.0, 1e-5, math.log(2.0), -3.0 * math.log(2.0),
                  2.0 ** -3.5], f32), (128, 6)).copy()
    lam = (2.0 ** (-8.0 * np.arange(1, H + 1) / H)).astype(np.float64)
    r = np.arange(CHUNK, dtype=np.float64)
    diag = np.where(r[:, None] - r[None, :] >= 0,
                    np.exp(-lam[:, None, None] * (r[:, None] - r[None, :])), 0.0)
    dT = np.zeros((CHUNK, H * CHUNK), f32)
    for h in range(H):
        dT[:, h * CHUNK:(h + 1) * CHUNK] = diag[h].T
    shared["diagT"] = np.concatenate([dT, dT], axis=0)
    qdec = np.exp(-lam[:, None] * (r[None, :] + 1.0))
    kdec = np.exp(-lam[:, None] * (CHUNK - 1.0 - r[None, :]))
    shared["qdec"] = np.broadcast_to(
        qdec.reshape(1, H * CHUNK).astype(BF), (128, H * CHUNK)).copy()
    kd = np.zeros((128, H), f32)
    for h in range(H):
        kd[:, h] = np.tile(kdec[h], 2)
    shared["kdec"] = kd

    x = np.asarray(I["L0_attn_x"], f32).reshape(B * S, D)
    in_maps = []
    for core in range(NCORES):
        m = dict(shared)
        xs = x[core * TPC:(core + 1) * TPC]
        m["x_in"] = np.ascontiguousarray(xs.reshape(TT, 128, D))
        pos0 = (core % (NCORES // B)) * TPC
        t = np.arange(pos0, pos0 + TPC, dtype=np.float64)
        inv = 1.0 / 10000.0 ** (np.arange(0, DH, 2, dtype=np.float64) / DH)
        fr = t[None, :] * inv[:, None]
        m["ropec"] = np.concatenate([np.cos(fr), np.cos(fr)], 0).astype(BF)
        m["ropes"] = np.concatenate([-np.sin(fr), np.sin(fr)], 0).astype(BF)
        rank = core % 4
        blkspan = np.exp(-lam * CHUNK) ** NCH
        coef = np.zeros((4, H), np.float64)
        for j in range(rank):
            coef[j] = blkspan ** (rank - 1 - j)
        m["coef"] = np.broadcast_to(
            coef.reshape(1, 4 * H).astype(f32), (128, 4 * H)).copy()
        in_maps.append(m)
    return in_maps



class Ctx:

    def __init__(self, nc):
        self.nc = nc
        self.sem = {}
        self.cnt = {}
        self._cm = []

    def mksem(self, *names):
        for n in names:
            cm = self.nc.semaphore(n)
            self.sem[n] = cm.__enter__()
            self._cm.append(cm)
            self.cnt[n] = 0

    def inc(self, name, instr, n=1):
        instr.then_inc(self.sem[name], n)
        self.cnt[name] += n
        return self.cnt[name]

    def bump(self, name, n=1):
        self.cnt[name] += n
        return self.cnt[name]

    def w(self, eng, name, val):
        eng.wait_ge(self.sem[name], val)


def build(cfg):
    B, S, D, H, HKV = cfg["B"], cfg["S"], cfg["D"], cfg["H"], cfg["HKV"]
    DH, CHUNK, TPC, TT, KT, IT = (cfg["DH"], cfg["CHUNK"], cfg["TPC"],
                                  cfg["TT"], cfg["KT"], cfg["IT"])
    INTER, H1, NCH = cfg["INTER"], cfg["L1H"], cfg["NCH"]
    subs = cfg["sublayers"]
    taps = {}
    TC = [(o, min(512, TPC - o)) for o in range(0, TPC, 512)]
    NDC = D // 512 if D >= 512 else 1
    DC = [(o, min(512, D - o)) for o in range(0, D, 512)]

    nc = bass.Bass("TRN2", target_bir_lowering=False, debug=False)
    dt = nc.dram_tensor

    x_in = dt("x_in", [TT, 128, D], F32, kind="ExternalInput").ap()
    y_out = dt("y_out", [TT, 128, D], F32, kind="ExternalOutput").ap()
    w_gu_d = dt("w_gu", [IT, 128, 2, KT, 128], F32, kind="ExternalInput").ap()
    w_down_d = dt("w_down", [IT, 128, D], F32, kind="ExternalInput").ap()
    gate_s_d = dt("gate_s", [128, IT], F32, kind="ExternalInput").ap()
    up_s_d = dt("up_s", [128, IT], F32, kind="ExternalInput").ap()
    down_s_d = dt("down_s", [1, D], F32, kind="ExternalInput").ap()
    ident_d = dt("ident", [128, 128], BF16, kind="ExternalInput").ap()
    ident32_d = dt("ident32", [128, 128], F32, kind="ExternalInput").ap()
    ones_row_d = dt("ones_row", [1, 128], BF16, kind="ExternalInput").ap()
    cvals_d = dt("cvals", [128, 6], F32, kind="ExternalInput").ap()
    ones32_d = dt("ones32", [1, 128], F32, kind="ExternalInput").ap()
    onec_b_d = dt("onec_b", [128, 1], BF16, kind="ExternalInput").ap()
    onec_f_d = dt("onec_f", [128, 1], F32, kind="ExternalInput").ap()
    w_q0_d = dt("w_q0", [H, 128, KT, 128], F32, kind="ExternalInput").ap()
    w_k0_d = dt("w_k0", [HKV, 128, KT, 128], F32, kind="ExternalInput").ap()
    w_v0_d = dt("w_v0", [HKV, 128, KT, 128], F32, kind="ExternalInput").ap()
    w_o0_d = dt("w_o0", [H, 128, D], F32, kind="ExternalInput").ap()
    ropec_d = dt("ropec", [128, TPC], BF16, kind="ExternalInput").ap()
    ropes_d = dt("ropes", [128, TPC], BF16, kind="ExternalInput").ap()
    diagT_d = dt("diagT", [128, H * CHUNK], F32, kind="ExternalInput").ap()
    qdec_d = dt("qdec", [128, H * CHUNK], BF16, kind="ExternalInput").ap()
    kdec_d = dt("kdec", [128, H], F32, kind="ExternalInput").ap()
    coef_d = dt("coef", [128, 4 * H], F32, kind="ExternalInput").ap()
    c_w1_d = {nm: dt(f"c_{nm}1", [D // 128, 128, KT, 128], F32,
                     kind="ExternalInput").ap() for nm in ("wq", "wk", "wv")}
    c_w1_d["wo"] = dt("c_wo1", [KT, 128, D], F32, kind="ExternalInput").ap()
    s_w1_d = {nm: dt(f"s_{nm}1", [D // 128, 128, KT, 128], F32,
                     kind="ExternalInput").ap() for nm in ("wq", "wk", "wv")}
    s_w1_d["wo"] = dt("s_wo1", [KT, 128, D], F32, kind="ExternalInput").ap()
    b_w1_d = {nm: dt(f"{nm}1", [1, D], F32, kind="ExternalInput").ap()
              for nm in ("bq", "bk", "bv", "bo")}
    w_f1_d = dt("w_f1", [KT, 128, D], F32, kind="ExternalInput").ap()
    b_f1_d = dt("b_f1", [1, D], F32, kind="ExternalInput").ap()
    ts_f1_d = dt("ts_f1", [1, 1], F32, kind="ExternalInput").ap()

    cx = Ctx(nc)
    cx.mksem("dA", "dS", "pe", "dve", "act", "gp", "cc", "dO")
    cx.mksem("gu0", "gu1", "dq0", "dq1", "dq2", "sw0", "sw1")
    cx.mksem("hw0", "hw1", "hw2", "hwA", "hwB", "hwC", "hwW0", "hwW1", "dqB")

    def tap(name, shape, dtype=F32):
        if name in cfg["taps"]:
            taps[name] = dt("tap_" + name, list(shape), dtype,
                            kind="ExternalOutput").ap()
            return taps[name]
        return None

    with (
        nc.sbuf_tensor("x_res", [128, TT * D], F32) as x_res,
        nc.sbuf_tensor("ident_t", [128, 128], BF16) as ident_t,
        nc.sbuf_tensor("ident32_t", [128, 128], F32) as ident32_t,
        nc.sbuf_tensor("ones_t", [1, 128], BF16) as ones_t,
        nc.sbuf_tensor("cvals_t", [128, 6], F32) as cvals_t,
        nc.sbuf_tensor("ones32_t", [1, 128], F32) as ones32_t,
        nc.sbuf_tensor("f1_scb", [128, 8], F32) as f1_scb,
        nc.sbuf_tensor("onec_b_t", [128, 1], BF16) as onec_b_t,
        nc.sbuf_tensor("onec_f_t", [128, 1], F32) as onec_f_t,
        nc.sbuf_tensor("lnsc", [128, 36], F32) as lnsc,
        nc.sbuf_tensor("ln_sc", [128, 8 * TT], F32) as ln_sc,
        nc.psum_tensor("P0", [128, 2048], F32) as P0,
        nc.psum_tensor("P1", [128, 1024], F32) as P1,
        nc.Block() as block,
    ):
        x_ok = [("dA", 128)] * TT
        for _ci, _cv in enumerate([0.0, 1.0, 1e-5, math.log(2.0),
                                   -3.0 * math.log(2.0), 2.0 ** -3.5]):
            nc.const_aps.register(F32, _cv, cvals_t[:, _ci:_ci + 1])
        P2box = [None]

        def _(sync):
            sync.dma_start(
                out=x_res[:].rearrange("p (t d) -> p t d", t=TT),
                in_=x_in.rearrange("t p d -> p t d")).then_inc(cx.sem["dA"], 16)
            sync.dma_start(out=ident_t[:], in_=ident_d).then_inc(cx.sem["dA"], 16)
            sync.dma_start(out=ident32_t[:], in_=ident32_d).then_inc(cx.sem["dA"], 16)
            sync.dma_start(out=ones_t[:], in_=ones_row_d).then_inc(cx.sem["dA"], 16)
            sync.dma_start(out=cvals_t[:], in_=cvals_d).then_inc(cx.sem["dA"], 16)
            sync.dma_start(out=ones32_t[:], in_=ones32_d).then_inc(cx.sem["dA"], 16)
            sync.dma_start(out=onec_b_t[:], in_=onec_b_d).then_inc(cx.sem["dA"], 16)
            sync.dma_start(out=onec_f_t[:], in_=onec_f_d).then_inc(cx.sem["dA"], 16)
            cx.bump("dA", 128)
        block.sync(_)

        def emit_ln(tt, dst_bf, dst_f32, pe_guard=None):
            xt = x_res[:, tt * D:(tt + 1) * D]
            nch = max(1, D // 512)
            st = ln_sc[:, tt * 8:tt * 8 + 8]
            sem0, c0 = x_ok[tt]

            def dve1(v):
                cx.w(v, sem0, c0)
                v.drain()
                for a_ in range(nch):
                    v.bn_stats(out=lnsc[:, 6 * a_:6 * a_ + 6],
                               in_=xt[:, 512 * a_:512 * a_ + min(512, D)])
                v.drain()
                cx.inc("dve", v.bn_aggr(out=st[:, 0:2], in_=lnsc[:, 0:6 * nch]))
            block.vector(dve1)
            c_dve1 = cx.cnt["dve"]

            def act1(a):
                cx.w(a, "dve", c_dve1)
                cx.inc("act", a.activation(out=st[:, 2:3], in_=st[:, 1:2],
                                           func=Act.Sqrt, bias=1e-5, scale=1.0))
            block.scalar(act1)
            c_act1 = cx.cnt["act"]

            def dve2(v):
                cx.w(v, "act", c_act1)
                v.reciprocal(out=st[:, 3:4], in_=st[:, 2:3])
                v.drain()
                v.tensor_tensor(out=st[:, 4:5], in0=st[:, 0:1], in1=st[:, 3:4],
                                op=Alu.mult)
                v.drain()
                cx.inc("dve", v.tensor_scalar(
                    out=st[:, 4:5], in0=st[:, 4:5], scalar1=-1.0, scalar2=None,
                    op0=Alu.mult))
            block.vector(dve2)
            c_dve2 = cx.cnt["dve"]

            def act2(a):
                cx.w(a, "dve", c_dve2)
                if pe_guard is not None:
                    cx.w(a, "pe", pe_guard)
                if dst_bf is not None:
                    a.activation(out=dst_bf, in_=xt, func=Act.Identity,
                                 bias=st[:, 4:5], scale=st[:, 3:4])
                    a.drain()
                if dst_f32 is not None:
                    a.activation(out=dst_f32, in_=xt, func=Act.Identity,
                                 bias=st[:, 4:5], scale=st[:, 3:4])
                cx.inc("act", a.drain())
            block.scalar(act2)
            return cx.cnt["act"]

        def emit_transposes(n_tiles, src, dst, ready, extra=None):
            slot = lambda i: (i % 4) * 512
            pe_base = cx.cnt["pe"]
            dve_base = cx.cnt["dve"]
            act_base = cx.cnt["act"]

            def pe(t):
                cx.w(t, "dA", 128)
                cx.w(t, ready[0], ready[1])
                if extra is not None:
                    cx.w(t, extra[0], extra[1])
                for i in range(n_tiles):
                    if i >= 4:
                        j = i - 4
                        if j % 2 == 0:
                            t.wait_ge(cx.sem["dve"], dve_base + j // 2 + 1)
                        else:
                            t.wait_ge(cx.sem["act"], act_base + j // 2 + 1)
                    t.transpose(P0[:, slot(i):slot(i) + 128], src(i),
                                ident32_t[:]).then_inc(cx.sem["pe"], 1)
            cx.bump("pe", n_tiles)
            block.tensor(pe)

            def dve(v):
                for i in range(0, n_tiles, 2):
                    v.wait_ge(cx.sem["pe"], pe_base + i + 1)
                    v.tensor_copy(out=dst(i), in_=P0[:, slot(i):slot(i) + 128]
                                  ).then_inc(cx.sem["dve"], 1)
            cx.bump("dve", (n_tiles + 1) // 2)
            block.vector(dve)

            def act(a):
                for i in range(1, n_tiles, 2):
                    a.wait_ge(cx.sem["pe"], pe_base + i + 1)
                    a.copy(out=dst(i), in_=P0[:, slot(i):slot(i) + 128]
                           ).then_inc(cx.sem["act"], 1)
            cx.bump("act", n_tiles // 2)
            block.scalar(act)
            return cx.cnt["dve"], cx.cnt["act"]

        def sublayer_ffn0(sidx):
            ent = {k: cx.cnt[k] for k in ("pe", "dve", "act")}
            n_oct = (IT + 7) // 8
            octs = [list(range(o * 8, min(IT, o * 8 + 8))) for o in range(n_oct)]
            if cfg.get("ffn0_octs"):
                octs = octs[:cfg["ffn0_octs"]]
                n_oct = len(octs)
            RING = cfg.get("ffn0_ring", 4)
            quads = [list(range(q * 2, min(IT, q * 2 + 2)))
                     for q in range((IT + 1) // 2)]
            NQ = len(quads)

            P2 = P2box[0]
            with (
                nc.sbuf_tensor(f"s{sidx}_xnT", [128, KT * TPC], BF16) as xnT,
                nc.sbuf_tensor(f"s{sidx}_xn32", [128, D], F32) as xn32,
                nc.sbuf_tensor(f"s{sidx}_gu", [128, 2 * 2 * KT * 128], BF16) as gub,
                nc.sbuf_tensor(f"s{sidx}_dwn", [128, RING * 2 * D], BF16) as dwn,
                nc.sbuf_tensor(f"s{sidx}_h", [128, 2 * 8 * TPC], BF16) as hsl,
                nc.sbuf_tensor(f"s{sidx}_gs", [128, IT], F32) as gs_t,
                nc.sbuf_tensor(f"s{sidx}_us", [128, IT], F32) as us_t,
                nc.sbuf_tensor(f"s{sidx}_dsb", [128, D], BF16) as dsb,
                nc.sbuf_tensor(f"s{sidx}_dsr", [1, D], F32) as dsr,
                nc.sbuf_tensor(f"s{sidx}_gt", [128, TPC], BF16) as gtmp,
                nc.sbuf_tensor(f"s{sidx}_ut", [128, TPC], BF16) as utmp,
            ):
                _gu2 = cfg.get("ffn0_phase") == "gu2"
                def aux(sc):
                    cx.w(sc, "act", ent["act"])
                    cx.w(sc, "dve", ent["dve"])
                    sc.dma_start(out=gs_t[:], in_=gate_s_d).then_inc(cx.sem["dS"], 16)
                    sc.dma_start(out=us_t[:], in_=up_s_d).then_inc(cx.sem["dS"], 16)
                    sc.dma_start(out=dsr[:], in_=down_s_d).then_inc(cx.sem["dS"], 16)
                    cx.bump("dS", 48)
                block.scalar(aux)
                aux_c = cx.cnt["dS"]

                dsrb = hsl[0:1, 0:D]

                def prep(v):
                    cx.w(v, "dS", aux_c)
                    v.tensor_scalar(out=us_t[:], in0=us_t[:], scalar1=0.5,
                                    scalar2=None, op0=Alu.mult)
                    v.drain()
                    cx.inc("dve", v.tensor_copy(out=dsrb, in_=dsr[:]))
                block.vector(prep)
                c0 = cx.cnt["dve"]

                def peb(t):
                    cx.w(t, "dve", max(c0, ent["dve"]))
                    cx.w(t, "dA", 128)
                    for co, cn in DC:
                        t.matmul(P0[:, co:co + cn], ones_t[:],
                                 dsrb[:, co:co + cn], start=True, stop=True)
                    cx.inc("pe", t.drain())
                block.tensor(peb)
                c1 = cx.cnt["pe"]

                def ev(v):
                    cx.w(v, "pe", c1)
                    cx.inc("dve", v.tensor_copy(out=dsb[:], in_=P0[:, 0:D]))
                block.vector(ev)
                dsb_c = cx.cnt["dve"]
                if _gu2:
                    dsb_c = cx.cnt["dve"]

                pe_g = None
                for tt in range(TT):
                    a_rdy = emit_ln(tt, None, xn32[:], pe_guard=pe_g)
                    emit_transposes(
                        KT, lambda kt: xn32[:, kt * 128:(kt + 1) * 128],
                        lambda kt, tt=tt: xnT[:, kt * TPC + tt * 128:
                                              kt * TPC + (tt + 1) * 128],
                        ("act", a_rdy), extra=("dve", dsb_c))
                    pe_g = cx.cnt["pe"]
                xnT_dve, xnT_act = cx.cnt["dve"], cx.cnt["act"]
                if cfg.get("ffn0_phase") == "ln":
                    return

                stream_order = []
                for oi, oct in enumerate(octs):
                    for i in oct:
                        stream_order.append(("gu", i))
                    if not _gu2:
                        for q in range(oi * 4, min(NQ, oi * 4 + 4)):
                            stream_order.append(("dq", q))
                gu_loaded = {i: (f"gu{i % 2}", cx.cnt[f"gu{i % 2}"] + 16 * (i // 2 + 1))
                             for i in range(IT)}
                _dqsems = ["dq0", "dq1", "dq2", "dqB"]
                dq_loaded = {q: (_dqsems[q % 4],
                                 cx.cnt[_dqsems[q % 4]] + 16 * (q // 4 + 1))
                             for q in range(NQ)}
                for i in range(IT):
                    cx.cnt[f"gu{i % 2}"] = gu_loaded[i][1]
                for q in range(NQ):
                    cx.cnt[_dqsems[q % 4]] = dq_loaded[q][1]

                incl = [i for oct_ in octs for i in oct_]
                act_g_done = {}
                act_c = cx.cnt["act"]
                for i in incl:
                    act_c += 1; act_g_done[i] = act_c
                _gu_only = cfg.get("ffn0_phase") in ("gu", "gu2", "gu3")
                _gu3 = cfg.get("ffn0_phase") == "gu3"
                ev_sched = []
                for oi, oct in enumerate(octs):
                    for i in oct:
                        ev_sched.append(("u", i))
                        ev_sched.append(("h", i))
                    if not (_gu2 or _gu3):
                        for q2 in range(oi * 4, min(NQ, oi * 4 + 4,
                                                    4 * len(octs))):
                            ev_sched.append(("scale", q2))
                    if not _gu_only:
                        for ts in range(TT):
                            ev_sched.append(("evac", oi, ts, 0))
                            if D >= 1024:
                                ev_sched.append(("evac", oi, ts, 1))
                dve_scale_done, dve_h_done, dve_evac_done = {}, {}, {}
                dve_u_done = {}
                dve_c = cx.cnt["dve"]
                for item in ev_sched:
                    dve_c += 1
                    if item[0] == "scale":
                        dve_scale_done[item[1]] = dve_c
                    elif item[0] == "u":
                        dve_u_done[item[1]] = dve_c
                    elif item[0] == "h":
                        dve_h_done[item[1]] = dve_c
                    else:
                        dve_evac_done[(item[1], item[2], item[3])] = dve_c

                pe_gu_done = {}
                pe_down_done = {}

                def pe_mm(t):
                    cx.w(t, "dve", max(xnT_dve, dsb_c))
                    cx.w(t, "act", xnT_act)
                    for oi, oct in enumerate(octs):
                        for i in oct:
                            t.wait_ge(cx.sem[gu_loaded[i][0]], gu_loaded[i][1])
                            for w in range(2):
                                ps = P1 if w == 0 else P2
                                if i > 0:
                                    if w == 0:
                                        t.wait_ge(cx.sem["act"],
                                                  act_g_done[i - 1])
                                    else:
                                        t.wait_ge(cx.sem["dve"],
                                                  dve_u_done[i - 1])
                                base = (i % 2) * (2 * KT * 128) + w * KT * 128
                                for kt in range(KT):
                                    wap = gub[:, base + kt * 128:base + kt * 128 + 128]
                                    t.ldweights(wap)
                                    for tco, tcn in TC:
                                        ii = t.matmul(ps[:, tco:tco + tcn], wap,
                                                      xnT[:, kt * TPC + tco:
                                                          kt * TPC + tco + tcn],
                                                      start=(kt == 0),
                                                      stop=(kt == KT - 1))
                                ii.then_inc(cx.sem["pe"], 1)
                                cx.bump("pe")
                            pe_gu_done[i] = cx.cnt["pe"]
                        HDC = DC[:len(DC) // 2] if len(DC) > 1 else list(DC)
                        HD2 = DC[len(DC) // 2:] if len(DC) > 1 else []
                        for ts in ([] if _gu_only else range(TT)):
                            for hf, dcs in enumerate((HDC, HD2)):
                                if not dcs:
                                    pe_down_done[(oi, ts, hf)] = cx.cnt["pe"]
                                    continue
                                prev = ((oi, ts - 1, hf) if ts > 0 else
                                        (oi - 1, TT - 1, hf) if oi > 0
                                        else None)
                                if prev is not None:
                                    t.wait_ge(cx.sem["dve"],
                                              dve_evac_done[prev])
                                for ii, i in enumerate(oct):
                                    q = i // 2
                                    if hf == 0:
                                        t.wait_ge(cx.sem["dve"],
                                                  dve_scale_done[q])
                                        t.wait_ge(cx.sem["dve"],
                                                  dve_h_done[i])
                                    hb = (((i // 8) % 2) * 8 * TPC +
                                          (i % 8) * TPC)
                                    hap = hsl[:, hb + ts * 128:
                                              hb + ts * 128 + 128]
                                    t.ldweights(hap)
                                    for co, cn in dcs:
                                        slab = q % RING
                                        wap = dwn[:, slab * 2 * D +
                                                  (i % 2) * D + co:
                                                  slab * 2 * D +
                                                  (i % 2) * D + co + cn]
                                        li = t.matmul(P0[:, co:co + cn],
                                                      hap, wap,
                                                      start=(ii == 0),
                                                      stop=(ii ==
                                                            len(oct) - 1))
                                li.then_inc(cx.sem["pe"], 1)
                                cx.bump("pe")
                                pe_down_done[(oi, ts, hf)] = cx.cnt["pe"]
                block.tensor(pe_mm)
                pe_dw_need = {}
                for q in range(NQ):
                    oi = quads[q][0] // 8
                    pe_dw_need[q] = pe_down_done.get(
                        (oi, TT - 1, 1),
                        pe_down_done.get((oi, TT - 1, 0),
                                         max(pe_gu_done.values())))

                def act_ep(a):
                    cx.w(a, "dS", aux_c)
                    for i in incl:
                        cx.w(a, "pe", pe_gu_done[i])
                        if i >= 1:
                            cx.w(a, "dve", dve_h_done[i - 1])
                        a.activation(out=gtmp[:], in_=P1[:, 0:TPC],
                                     func=Act.Silu, scale=gs_t[:, i:i + 1]
                                     ).then_inc(cx.sem["act"], 1)
                cx.cnt["act"] = act_c
                block.scalar(act_ep)

                def dve_ep(v):
                    for item in ev_sched:
                        if item[0] == "scale":
                            q = item[1]
                            nq = len(quads[q])
                            v.wait_ge(cx.sem[dq_loaded[q][0]], dq_loaded[q][1])
                            cx.w(v, "dve", dsb_c)
                            if q >= RING:
                                cx.w(v, "pe", pe_dw_need[q - RING])
                            slab = q % RING
                            ap3 = dwn[:, slab * 2 * D:slab * 2 * D + nq * D] \
                                .rearrange("p (i d) -> p i d", i=nq)
                            v.tensor_tensor(
                                out=ap3, in0=ap3,
                                in1=dsb[:].unsqueeze(1).broadcast_to([128, nq, D]),
                                op=Alu.mult).then_inc(cx.sem["dve"], 1)
                        elif item[0] == "u":
                            i = item[1]
                            cx.w(v, "pe", pe_gu_done[i])
                            cx.w(v, "dS", aux_c)
                            if i >= 1:
                                cx.w(v, "dve", dve_h_done[i - 1])
                            if i // 8 >= 2 and not _gu_only:
                                cx.w(v, "pe", pe_down_done[
                                    (i // 8 - 2, TT - 1, len(DC) > 1 and 1 or 0)])
                            v.tensor_scalar(out=utmp[:], in0=P2[:, 0:TPC],
                                            scalar1=us_t[:, i:i + 1],
                                            scalar2=None, op0=Alu.mult
                                            ).then_inc(cx.sem["dve"], 1)
                        elif item[0] == "h":
                            i = item[1]
                            cx.w(v, "act", act_g_done[i])
                            cx.w(v, "dve", dve_u_done[i])
                            if i // 8 >= 2 and not _gu_only:
                                cx.w(v, "pe", pe_down_done[
                                    (i // 8 - 2, TT - 1, len(DC) > 1 and 1 or 0)])
                            hb = ((i // 8) % 2) * 8 * TPC + (i % 8) * TPC
                            v.tensor_tensor(out=hsl[:, hb:hb + TPC],
                                            in0=gtmp[:],
                                            in1=utmp[:],
                                            op=Alu.mult).then_inc(cx.sem["dve"], 1)
                        else:
                            oi, ts, hf = item[1], item[2], item[3]
                            if len(DC) > 1:
                                ho = hf * (D // 2)
                                hw = D // 2
                            else:
                                ho, hw = 0, D
                            cx.w(v, "pe", pe_down_done[(oi, ts, hf)])
                            xt = x_res[:, ts * D + ho:ts * D + ho + hw]
                            v.tensor_tensor(out=xt, in0=P0[:, ho:ho + hw],
                                            in1=xt,
                                            op=Alu.add).then_inc(cx.sem["dve"], 1)
                cx.cnt["dve"] = dve_c
                block.vector(dve_ep)

                def stream(g):
                    cx.w(g, "pe", ent["pe"])
                    cx.w(g, "dve", ent["dve"])
                    for kind, idx in stream_order:
                        if kind == "gu":
                            i = idx
                            if i >= 2:
                                g.wait_ge(cx.sem["pe"], pe_gu_done[i - 2])
                            g.dma_start(
                                out=gub[:, (i % 2) * 2 * KT * 128:
                                        ((i % 2) + 1) * 2 * KT * 128]
                                .rearrange("p (w k n) -> p w k n", w=2, k=KT),
                                in_=w_gu_d[i]).then_inc(cx.sem[f"gu{i % 2}"], 16)
                        else:
                            q = idx
                            nq = len(quads[q])
                            if q >= RING:
                                if not _gu3:
                                    g.wait_ge(cx.sem["dve"], dve_scale_done[q - RING])
                                g.wait_ge(cx.sem["pe"], pe_dw_need[q - RING])
                            slab = q % RING
                            g.dma_start(
                                out=dwn[:, slab * 2 * D:slab * 2 * D + nq * D]
                                .rearrange("p (i d) -> p i d", i=nq),
                                in_=w_down_d[quads[q][0]:quads[q][0] + nq]
                                .rearrange("i p d -> p i d")).then_inc(
                                cx.sem[dq_loaded[q][0]], 16)
                block.gpsimd(stream)

            if not _gu_only:
                for ts in range(TT):
                    x_ok[ts] = ("dve", max(
                        dve_evac_done[(n_oct - 1, ts, 0)],
                        dve_evac_done.get((n_oct - 1, ts, 1), 0)))


        def sublayer_attn0(sidx):
            ent = {k: cx.cnt[k] for k in ("pe", "dve", "act")}
            G = H // HKV
            lam = [2.0 ** (-8.0 * (h + 1) / H) for h in range(H)]
            blkd = [math.exp(-lam[h] * CHUNK) for h in range(H)]
            NPH = 2 * HKV + H

            with (
                nc.sbuf_tensor(f"s{sidx}_xoT", [128, max(KT, H) * TPC], BF16) as xoT,
                nc.sbuf_tensor(f"s{sidx}_ropec", [128, TPC], BF16) as ropec_t,
                nc.sbuf_tensor(f"s{sidx}_ropes", [128, TPC], BF16) as ropes_t,
                nc.sbuf_tensor(f"s{sidx}_diagT", [128, H * CHUNK], F32) as diagT_t,
                nc.sbuf_tensor(f"s{sidx}_qdec", [128, H * CHUNK], BF16) as qdec_t,
                nc.sbuf_tensor(f"s{sidx}_kdect", [128, H], F32) as kdec_t,
                nc.sbuf_tensor(f"s{sidx}_coef", [128, 4 * H], F32) as coef_t,
            ):
                xnT = xoT[:, 0:KT * TPC]
                oT = xoT[:, 0:H * TPC]

                def aux(sc):
                    cx.w(sc, "act", ent["act"])
                    cx.w(sc, "dve", ent["dve"])
                    for dst_, src_ in ((ropec_t, ropec_d), (ropes_t, ropes_d),
                                       (diagT_t, diagT_d), (qdec_t, qdec_d),
                                       (kdec_t, kdec_d), (coef_t, coef_d)):
                        sc.dma_start(out=dst_[:], in_=src_).then_inc(cx.sem["dS"], 16)
                        cx.bump("dS", 16)
                block.scalar(aux)
                aux_c = cx.cnt["dS"]

                with (
                    nc.sbuf_tensor(f"s{sidx}_qf", [128, H * TPC], BF16) as qf,
                    nc.sbuf_tensor(f"s{sidx}_kf", [128, HKV * TPC], BF16) as kf,
                    nc.sbuf_tensor(f"s{sidx}_vf", [128, HKV * TPC], BF16) as vf,
                    nc.sbuf_tensor(f"s{sidx}_vtok", [128, TT * HKV * DH], BF16) as vtok,
                ):
                    act_proj_done = {}
                    with (
                        nc.sbuf_tensor(f"s{sidx}_xn32", [128, D], F32) as xn32,
                        nc.sbuf_tensor(f"s{sidx}_qkb", [128, 2 * KT * 128], BF16) as qkb,
                    ):
                        pe_g = None
                        for tt in range(TT):
                            a_rdy = emit_ln(tt, None, xn32[:], pe_guard=pe_g)
                            emit_transposes(
                                KT, lambda kt: xn32[:, kt * 128:(kt + 1) * 128],
                                lambda kt, tt=tt: xnT[:, kt * TPC + tt * 128:
                                                      kt * TPC + (tt + 1) * 128],
                                ("act", a_rdy), extra=("dve", ent["dve"]))
                            pe_g = cx.cnt["pe"]
                        xnT_dve, xnT_act = cx.cnt["dve"], cx.cnt["act"]

                        qk_loaded = {}
                        for si in range(NPH):
                            s_ = f"gu{si % 2}"
                            qk_loaded[si] = (s_, cx.cnt[s_] + 16)
                            cx.cnt[s_] += 16

                        act_c = cx.cnt["act"]
                        for si in range(NPH):
                            act_c += 1
                            act_proj_done[si] = act_c

                        def proj_dst(si):
                            if si < HKV:
                                return kf[:, si * TPC:(si + 1) * TPC]
                            if si < 2 * HKV:
                                return vf[:, (si - HKV) * TPC:(si - HKV + 1) * TPC]
                            return qf[:, (si - 2 * HKV) * TPC:
                                      (si - 2 * HKV + 1) * TPC]

                        pe_proj_done = {}

                        def pe_proj(t):
                            cx.w(t, "dve", xnT_dve)
                            cx.w(t, "act", xnT_act)
                            for si in range(NPH):
                                t.wait_ge(cx.sem[qk_loaded[si][0]], qk_loaded[si][1])
                                if si >= 2:
                                    t.wait_ge(cx.sem["act"], act_proj_done[si - 2])
                                ps = P1 if si % 2 == 0 else P0
                                base = (si % 2) * KT * 128
                                for kt in range(KT):
                                    wap = qkb[:, base + kt * 128:base + kt * 128 + 128]
                                    t.ldweights(wap)
                                    for tco, tcn in TC:
                                        ii = t.matmul(ps[:, tco:tco + tcn], wap,
                                                      xnT[:, kt * TPC + tco:
                                                          kt * TPC + tco + tcn],
                                                      start=(kt == 0),
                                                      stop=(kt == KT - 1))
                                ii.then_inc(cx.sem["pe"], 1)
                                cx.bump("pe")
                                pe_proj_done[si] = cx.cnt["pe"]
                        block.tensor(pe_proj)

                        def act_proj(a):
                            for si in range(NPH):
                                cx.w(a, "pe", pe_proj_done[si])
                                ps = P1 if si % 2 == 0 else P0
                                a.activation(out=proj_dst(si), in_=ps[:, 0:TPC],
                                             func=Act.Copy
                                             ).then_inc(cx.sem["act"], 1)
                        cx.cnt["act"] = act_c
                        block.scalar(act_proj)

                        def stream1(g):
                            cx.w(g, "pe", ent["pe"])
                            cx.w(g, "dve", ent["dve"])
                            for si in range(NPH):
                                if si >= 2:
                                    g.wait_ge(cx.sem["pe"], pe_proj_done[si - 2])
                                if si < HKV:
                                    w_src = w_k0_d[si]
                                elif si < 2 * HKV:
                                    w_src = w_v0_d[si - HKV]
                                else:
                                    w_src = w_q0_d[si - 2 * HKV]
                                g.dma_start(out=qkb[:, (si % 2) * KT * 128:
                                                   (si % 2 + 1) * KT * 128]
                                            .rearrange("p (k n) -> p k n", k=KT),
                                            in_=w_src).then_inc(
                                    cx.sem[qk_loaded[si][0]], 16)
                        block.gpsimd(stream1)

                    with (
                        nc.sbuf_tensor(f"s{sidx}_swp", [128, 2 * TPC], BF16) as swp,
                        nc.sbuf_tensor(f"s{sidx}_ktok", [128, HKV * TT * 128], BF16) as ktok,
                        nc.sbuf_tensor(f"s{sidx}_kv", [128, H * 128], F32) as kvst,
                        nc.sbuf_tensor(f"s{sidx}_kvbf", [128, 2 * 128], BF16) as kvbf,
                        nc.sbuf_tensor(f"s{sidx}_sj", [128, 4 * 128], F32) as sjb,
                        nc.sbuf_tensor(f"s{sidx}_ploc", [128, 2 * CHUNK], BF16) as ploc,
                        nc.sbuf_tensor(f"s{sidx}_qdq", [128, 2 * CHUNK], BF16) as qdqs,
                        nc.sbuf_tensor(f"s{sidx}_kdk", [128, 2 * 128], BF16) as kdks,
                    ):
                        rope_heads = list(range(HKV)) + \
                                     list(range(2 * HKV, NPH))
                        dve_rope_done = {}
                        dve_rope_m2 = {}
                        _r0 = cx.cnt["dve"]
                        R_ = len(rope_heads)
                        for ri in range(R_):
                            dve_rope_m2[ri] = _r0 + (2 if ri == 0
                                                     else 2 + 3 * ri)
                            dve_rope_done[ri] = _r0 + (5 + 3 * ri
                                                       if ri < R_ - 1
                                                       else 3 * R_)
                        dve_c2 = _r0 + 3 * R_
                        dA_swp = {}

                        def head_ap(si):
                            if si < HKV:
                                return kf, si * TPC
                            return qf, (si - 2 * HKV) * TPC

                        def swap_all(sync):
                            cx.w(sync, "pe", pe_proj_done[NPH - 1])
                            for ri, si in enumerate(rope_heads):
                                base, off = head_ap(si)
                                sl = (ri % 2) * TPC
                                sw = f"sw{ri % 2}"
                                cx.w(sync, "act", act_proj_done[si])
                                cx.w(sync, "dS", aux_c)
                                if ri >= 2:
                                    cx.w(sync, "dve", dve_rope_done[ri - 2])
                                sync.dma_start(out=swp[0:64, sl:sl + TPC],
                                               in_=base[64:128, off:off + TPC]
                                               ).then_inc(cx.sem[sw], 16)
                                sync.dma_start(out=swp[64:128, sl:sl + TPC],
                                               in_=base[0:64, off:off + TPC]
                                               ).then_inc(cx.sem[sw], 16)
                                cx.bump(sw, 32)
                                dA_swp[ri] = (sw, cx.cnt[sw])
                        block.sync(swap_all)

                        def rope_v(v):
                            def m12(ri, si):
                                base, off = head_ap(si)
                                sl = (ri % 2) * TPC
                                v.wait_ge(cx.sem[dA_swp[ri][0]], dA_swp[ri][1])
                                cx.w(v, "dS", aux_c)
                                v.tensor_tensor(out=swp[:, sl:sl + TPC],
                                                in0=swp[:, sl:sl + TPC],
                                                in1=ropes_t[:], op=Alu.mult
                                                ).then_inc(cx.sem["dve"], 1)
                                v.tensor_tensor(out=base[:, off:off + TPC],
                                                in0=base[:, off:off + TPC],
                                                in1=ropec_t[:], op=Alu.mult
                                                ).then_inc(cx.sem["dve"], 1)

                            def a3(ri, si):
                                base, off = head_ap(si)
                                sl = (ri % 2) * TPC
                                v.wait_ge(cx.sem["dve"], dve_rope_m2[ri])
                                v.tensor_tensor(out=base[:, off:off + TPC],
                                                in0=base[:, off:off + TPC],
                                                in1=swp[:, sl:sl + TPC],
                                                op=Alu.add
                                                ).then_inc(cx.sem["dve"], 1)

                            for ri, si in enumerate(rope_heads):
                                m12(ri, si)
                                if ri >= 1:
                                    a3(ri - 1, rope_heads[ri - 1])
                            a3(R_ - 1, rope_heads[R_ - 1])
                            cx.cnt["dve"] = dve_c2
                        block.vector(rope_v)
                        dve_rope_k = dve_rope_done[HKV - 1]
                        dve_rope_all = dve_c2

                        PB = P0.bitcast(BF16)
                        tr_list = []
                        for kv_ in range(HKV):
                            for tt_ in range(TT):
                                tr_list.append(
                                    (kf[:, kv_ * TPC + tt_ * 128:
                                        kv_ * TPC + tt_ * 128 + 128],
                                     ktok[:, (kv_ * TT + tt_) * 128:
                                          (kv_ * TT + tt_) * 128 + 128]))
                        for kv_ in range(HKV):
                            for tt_ in range(TT):
                                tr_list.append(
                                    (vf[:, kv_ * TPC + tt_ * 128:
                                        kv_ * TPC + tt_ * 128 + 128],
                                     vtok[:, (tt_ * HKV + kv_) * 128:
                                          (tt_ * HKV + kv_) * 128 + 128]))
                        ntr = len(tr_list)
                        ktr_pe_base = cx.cnt["pe"]
                        ktr_dve_base = cx.cnt["dve"]

                        def ktr_pe(t):
                            cx.w(t, "dve", dve_rope_k)
                            cx.w(t, "act", act_proj_done[2 * HKV - 1])
                            for i in range(ntr):
                                if i >= 2:
                                    t.wait_ge(cx.sem["dve"], ktr_dve_base + i - 1)
                                t.transpose(PB[:, 2048 + (i % 2) * 1024:
                                               2048 + (i % 2) * 1024 + 128],
                                            tr_list[i][0], ident_t[:]
                                            ).then_inc(cx.sem["pe"], 1)
                        cx.bump("pe", ntr)
                        block.tensor(ktr_pe)

                        def ktr_ev(v):
                            for i in range(ntr):
                                v.wait_ge(cx.sem["pe"], ktr_pe_base + i + 1)
                                v.tensor_copy(out=tr_list[i][1],
                                              in_=PB[:, 2048 + (i % 2) * 1024:
                                                     2048 + (i % 2) * 1024 + 128]
                                              ).then_inc(cx.sem["dve"], 1)
                        cx.bump("dve", ntr)
                        dve_ktok_done = cx.cnt["dve"]
                        block.vector(ktr_ev)

                        def vtok_chunk(h, c):
                            kv_ = h // G
                            pb = (c % 2) * 64
                            return vtok[pb:pb + 64,
                                        ((c // 2) * HKV + kv_) * 128:
                                        ((c // 2) * HKV + kv_) * 128 + 128]

                        def ktok_chunk(h, c):
                            kv_ = h // G
                            pb = (c % 2) * 64
                            i = kv_ * TT + c // 2
                            return ktok[pb:pb + 64, i * 128:i * 128 + 128]

                        def kf_chunk(h, c):
                            kv_ = h // G
                            return kf[:, kv_ * TPC + c * CHUNK:
                                      kv_ * TPC + (c + 1) * CHUNK]

                        def qf_chunk(h, c):
                            return qf[:, h * TPC + c * CHUNK:
                                      h * TPC + (c + 1) * CHUNK]

                        def mset(v):
                            v.memset(kvst[:], 0.0)
                            v.drain().then_inc(cx.sem["dve"], 1)
                        cx.bump("dve")
                        mset_c = cx.cnt["dve"]
                        block.vector(mset)

                        NSEQ = NCH * H
                        dveA_kdk = {}
                        dveA_upd = {}
                        dve_cA = cx.cnt["dve"]
                        for n_ in range(NSEQ):
                            dve_cA += 1
                            dveA_kdk[n_] = dve_cA
                            dve_cA += 1
                            dveA_upd[n_] = dve_cA

                        peA = {}

                        def phA_pe(t):
                            cx.w(t, "dve", max(dve_ktok_done, mset_c))
                            for c in range(NCH):
                                for h in range(H):
                                    n_ = c * H + h
                                    t.wait_ge(cx.sem["dve"], dveA_kdk[n_])
                                    if n_ >= 2:
                                        t.wait_ge(cx.sem["dve"], dveA_upd[n_ - 2])
                                    pb = (c % 2) * 64
                                    kap = kdks[pb:pb + 64,
                                               (n_ % 2) * 128:(n_ % 2) * 128 + 128]
                                    t.ldweights(kap, tile_position=(pb, 0))
                                    t.matmul(P2[:, (n_ % 2) * 512:
                                                (n_ % 2) * 512 + 128],
                                             kap, vtok_chunk(h, c),
                                             start=True, stop=True,
                                             tile_position=(pb, 0)
                                             ).then_inc(cx.sem["pe"], 1)
                                    cx.bump("pe")
                                    peA[n_] = cx.cnt["pe"]
                        block.tensor(phA_pe)

                        def phA_dve(v):
                            cx.w(v, "dve", max(dve_ktok_done, mset_c))
                            for c in range(NCH):
                                for h in range(H):
                                    n_ = c * H + h
                                    pb = (c % 2) * 64
                                    if n_ >= 2:
                                        v.wait_ge(cx.sem["pe"], peA[n_ - 2])
                                    v.tensor_scalar(
                                        out=kdks[pb:pb + 64,
                                                 (n_ % 2) * 128:(n_ % 2) * 128 + 128],
                                        in0=ktok_chunk(h, c),
                                        scalar1=kdec_t[pb:pb + 64, h:h + 1],
                                        scalar2=None, op0=Alu.mult
                                        ).then_inc(cx.sem["dve"], 1)
                                    v.wait_ge(cx.sem["pe"], peA[n_])
                                    if n_ >= H:
                                        v.wait_ge(cx.sem["dve"],
                                                  dveA_upd[n_ - H])
                                    v.scalar_tensor_tensor(
                                        out=kvst[:, h * 128:(h + 1) * 128],
                                        in0=kvst[:, h * 128:(h + 1) * 128],
                                        scalar=blkd[h],
                                        in1=P2[:, (n_ % 2) * 512:
                                               (n_ % 2) * 512 + 128],
                                        op0=Alu.mult, op1=Alu.add
                                        ).then_inc(cx.sem["dve"], 1)
                            cx.cnt["dve"] = dve_cA
                        block.vector(phA_dve)

                        Nag = 128 * H * 128
                        cc_in0 = dt(f"s{sidx}_ccin", [1, Nag], F32,
                                    kind="Internal").ap()
                        cc_out0 = dt(f"s{sidx}_ccout", [4, Nag], F32,
                                     kind="Internal").ap()

                        def s_exp(sync):
                            cx.w(sync, "dve", dveA_upd[NSEQ - 1])
                            sync.dma_start(
                                out=cc_in0.rearrange("o (p f) -> (o p) f", p=128),
                                in_=kvst[:]).then_inc(cx.sem["dA"], 16)
                            cx.bump("dA", 16)
                        block.sync(s_exp)
                        dA_sexp = cx.cnt["dA"]

                        def agather(g):
                            g.wait_ge(cx.sem["dA"], dA_sexp)
                            g.collective_compute(
                                "AllGather", Alu.bypass,
                                replica_groups=[[0, 1, 2, 3], [4, 5, 6, 7]],
                                ins=[cc_in0], outs=[cc_out0],
                            ).then_inc(cx.sem["cc"], 1)
                            cx.bump("cc")
                        block.gpsimd(agather)
                        cc_done = cx.cnt["cc"]

                        NQH = max(1, H // 4)
                        comb_dve = {}
                        dve_cB = cx.cnt["dve"]
                        for m_ in range(4 * NQH):
                            j, hq = divmod(m_, NQH)
                            nheads = min(4, H - hq * 4)
                            dve_cB += nheads
                            comb_dve[m_] = dve_cB
                        dA_comb = {}

                        def comb_dma(sync):
                            cx.w(sync, "cc", cc_done)
                            for m_ in range(4 * NQH):
                                j, hq = divmod(m_, NQH)
                                nheads = min(4, H - hq * 4)
                                if m_ >= 1:
                                    cx.w(sync, "dve", comb_dve[m_ - 1])
                                sync.dma_start(
                                    out=sjb[:, 0:nheads * 128],
                                    in_=cc_out0[j:j + 1]
                                    .rearrange("o (p f) -> (o p) f", p=128)
                                    [:, hq * 512:hq * 512 + nheads * 128]
                                    ).then_inc(cx.sem["dA"], 16)
                                cx.bump("dA", 16)
                                dA_comb[m_] = cx.cnt["dA"]
                        block.sync(comb_dma)

                        def comb_v(v):
                            for m_ in range(4 * NQH):
                                j, hq = divmod(m_, NQH)
                                nheads = min(4, H - hq * 4)
                                cx.w(v, "dA", dA_comb[m_])
                                for hi in range(nheads):
                                    h = hq * 4 + hi
                                    if j == 0:
                                        v.tensor_scalar(
                                            out=kvst[:, h * 128:(h + 1) * 128],
                                            in0=sjb[:, hi * 128:(hi + 1) * 128],
                                            scalar1=coef_t[:, j * H + h:
                                                           j * H + h + 1],
                                            scalar2=None, op0=Alu.mult
                                            ).then_inc(cx.sem["dve"], 1)
                                    else:
                                        v.scalar_tensor_tensor(
                                            out=kvst[:, h * 128:(h + 1) * 128],
                                            in0=sjb[:, hi * 128:(hi + 1) * 128],
                                            scalar=coef_t[:, j * H + h:
                                                          j * H + h + 1],
                                            in1=kvst[:, h * 128:(h + 1) * 128],
                                            op0=Alu.mult, op1=Alu.add
                                            ).then_inc(cx.sem["dve"], 1)
                                    v.drain()
                            cx.cnt["dve"] = dve_cB
                        block.vector(comb_v)

                        NSEQB = NCH * H
                        pe_seq = []
                        dve_seq = []
                        for n_ in range(NSEQB):
                            pe_seq.append(("sc", n_))
                            if n_ >= 1:
                                pe_seq.append(("o", n_ - 1))
                                pe_seq.append(("kvU", n_ - 1))
                            dve_seq.append(("prep", n_))
                            if n_ >= 1:
                                dve_seq.append(("fin", n_ - 1))
                        pe_seq.append(("o", NSEQB - 1))
                        pe_seq.append(("kvU", NSEQB - 1))
                        dve_seq.append(("fin", NSEQB - 1))
                        peB_sc = {}
                        peB_o = {}
                        peB_kvU = {}
                        _pc = cx.cnt["pe"]
                        for kind, n_ in pe_seq:
                            _pc += 1
                            (peB_sc if kind == "sc" else
                             peB_o if kind == "o" else peB_kvU)[n_] = _pc
                        dveB_ploc = {}
                        dveB_qdq = {}
                        dveB_kdk = {}
                        dveB_kvbf = {}
                        dveB_evac = {}
                        dveB_upd = {}
                        _dc = cx.cnt["dve"]
                        for kind, n_ in dve_seq:
                            if kind == "prep":
                                _dc += 1; dveB_ploc[n_] = _dc
                                _dc += 1; dveB_qdq[n_] = _dc
                                _dc += 1; dveB_kdk[n_] = _dc
                                _dc += 1; dveB_kvbf[n_] = _dc
                            else:
                                _dc += 1; dveB_evac[n_] = _dc
                                _dc += 1; dveB_upd[n_] = _dc
                        dve_cC = _dc
                        pe_cC = _pc

                        def phB_pe(t):
                            cx.w(t, "dve", max(dve_rope_all, dve_cB))
                            cx.w(t, "dS", aux_c)
                            for kind, n_ in pe_seq:
                                c, h = divmod(n_, H)
                                pb = (c % 2) * 64
                                if kind == "sc":
                                    if n_ >= 2:
                                        t.wait_ge(cx.sem["dve"], dveB_ploc[n_ - 2])
                                    kap = kf_chunk(h, c)
                                    t.ldweights(kap)
                                    t.matmul(P0[pb:pb + 64,
                                                (n_ % 2) * 512:(n_ % 2) * 512 + 64],
                                             kap, qf_chunk(h, c),
                                             start=True, stop=True
                                             ).then_inc(cx.sem["pe"], 1)
                                    cx.bump("pe")
                                elif kind == "o":
                                    t.wait_ge(cx.sem["dve"], dveB_kvbf[n_])
                                    if n_ >= 2:
                                        t.wait_ge(cx.sem["dve"], dveB_evac[n_ - 2])
                                    osl = P0[:, 1024 + (n_ % 2) * 512:
                                             1024 + (n_ % 2) * 512 + 64]
                                    vap = vtok_chunk(h, c)
                                    t.ldweights(vap, tile_position=(pb, 0))
                                    t.matmul(osl, vap,
                                             ploc[pb:pb + 64, (n_ % 2) * CHUNK:
                                                  (n_ % 2) * CHUNK + CHUNK],
                                             start=True, stop=False,
                                             tile_position=(pb, 0))
                                    kvap = kvbf[:, (n_ % 2) * 128:
                                                (n_ % 2) * 128 + 128]
                                    t.ldweights(kvap)
                                    t.matmul(osl, kvap,
                                             qdqs[:, (n_ % 2) * CHUNK:
                                                  (n_ % 2) * CHUNK + CHUNK],
                                             start=False, stop=True
                                             ).then_inc(cx.sem["pe"], 1)
                                    cx.bump("pe")
                                else:
                                    t.wait_ge(cx.sem["dve"], dveB_kdk[n_])
                                    if n_ >= 2:
                                        t.wait_ge(cx.sem["dve"], dveB_upd[n_ - 2])
                                    kdap = kdks[pb:pb + 64,
                                                (n_ % 2) * 128:(n_ % 2) * 128 + 128]
                                    t.ldweights(kdap, tile_position=(pb, 0))
                                    t.matmul(P2[:, (n_ % 2) * 512:
                                                (n_ % 2) * 512 + 128],
                                             kdap, vtok_chunk(h, c),
                                             start=True, stop=True,
                                             tile_position=(pb, 0)
                                             ).then_inc(cx.sem["pe"], 1)
                                    cx.bump("pe")
                        assert cx.cnt["pe"] + len(pe_seq) - len(pe_seq) >= 0
                        block.tensor(phB_pe)
                        assert cx.cnt["pe"] == pe_cC

                        def phB_dve(v):
                            cx.w(v, "dve", dve_cB)
                            for kind, n_ in dve_seq:
                                c, h = divmod(n_, H)
                                pb = (c % 2) * 64
                                if kind == "prep":
                                    v.wait_ge(cx.sem["pe"], peB_sc[n_])
                                    if n_ >= 2:
                                        v.wait_ge(cx.sem["pe"], peB_o[n_ - 2])
                                    v.tensor_tensor(
                                        out=ploc[pb:pb + 64, (n_ % 2) * CHUNK:
                                                 (n_ % 2) * CHUNK + CHUNK],
                                        in0=P0[pb:pb + 64,
                                               (n_ % 2) * 512:(n_ % 2) * 512 + 64],
                                        in1=diagT_t[pb:pb + 64,
                                                    h * CHUNK:(h + 1) * CHUNK],
                                        op=Alu.mult).then_inc(cx.sem["dve"], 1)
                                    cx.bump("dve")
                                    v.tensor_tensor(
                                        out=qdqs[:, (n_ % 2) * CHUNK:
                                                 (n_ % 2) * CHUNK + CHUNK],
                                        in0=qf_chunk(h, c),
                                        in1=qdec_t[:, h * CHUNK:(h + 1) * CHUNK],
                                        op=Alu.mult).then_inc(cx.sem["dve"], 1)
                                    cx.bump("dve")
                                    if n_ >= 2:
                                        v.wait_ge(cx.sem["pe"], peB_kvU[n_ - 2])
                                    v.tensor_scalar(
                                        out=kdks[pb:pb + 64,
                                                 (n_ % 2) * 128:(n_ % 2) * 128 + 128],
                                        in0=ktok_chunk(h, c),
                                        scalar1=kdec_t[pb:pb + 64, h:h + 1],
                                        scalar2=None, op0=Alu.mult
                                        ).then_inc(cx.sem["dve"], 1)
                                    cx.bump("dve")
                                    if n_ >= H:
                                        v.wait_ge(cx.sem["dve"],
                                                  dveB_upd[n_ - H])
                                    v.tensor_copy(
                                        out=kvbf[:, (n_ % 2) * 128:
                                                 (n_ % 2) * 128 + 128],
                                        in_=kvst[:, h * 128:(h + 1) * 128]
                                        ).then_inc(cx.sem["dve"], 1)
                                    cx.bump("dve")
                                else:
                                    v.wait_ge(cx.sem["pe"], peB_o[n_])
                                    v.tensor_copy(
                                        out=oT[:, h * TPC + c * CHUNK:
                                               h * TPC + (c + 1) * CHUNK],
                                        in_=P0[:, 1024 + (n_ % 2) * 512:
                                               1024 + (n_ % 2) * 512 + 64]
                                        ).then_inc(cx.sem["dve"], 1)
                                    cx.bump("dve")
                                    v.wait_ge(cx.sem["pe"], peB_kvU[n_])
                                    if n_ >= H:
                                        v.wait_ge(cx.sem["dve"],
                                                  dveB_upd[n_ - H])
                                    v.scalar_tensor_tensor(
                                        out=kvst[:, h * 128:(h + 1) * 128],
                                        in0=kvst[:, h * 128:(h + 1) * 128],
                                        scalar=blkd[h],
                                        in1=P2[:, (n_ % 2) * 512:
                                               (n_ % 2) * 512 + 128],
                                        op0=Alu.mult, op1=Alu.add
                                        ).then_inc(cx.sem["dve"], 1)
                                    cx.bump("dve")
                        block.vector(phB_dve)
                        assert cx.cnt["dve"] == dve_cC
                    phB_dve_end = cx.cnt["dve"]
                    phB_pe_end = cx.cnt["pe"]

                with nc.sbuf_tensor(f"s{sidx}_wob", [128, 2 * 8 * D], BF16) as wob:
                    halves = ([list(range(0, 8)), list(range(8, H))]
                              if H > 8 else [list(range(H))])
                    wo_loaded = {}
                    for hi in range(len(halves)):
                        s_ = f"gu{hi % 2}"
                        wo_loaded[hi] = (s_, cx.cnt[s_] + 16)
                        cx.cnt[s_] += 16

                    dve_wo_evac = {}
                    dve_cD = cx.cnt["dve"]
                    for hi in range(len(halves)):
                        for ts_ in range(TT):
                            dve_cD += 1
                            dve_wo_evac[(hi, ts_)] = dve_cD

                    pe_wo = {}

                    def wo_pe(t):
                        cx.w(t, "dve", phB_dve_end)
                        for hi, hl in enumerate(halves):
                            t.wait_ge(cx.sem[wo_loaded[hi][0]], wo_loaded[hi][1])
                            for ts_ in range(TT):
                                prev = ((hi, ts_ - 1) if ts_ > 0 else
                                        ((hi - 1, TT - 1) if hi > 0 else None))
                                if prev is not None:
                                    t.wait_ge(cx.sem["dve"], dve_wo_evac[prev])
                                for gi, h in enumerate(hl):
                                    oap = oT[:, h * TPC + ts_ * 128:
                                             h * TPC + ts_ * 128 + 128]
                                    t.ldweights(oap)
                                    for co, cn in DC:
                                        ii = t.matmul(P0[:, co:co + cn], oap,
                                                      wob[:, (hi % 2) * 8 * D +
                                                          gi * D + co:
                                                          (hi % 2) * 8 * D +
                                                          gi * D + co + cn],
                                                      start=(gi == 0),
                                                      stop=(gi == len(hl) - 1))
                                ii.then_inc(cx.sem["pe"], 1)
                                cx.bump("pe")
                                pe_wo[(hi, ts_)] = cx.cnt["pe"]
                    block.tensor(wo_pe)

                    def wo_ev(v):
                        for hi in range(len(halves)):
                            for ts_ in range(TT):
                                cx.w(v, "pe", pe_wo[(hi, ts_)])
                                xt = x_res[:, ts_ * D:(ts_ + 1) * D]
                                v.tensor_tensor(out=xt, in0=P0[:, 0:D], in1=xt,
                                                op=Alu.add
                                                ).then_inc(cx.sem["dve"], 1)
                                v.drain()
                        cx.cnt["dve"] = dve_cD
                    block.vector(wo_ev)

                    def wo_stream(g):
                        cx.w(g, "pe", phB_pe_end)
                        cx.w(g, "dve", phB_dve_end)
                        for hi, hl in enumerate(halves):
                            g.dma_start(
                                out=wob[:, (hi % 2) * 8 * D:
                                        (hi % 2) * 8 * D + len(hl) * D]
                                .rearrange("p (i d) -> p i d", i=len(hl)),
                                in_=w_o0_d[hl[0]:hl[0] + len(hl)]
                                .rearrange("i p d -> p i d")
                                ).then_inc(cx.sem[wo_loaded[hi][0]], 16)
                    block.gpsimd(wo_stream)

                for ts_ in range(TT):
                    x_ok[ts_] = ("dve", dve_wo_evac[(len(halves) - 1, ts_)])


        f1_state = {}

        def sublayer_ffn1(sidx, first):
            ent = {k: cx.cnt[k] for k in ("pe", "dve", "act")}
            NT = D // 128
            LN2 = math.log(2.0)
            PB1 = P1.bitcast(BF16)
            NBK = 2 if D >= 2048 else 1
            HB_ = D // NBK

            def q_part1(src32, sc, r6_ap, ts_ap, W, parity=0, b0sel=0,
                        pre=None, hazard=()):
                HW = W // 2
                nbh = HW // 16
                amax, bs, q, t, tmp = (sc["amax"], sc["bs"], sc["q"],
                                       sc["t"], sc["tmp"])
                qi, tI, qI = sc["qi"], sc["t"].bitcast(I32), \
                    sc["q"].bitcast(I32)
                co0 = parity * 2 * nbh

                def S(buf, h):
                    return buf[:, co0 + h * nbh:co0 + (h + 1) * nbh]

                A3 = [src32[:, h * HW:(h + 1) * HW]
                      .rearrange("p (a b) -> p a b", b=16) for h in range(2)]
                B0 = [sc["big0"][:, b0sel * W + h * HW:
                                 b0sel * W + (h + 1) * HW] for h in range(2)]
                last = [None, None]

                def op(v, h, instr_fn):
                    if last[h] is not None:
                        v.wait_ge(cx.sem["dve"], last[h])
                    instr_fn().then_inc(cx.sem["dve"], 1)
                    cx.bump("dve")
                    last[h] = cx.cnt["dve"]

                def v1(v):
                    if pre is not None:
                        pre(v)
                    for h in range(2):
                        op(v, h, lambda h=h: v.tensor_reduce(
                            out=S(amax, h), in_=A3[h], axis=AX, op=Alu.max,
                            apply_absolute_value=True))
                    for h in range(2):
                        op(v, h, lambda h=h: v.tensor_scalar(
                            out=S(bs, h), in0=S(amax, h), scalar1=r6_ap,
                            scalar2=None, op0=Alu.mult))
                    for h in range(2):
                        op(v, h, lambda h=h: v.tensor_scalar(
                            out=S(bs, h), in0=S(bs, h), scalar1=448.0,
                            scalar2=None, op0=Alu.min))
                    for h in range(2):
                        op(v, h, lambda h=h: v.tensor_scalar(
                            out=S(t, h), in0=S(bs, h), scalar1=2.0 ** -30,
                            scalar2=None, op0=Alu.max))
                    for h in range(2):
                        op(v, h, lambda h=h: v.tensor_scalar(
                            out=S(qi, h), in0=S(tI, h), scalar1=23,
                            scalar2=None, op0=Alu.logical_shift_right))
                    for h in range(2):
                        op(v, h, lambda h=h: v.tensor_scalar(
                            out=S(qi, h), in0=S(qi, h), scalar1=121,
                            scalar2=-3, op0=Alu.max, op1=Alu.add))
                    for h in range(2):
                        op(v, h, lambda h=h: v.tensor_scalar(
                            out=S(qI, h), in0=S(qi, h), scalar1=23,
                            scalar2=None, op0=Alu.logical_shift_left))
                    for hz in hazard:
                        cx.w(v, hz[0], hz[1])
                    for h in range(2):
                        op(v, h, lambda h=h: v.reciprocal(
                            out=S(tmp, h), in_=S(q, h)))
                    for h in range(2):
                        op(v, h, lambda h=h: v.tensor_tensor(
                            out=S(t, h), in0=S(bs, h), in1=S(tmp, h),
                            op=Alu.mult))
                    for h in range(2):
                        op(v, h, lambda h=h: v.tensor_copy(
                            out=S(qi, h), in_=S(t, h)))
                    for h in range(2):
                        op(v, h, lambda h=h: v.tensor_tensor(
                            out=S(bs, h), in0=S(qi, h), in1=S(q, h),
                            op=Alu.mult))
                    for h in range(2):
                        op(v, h, lambda h=h: v.tensor_scalar(
                            out=S(tmp, h), in0=S(bs, h), scalar1=0.0,
                            scalar2=None, op0=Alu.is_equal))
                    for h in range(2):
                        op(v, h, lambda h=h: v.tensor_tensor(
                            out=S(tmp, h), in0=S(bs, h), in1=S(tmp, h),
                            op=Alu.add))
                    for h in range(2):
                        op(v, h, lambda h=h: v.tensor_scalar(
                            out=S(tmp, h), in0=S(tmp, h), scalar1=ts_ap,
                            scalar2=None, op0=Alu.mult))
                    for h in range(2):
                        op(v, h, lambda h=h: v.reciprocal(
                            out=S(q, h), in_=S(tmp, h)))
                    for h in range(2):
                        qe = S(q, h).unsqueeze(2).broadcast_to([128, nbh, 16])
                        op(v, h, lambda h=h, qe=qe: v.tensor_tensor(
                            out=B0[h].rearrange("p (a b) -> p a b", b=16),
                            in0=A3[h], in1=qe, op=Alu.mult))
                block.vector(v1)
                return {"last": last, "S": S, "B0": B0, "nbh": nbh,
                        "bs": bs, "co0": co0, "W": W, "HW": HW}

            def q_part2(st, dst_bf, sc, hazard=(), act_prev=None):
                HW = st["HW"]
                nbh = st["nbh"]
                B0 = st["B0"]
                S = st["S"]
                bs = st["bs"]
                B1 = [sc["big1"][:, h * HW:(h + 1) * HW] for h in range(2)]
                B2 = [sc["big2"][:, h * HW:(h + 1) * HW] for h in range(2)]
                DST = [dst_bf[:, h * HW:(h + 1) * HW] for h in range(2)]
                D3 = [d.rearrange("p (a b) -> p a b", b=16) for d in DST]
                last = st["last"]
                v3e = list(last)

                def op(v, h, instr_fn):
                    if last[h] is not None:
                        v.wait_ge(cx.sem["dve"], last[h])
                    instr_fn().then_inc(cx.sem["dve"], 1)
                    cx.bump("dve")
                    last[h] = cx.cnt["dve"]

                def a_sa(a):
                    if act_prev is not None:
                        cx.w(a, "act", act_prev)
                    for hz in hazard:
                        cx.w(a, hz[0], hz[1])
                    for h in range(2):
                        a.wait_ge(cx.sem["dve"], v3e[h])
                        a.activation(out=DST[h], in_=B0[h], func=Act.Sign
                                     ).then_inc(cx.sem["act"], 1)
                        cx.bump("act")
                    for h in range(2):
                        a.activation(out=B1[h], in_=B0[h], func=Act.Abs
                                     ).then_inc(cx.sem["act"], 1)
                        cx.bump("act")
                block.scalar(a_sa)
                c_abs = [cx.cnt["act"] - 1, cx.cnt["act"]]

                def v4(v):
                    for h in range(2):
                        v.wait_ge(cx.sem["act"], c_abs[h])
                        op(v, h, lambda h=h: v.tensor_scalar(
                            out=B2[h], in0=B1[h], scalar1=0.25, scalar2=None,
                            op0=Alu.is_ge))
                    for thr in (0.75, 1.25, 1.75):
                        for h in range(2):
                            op(v, h, lambda h=h, thr=thr:
                               v.scalar_tensor_tensor(
                                   out=B2[h], in0=B1[h], scalar=thr,
                                   in1=B2[h], op0=Alu.is_ge, op1=Alu.add))
                    for h in range(2):
                        op(v, h, lambda h=h: v.tensor_scalar(
                            out=B0[h], in0=B1[h], scalar1=2.5, scalar2=None,
                            op0=Alu.is_ge))
                    for h in range(2):
                        op(v, h, lambda h=h: v.scalar_tensor_tensor(
                            out=B0[h], in0=B1[h], scalar=3.5, in1=B0[h],
                            op0=Alu.is_ge, op1=Alu.add))
                    for h in range(2):
                        op(v, h, lambda h=h: v.scalar_tensor_tensor(
                            out=B0[h], in0=B2[h], scalar=0.5, in1=B0[h],
                            op0=Alu.mult, op1=Alu.add))
                    for h in range(2):
                        op(v, h, lambda h=h: v.tensor_scalar(
                            out=B2[h], in0=B1[h], scalar1=5.0, scalar2=None,
                            op0=Alu.is_ge))
                    for h in range(2):
                        op(v, h, lambda h=h: v.scalar_tensor_tensor(
                            out=B0[h], in0=B2[h], scalar=2.0, in1=B0[h],
                            op0=Alu.mult, op1=Alu.add))
                    for h in range(2):
                        op(v, h, lambda h=h: v.tensor_tensor(
                            out=B0[h], in0=B0[h], in1=DST[h], op=Alu.mult))
                    for h in range(2):
                        bse = S(bs, h).unsqueeze(2).broadcast_to([128, nbh, 16])
                        op(v, h, lambda h=h, bse=bse: v.tensor_tensor(
                            out=D3[h],
                            in0=B0[h].rearrange("p (a b) -> p a b", b=16),
                            in1=bse, op=Alu.mult))
                block.vector(v4)
                return list(last)

            scb = f1_scb
            with (
                nc.sbuf_tensor(f"s{sidx}_sc1", [1, 8], F32) as sc1,
                nc.sbuf_tensor(f"s{sidx}_tsx", [1, 1], F32) as tsx_t,
                nc.sbuf_tensor(f"s{sidx}_rsw", [1, 132], F32) as rsw,
            ):
                def mk_scratch(stk, tag, b0depth=1):
                    d = {}
                    for k_, nm in (("amax", "qa0"), ("bs", "qa1"),
                                   ("q", "qa3"),
                                   ("t", "qa4"), ("tmp", "qa6")):
                        d[k_] = stk.enter_context(
                            nc.sbuf_tensor(f"s{sidx}{tag}_{nm}", [128, 256],
                                           F32))
                    d["qi"] = stk.enter_context(
                        nc.sbuf_tensor(f"s{sidx}{tag}_qai", [128, 256], I32))
                    d["big0"] = stk.enter_context(
                        nc.sbuf_tensor(f"s{sidx}{tag}_big0",
                                       [128, b0depth * D], F32))
                    for nm in ("big1", "big2"):
                        d[nm] = stk.enter_context(
                            nc.sbuf_tensor(f"s{sidx}{tag}_{nm}", [128, D],
                                           F32))
                    return d

                import contextlib

                if first:
                    def tsxl(sc_):
                        cx.w(sc_, "act", ent["act"])
                        cx.w(sc_, "dve", ent["dve"])
                        sc_.dma_start(out=tsx_t[:], in_=ts_f1_d).then_inc(
                            cx.sem["dS"], 16)
                        cx.bump("dS", 16)
                    block.scalar(tsxl)

                    wq_d = dt("wq_stash", [KT, NT, 128, 128], BF16,
                              kind="Internal").ap()
                    f1_state["wq_d"] = wq_d
                    with contextlib.ExitStack() as stk:
                        sc = mk_scratch(stk, "w", b0depth=2)
                        wbuf = stk.enter_context(
                            nc.sbuf_tensor(f"s{sidx}_wb", [128, 2 * D], F32))
                        gm = stk.enter_context(
                            nc.sbuf_tensor(f"s{sidx}_gm", [128, NT], F32))
                        wdq = stk.enter_context(
                            nc.sbuf_tensor(f"s{sidx}_wdq", [128, D], BF16))
                        tpc_ = stk.enter_context(
                            nc.sbuf_tensor(f"s{sidx}_tp", [128, 2 * D], BF16))
                        wl = {}
                        for nt in range(NT):
                            s_ = f"hw{nt % 2}"
                            wl[nt] = (s_, cx.cnt[s_] + 16)
                            cx.cnt[s_] += 16
                        gm_red = {}
                        dve_cW = cx.cnt["dve"]
                        for nt in range(NT):
                            dve_cW += 1
                            gm_red[nt] = dve_cW

                        def gml(sync):
                            cx.w(sync, "dve", ent["dve"])
                            cx.w(sync, "act", ent["act"])
                            for nt in range(NT):
                                if nt >= 2:
                                    cx.w(sync, "dve", gm_red[nt - 2])
                                sync.dma_start(out=wbuf[:, (nt % 2) * D:
                                                       (nt % 2) * D + D],
                                               in_=w_f1_d[nt]).then_inc(
                                    cx.sem[wl[nt][0]], 16)
                        block.sync(gml)

                        def gmr(v):
                            for nt in range(NT):
                                v.wait_ge(cx.sem[wl[nt][0]], wl[nt][1])
                                v.tensor_reduce(out=gm[:, nt:nt + 1],
                                                in_=wbuf[:, (nt % 2) * D:
                                                         (nt % 2) * D + D],
                                                axis=AX, op=Alu.max,
                                                apply_absolute_value=True
                                                ).then_inc(cx.sem["dve"], 1)
                            v.wait_ge(cx.sem["dve"], dve_cW)
                            v.tensor_reduce(out=gm[:, 0:1], in_=gm[:, 0:NT],
                                            axis=AX, op=Alu.max)
                            v.drain().then_inc(cx.sem["dve"], 1)
                        cx.cnt["dve"] = dve_cW + 1
                        block.vector(gmr)
                        gm_c = cx.cnt["dve"]

                        def gmt(t):
                            cx.w(t, "dve", gm_c)
                            cx.w(t, "dA", 128)
                            t.transpose(P0[0:1, 0:128], gm[:, 0:1], ident32_t[:]
                                        ).then_inc(cx.sem["pe"], 1)
                        cx.bump("pe")
                        block.tensor(gmt)
                        gmt_c = cx.cnt["pe"]

                        def vv(v, instr):
                            instr.then_inc(cx.sem["dve"], 1)
                            cx.bump("dve")
                            v.drain()

                        def gms(v):
                            cx.w(v, "pe", gmt_c)
                            cx.w(v, "dS", cx.cnt["dS"])
                            vv(v, v.tensor_reduce(out=sc1[0:1, 0:1],
                                                  in_=P0[0:1, 0:128], axis=AX,
                                                  op=Alu.max,
                                                  apply_absolute_value=True))
                            vv(v, v.tensor_scalar(out=sc1[0:1, 2:3],
                                                  in0=sc1[0:1, 0:1],
                                                  scalar1=1.0 / 2688.0,
                                                  scalar2=None, op0=Alu.mult))
                            vv(v, v.tensor_scalar(out=sc1[0:1, 1:2],
                                                  in0=sc1[0:1, 0:1],
                                                  scalar1=6.0 / 2688.0,
                                                  scalar2=None, op0=Alu.mult))
                            vv(v, v.reciprocal(out=sc1[0:1, 1:2],
                                               in_=sc1[0:1, 1:2]))
                            vv(v, v.tensor_copy(out=sc1[0:1, 4:5],
                                                in_=tsx_t[0:1, 0:1]))
                            vv(v, v.tensor_scalar(out=sc1[0:1, 3:4],
                                                  in0=sc1[0:1, 4:5],
                                                  scalar1=6.0, scalar2=None,
                                                  op0=Alu.mult))
                            vv(v, v.reciprocal(out=sc1[0:1, 3:4],
                                               in_=sc1[0:1, 3:4]))
                            vv(v, v.tensor_tensor(out=sc1[0:1, 5:6],
                                                  in0=sc1[0:1, 4:5],
                                                  in1=sc1[0:1, 2:3],
                                                  op=Alu.mult))
                            vv(v, v.tensor_scalar(out=sc1[0:1, 5:6],
                                                  in0=sc1[0:1, 5:6],
                                                  scalar1=0.5, scalar2=None,
                                                  op0=Alu.mult))
                        block.vector(gms)
                        sc1_c = cx.cnt["dve"]

                        def bct(t):
                            cx.w(t, "dve", sc1_c)
                            cx.w(t, "dA", 128)
                            t.matmul(P0[:, 0:6], ones32_t[:], sc1[0:1, 0:6],
                                     start=True, stop=True
                                     ).then_inc(cx.sem["pe"], 1)
                        cx.bump("pe")
                        block.tensor(bct)
                        bct_c = cx.cnt["pe"]

                        def bcv(v):
                            cx.w(v, "pe", bct_c)
                            vv(v, v.tensor_copy(out=scb[:, 0:6], in_=P0[:, 0:6]))
                        block.vector(bcv)
                        scb_c = cx.cnt["dve"]
                        f1_state["scb_c"] = scb_c

                        wtr_done = {}
                        states = {}
                        qe = {}

                        def emit_wload(nt):
                            s_ = f"hw{nt % 2}"
                            ld_c = cx.cnt[s_] + 16
                            cx.cnt[s_] += 16
                            prev = max(qe[nt - 2]) if nt - 2 in qe else scb_c

                            def wload(sync, nt=nt, s_=s_, prev=prev):
                                cx.w(sync, "dve", prev)
                                sync.dma_start(out=wbuf[:, (nt % 2) * D:
                                                       (nt % 2) * D + D],
                                               in_=w_f1_d[nt]).then_inc(
                                    cx.sem[s_], 16)
                            block.sync(wload)
                            return s_, ld_c

                        def emit_p1(nt, s_, ld_c):
                            def pre(v, s_=s_, ld_c=ld_c):
                                v.wait_ge(cx.sem[s_], ld_c)
                                cx.w(v, "dve", scb_c)
                            states[nt] = q_part1(
                                wbuf[:, (nt % 2) * D:(nt % 2) * D + D], sc,
                                scb[:, 1:2], scb[:, 2:3], D,
                                parity=nt % 2, b0sel=nt % 2, pre=pre)

                        ap_cell = {"a": ent["act"]}

                        def emit_p2_tr(nt):
                            hz = ([("dve", wtr_done[nt - 1])]
                                  if nt >= 1 else [])
                            if nt - 1 in qe:
                                hz.append(("dve", max(qe[nt - 1])))
                            qe[nt] = q_part2(states[nt], wdq[:], sc, hazard=hz,
                                             act_prev=ap_cell["a"])
                            ap_cell["a"] = cx.cnt["act"]
                            qd = max(qe[nt])
                            f1_state["wq_done"] = qd
                            tp_pe = cx.cnt["pe"]

                            def wtr(t, nt=nt, qd=qd):
                                cx.w(t, "dve", qd)
                                if nt >= 1:
                                    cx.w(t, "dve", wtr_done[nt - 1])
                                if nt == 0:
                                    cx.w(t, "dA", 128)
                                for kt in range(KT):
                                    t.transpose(PB1[:, kt * 128:kt * 128 + 128],
                                                wdq[:, kt * 128:kt * 128 + 128],
                                                ident_t[:]).then_inc(
                                        cx.sem["pe"], 1)
                            cx.bump("pe", KT)
                            block.tensor(wtr)

                            hwW_c = cx.cnt[f"hwW{nt % 2}"]

                            def wtre(v, nt=nt, tp_pe=tp_pe, hwW_c=hwW_c):
                                for b_ in range(NBK):
                                    v.wait_ge(cx.sem["pe"],
                                              tp_pe + (KT // NBK) * (b_ + 1))
                                    if b_ == 0 and nt >= 2:
                                        v.wait_ge(cx.sem[f"hwW{nt % 2}"], hwW_c)
                                    v.tensor_copy(
                                        out=tpc_[:, (nt % 2) * D + b_ * HB_:
                                                (nt % 2) * D + (b_ + 1) * HB_],
                                        in_=PB1[:, b_ * HB_:(b_ + 1) * HB_]
                                        ).then_inc(cx.sem["dve"], 1)
                            cx.bump("dve", NBK)
                            wtr_done[nt] = cx.cnt["dve"]
                            block.vector(wtre)
                            tre_c = cx.cnt["dve"]

                            def wst(sync, nt=nt, tre_c=tre_c):
                                cx.w(sync, "dve", tre_c)
                                sync.dma_start(
                                    out=wq_d[:, nt].rearrange("k p m -> p k m"),
                                    in_=tpc_[:, (nt % 2) * D:(nt % 2) * D + D]
                                    .rearrange("p (k m) -> p k m", k=KT)
                                    ).then_inc(cx.sem[f"hwW{nt % 2}"], 16)
                                cx.bump(f"hwW{nt % 2}", 16)
                            block.sync(wst)

                        for nt in range(NT):
                            s_, ld_c = emit_wload(nt)
                            emit_p1(nt, s_, ld_c)
                            if nt >= 1:
                                emit_p2_tr(nt - 1)
                        emit_p2_tr(NT - 1)
                        f1_state["wq_w_end"] = (cx.cnt["hwW0"], cx.cnt["hwW1"])
                        f1_state["prep_dve"] = cx.cnt["dve"]
                else:
                    wq_d = f1_state["wq_d"]

                with contextlib.ExitStack() as stk:
                    sc = mk_scratch(stk, "x")
                    xn32 = stk.enter_context(
                        nc.sbuf_tensor(f"s{sidx}_xn32", [128, D], F32))
                    xdq = stk.enter_context(
                        nc.sbuf_tensor(f"s{sidx}_xdq", [128, D], BF16))
                    xdqT = stk.enter_context(
                        nc.sbuf_tensor(f"s{sidx}_xdqT", [128, 2 * D], BF16))
                    wqs = stk.enter_context(
                        nc.sbuf_tensor(f"s{sidx}_wqs", [128, KT * D], BF16))
                    br = stk.enter_context(
                        nc.sbuf_tensor(f"s{sidx}_br", [1, D], F32))
                    scb_c0 = (max(f1_state["scb_c"], f1_state["prep_dve"])
                              if first else f1_state["scb_c"])

                    def wql(sync):
                        if first:
                            sync.wait_ge(cx.sem["hwW0"], f1_state["wq_w_end"][0])
                            sync.wait_ge(cx.sem["hwW1"], f1_state["wq_w_end"][1])
                            cx.w(sync, "dve", f1_state["prep_dve"])
                        else:
                            cx.w(sync, "dve", ent["dve"])
                            cx.w(sync, "act", ent["act"])
                        for kt in range(KT):
                            sync.dma_start(
                                out=wqs[:, kt * D:(kt + 1) * D]
                                .rearrange("p (n m) -> p n m", n=NT),
                                in_=wq_d[kt].rearrange("n p m -> p n m")
                                ).then_inc(cx.sem["hw0"], 16)
                            cx.bump("hw0", 16)
                    block.sync(wql)
                    wq_all = cx.cnt["hw0"]

                    def bld(sc_):
                        cx.w(sc_, "act", ent["act"])
                        cx.w(sc_, "dve", ent["dve"])
                        if first:
                            sc_.wait_ge(cx.sem["hwW0"], f1_state["wq_w_end"][0])
                            sc_.wait_ge(cx.sem["hwW1"], f1_state["wq_w_end"][1])
                            cx.w(sc_, "dve", f1_state["prep_dve"])
                        sc_.dma_start(out=br[:], in_=b_f1_d).then_inc(
                            cx.sem["dS"], 16)
                        cx.bump("dS", 16)
                    block.scalar(bld)
                    bld_c = cx.cnt["dS"]

                    def brs(v):
                        cx.w(v, "dve", scb_c0)
                        v.reciprocal(out=rsw[0:1, 128:129], in_=scb[0:1, 5:6]
                                     ).then_inc(cx.sem["dve"], 1)
                        cx.bump("dve")
                        v.wait_ge(cx.sem["dve"], cx.cnt["dve"])
                        v.tensor_scalar(out=rsw[0:1, 0:128],
                                        in0=ones32_t[0:1, 0:128],
                                        scalar1=rsw[0:1, 128:129],
                                        scalar2=0.5, op0=Alu.mult,
                                        op1=Alu.mult).then_inc(
                            cx.sem["dve"], 1)
                        cx.bump("dve")
                    block.vector(brs)
                    rsw_c = cx.cnt["dve"]

                    te_done = {}
                    mm_done = {}
                    mmev_done = {}

                    def emit_mmev(ts_):
                        def f(v, ts_=ts_):
                            cx.w(v, "pe", mm_done[ts_])
                            xt_ = x_res[:, ts_ * D:(ts_ + 1) * D]
                            v.scalar_tensor_tensor(
                                out=xt_, in0=P0[:, 0:D],
                                scalar=scb[:, 5:6], in1=xt_,
                                op0=Alu.mult, op1=Alu.add).then_inc(
                                cx.sem["dve"], 1)
                            cx.bump("dve")
                        block.vector(f)
                        mmev_done[ts_] = cx.cnt["dve"]
                        x_ok[ts_] = ("dve", cx.cnt["dve"])

                    qe_prev = None
                    xap_cell = {"a": ent["act"]}
                    for tt in range(TT):
                        a_rdy = emit_ln(tt, None, xn32[:], pe_guard=None)

                        def pre(v, a_rdy=a_rdy):
                            cx.w(v, "act", a_rdy)
                            cx.w(v, "dve", scb_c0)
                        st_q = q_part1(xn32[:], sc, scb[:, 3:4], scb[:, 4:5],
                                       D, parity=0, b0sel=0, pre=pre)

                        if tt >= 1:
                            emit_mmev(tt - 1)

                        hz = ([("dve", te_done[tt - 1])] if tt >= 1 else [])
                        if qe_prev is not None:
                            hz.append(("dve", max(qe_prev)))
                        qe_ = q_part2(st_q, xdq[:], sc, hazard=hz,
                                      act_prev=xap_cell["a"])
                        xap_cell["a"] = cx.cnt["act"]
                        qe_prev = qe_
                        qd = max(qe_)

                        tp_pe = cx.cnt["pe"]

                        def xtr(t, tt=tt, qd=qd):
                            cx.w(t, "dve", qd)
                            if tt >= 1:
                                cx.w(t, "dve", te_done[tt - 1])
                            if tt == 0:
                                cx.w(t, "dA", 128)
                            for kt in range(KT):
                                t.transpose(PB1[:, kt * 128:kt * 128 + 128],
                                            xdq[:, kt * 128:kt * 128 + 128],
                                            ident_t[:]).then_inc(cx.sem["pe"], 1)
                        cx.bump("pe", KT)
                        block.tensor(xtr)

                        slot = (tt % 2) * D

                        def xte(v, tt=tt, tp_pe=tp_pe, slot=slot):
                            if tt == 0:
                                cx.w(v, "act", ent["act"])
                            for b in range(NBK):
                                v.wait_ge(cx.sem["pe"],
                                          tp_pe + (KT // NBK) * (b + 1))
                                if tt >= 2 and b == 0:
                                    v.wait_ge(cx.sem["pe"], mm_done[tt - 2])
                                v.tensor_copy(
                                    out=xdqT[:, slot + b * HB_:
                                            slot + (b + 1) * HB_],
                                    in_=PB1[:, b * HB_:(b + 1) * HB_]
                                    ).then_inc(cx.sem["dve"], 1)
                        cx.bump("dve", NBK)
                        te_done[tt] = cx.cnt["dve"]
                        block.vector(xte)

                        def fmm(t, tt=tt, slot=slot):
                            cx.w(t, "dve", te_done[tt])
                            if tt == 0:
                                t.wait_ge(cx.sem["hw0"], wq_all)
                                cx.w(t, "dve", rsw_c)
                                cx.w(t, "dS", bld_c)
                                cx.w(t, "act", ent["act"])
                            else:
                                cx.w(t, "dve", mmev_done[tt - 1])
                            for co, cn in DC:
                                t.matmul(P0[:, co:co + cn], rsw[0:1, 0:128],
                                         br[0:1, co:co + cn], start=True,
                                         stop=False)
                            for kt in range(KT):
                                xap = xdqT[:, slot + kt * 128:
                                           slot + kt * 128 + 128]
                                t.ldweights(xap)
                                for co, cn in DC:
                                    ii = t.matmul(P0[:, co:co + cn], xap,
                                                  wqs[:, kt * D + co:
                                                      kt * D + co + cn],
                                                  start=False,
                                                  stop=(kt == KT - 1))
                            ii.then_inc(cx.sem["pe"], 1)
                        cx.bump("pe", 1)
                        mm_done[tt] = cx.cnt["pe"]
                        block.tensor(fmm)
                    emit_mmev(TT - 1)


        def sublayer_attn1(sidx):
            ent = {k: cx.cnt[k] for k in ("pe", "dve", "act")}
            H1 = D // 128
            S1 = 4 * TPC
            RS = 2.0 ** -3.5
            NP = 3 * H1
            nrm_ev = len(TC) + (1 if len(TC) > 1 else 0)

            with (
                nc.sbuf_tensor(f"s{sidx}_xoT", [128, KT * TPC], BF16) as xoT,
                nc.sbuf_tensor(f"s{sidx}_negM", [128, H1], F32) as negM,
                nc.sbuf_tensor(f"s{sidx}_mq", [128, H1], F32) as mq,
                nc.sbuf_tensor(f"s{sidx}_mkb", [128, H1], BF16) as mkb,
                nc.sbuf_tensor(f"s{sidx}_bqt", [128, 3 * H1], F32) as bqt,
                nc.sbuf_tensor(f"s{sidx}_nrow", [1, 2 * H1], F32) as nrow,
                nc.sbuf_tensor(f"s{sidx}_qf", [128, H1 * TPC], BF16) as qf,
            ):
                xnT = xoT[:, 0:KT * TPC]
                oTn = xoT[:, 0:H1 * TPC]
                Nag1 = 128 * (2 * H1 * TPC + H1)
                cc_in1 = dt(f"s{sidx}_ccin", [1, Nag1], BF16, kind="Internal").ap()
                cc_out1 = dt(f"s{sidx}_ccout", [4, Nag1], BF16, kind="Internal").ap()
                cc_in1v = cc_in1.rearrange("o (p f) -> (o p) f", p=128)

                def ccoutv(r):
                    return cc_out1[r:r + 1].rearrange("o (p f) -> (o p) f", p=128)

                def aux(sc_):
                    cx.w(sc_, "act", ent["act"])
                    cx.w(sc_, "dve", ent["dve"])
                    for i_, nm in enumerate(("bq", "bk", "bv")):
                        sc_.dma_start(
                            out=bqt[:, i_ * H1:(i_ + 1) * H1],
                            in_=b_w1_d[nm].rearrange("o (h p) -> (o p) h", p=128),
                            allow_slow_non_contiguous=True
                            ).then_inc(cx.sem["dS"], 16)
                        cx.bump("dS", 16)
                block.scalar(aux)
                aux_c = cx.cnt["dS"]

                def bprep(v):
                    cx.w(v, "dS", aux_c)
                    v.tensor_scalar(out=bqt[:, 0:H1], in0=bqt[:, 0:H1],
                                    scalar1=RS, scalar2=None, op0=Alu.mult
                                    ).then_inc(cx.sem["dve"], 1)
                    cx.bump("dve")
                    v.drain()
                block.vector(bprep)
                bprep_c = cx.cnt["dve"]

                with (
                    nc.sbuf_tensor(f"s{sidx}_kst", [128, 2 * TPC], BF16) as kst,
                    nc.sbuf_tensor(f"s{sidx}_vtt", [128, 2 * TPC], BF16) as vtt,
                    nc.sbuf_tensor(f"s{sidx}_xn32", [128, D], F32) as xn32,
                    nc.sbuf_tensor(f"s{sidx}_qkb", [128, 2 * KT * 128], BF16) as qkb,
                    nc.sbuf_tensor(f"s{sidx}_ssb", [128, 2 * KT * 128], BF16) as ssb,
                    nc.sbuf_tensor(f"s{sidx}_sq32", [128, TPC], F32) as sq32,
                ):
                    pe_g = None
                    for tt in range(TT):
                        a_rdy = emit_ln(tt, None, xn32[:], pe_guard=pe_g)
                        emit_transposes(
                            KT, lambda kt: xn32[:, kt * 128:(kt + 1) * 128],
                            lambda kt, tt=tt: xnT[:, kt * TPC + tt * 128:
                                                  kt * TPC + (tt + 1) * 128],
                            ("act", a_rdy), extra=("dve", ent["dve"]))
                        pe_g = cx.cnt["pe"]
                    xnT_dve, xnT_act = cx.cnt["dve"], cx.cnt["act"]

                    qk_loaded = {}
                    for si in range(NP):
                        s_ = f"gu{si % 2}"
                        qk_loaded[si] = (s_, cx.cnt[s_] + 32)
                        cx.cnt[s_] += 32

                    def wname(si):
                        return ("wk", si) if si < H1 else \
                               (("wv", si - H1) if si < 2 * H1 else
                                ("wq", si - 2 * H1))

                    psched = [("dq", 0)]
                    if NP > 1:
                        psched.append(("dq", 1))
                    for si in range(2, NP):
                        psched.append(("x", si - 2))
                        psched.append(("dq", si))
                    if NP > 1:
                        psched.append(("x", NP - 2))
                    psched.append(("x", NP - 1))
                    dve_dq = {}
                    dve_nrm = {}
                    dve_vtr = {}
                    dve_c1 = cx.cnt["dve"]
                    for item in psched:
                        si = item[1]
                        nm, _h = wname(si)
                        if item[0] == "dq":
                            dve_c1 += 1
                            dve_dq[si] = dve_c1
                        elif nm in ("wk", "wq"):
                            dve_c1 += nrm_ev
                            dve_nrm[si] = dve_c1
                        else:
                            dve_c1 += TT
                            dve_vtr[si] = dve_c1
                    act_evac = {}
                    act_sq = {}
                    act_c0 = cx.cnt["act"]
                    for si in range(NP):
                        act_c0 += 1
                        act_evac[si] = act_c0
                        if si < H1 or si >= 2 * H1:
                            act_c0 += 1
                            act_sq[si] = act_c0
                    def exp_sem(si):
                        if si < H1:
                            return "hw2" if si % 2 == 0 else "hwA"
                        return "hwB" if si % 2 == 0 else "hwC"
                    hw2_plan = {}
                    _ec = {s: cx.cnt[s] for s in ("hw2", "hwA", "hwB", "hwC")}
                    for si in range(2 * H1):
                        s_ = exp_sem(si)
                        _ec[s_] += 16
                        hw2_plan[si] = (s_, _ec[s_])
                    _ec["hw2"] += 16
                    hw2_all = dict(_ec)
                    for s_ in _ec:
                        cx.cnt[s_] = _ec[s_]

                    pe_proj = {}
                    pe_norm = {}
                    pe_vtr = {}
                    prev_norm = {}
                    lastn = None
                    for si in range(NP):
                        nm, _h = wname(si)
                        if nm in ("wk", "wq"):
                            prev_norm[si] = lastn
                            lastn = si

                    PB2 = P2.bitcast(BF16)

                    def pe_proj_fn(t):
                        cx.w(t, "dve", max(xnT_dve, bprep_c))
                        cx.w(t, "act", xnT_act)
                        for si in range(NP):
                            nm, h = wname(si)
                            t.wait_ge(cx.sem["dve"], dve_dq[si])
                            if si >= 2:
                                t.wait_ge(cx.sem["act"], act_evac[si - 2])
                            ps = P1 if si % 2 == 0 else P0
                            sl = (si % 2) * KT * 128
                            for kt in range(KT):
                                wap = qkb[:, sl + kt * 128:sl + kt * 128 + 128]
                                t.ldweights(wap)
                                for tco, tcn in TC:
                                    ii = t.matmul(ps[:, tco:tco + tcn], wap,
                                                  xnT[:, kt * TPC + tco:
                                                      kt * TPC + tco + tcn],
                                                  start=(kt == 0),
                                                  stop=(kt == KT - 1))
                            ii.then_inc(cx.sem["pe"], 1)
                            cx.bump("pe")
                            pe_proj[si] = cx.cnt["pe"]
                            if nm in ("wk", "wq"):
                                t.wait_ge(cx.sem["act"], act_sq[si])
                                if prev_norm[si] is not None:
                                    t.wait_ge(cx.sem["dve"],
                                              dve_nrm[prev_norm[si]])
                                for ci, (co, cn) in enumerate(TC):
                                    ii = t.matmul(P0[0:1, 1024 + ci * 512:
                                                     1024 + ci * 512 + cn],
                                                  onec_f_t[:],
                                                  sq32[:, co:co + cn],
                                                  start=True, stop=True)
                                ii.then_inc(cx.sem["pe"], 1)
                                cx.bump("pe")
                                pe_norm[si] = cx.cnt["pe"]
                            else:
                                t.wait_ge(cx.sem["act"], act_evac[si])
                                for tt_ in range(TT):
                                    if tt_ >= 2:
                                        t.wait_ge(cx.sem["dve"],
                                                  dve_vtr[si] - TT + tt_ - 1)
                                    elif si - 1 >= H1:
                                        t.wait_ge(cx.sem["dve"],
                                                  dve_vtr[si - 1] + tt_ - 1)
                                    t.transpose(
                                        PB2[:, (tt_ % 2) * 1024:
                                            (tt_ % 2) * 1024 + 128],
                                        vtt[:, (si % 2) * TPC + tt_ * 128:
                                            (si % 2) * TPC + tt_ * 128 + 128],
                                        ident_t[:]).then_inc(cx.sem["pe"], 1)
                                    cx.bump("pe")
                                pe_vtr[si] = cx.cnt["pe"]
                    block.tensor(pe_proj_fn)
                    pe_proj_end = cx.cnt["pe"]

                    def stream1(g):
                        cx.w(g, "pe", ent["pe"])
                        cx.w(g, "dve", ent["dve"])
                        for si in range(NP):
                            nm, h = wname(si)
                            if si >= 2:
                                g.wait_ge(cx.sem["pe"], pe_proj[si - 2])
                            sl = (si % 2) * KT * 128
                            g.dma_start(out=qkb[:, sl:sl + KT * 128]
                                        .rearrange("p (k n) -> p k n", k=KT),
                                        in_=c_w1_d[nm][h]).then_inc(
                                cx.sem[qk_loaded[si][0]], 16)
                            g.dma_start(
                                out=ssb[:, sl:sl + KT * 128]
                                .rearrange("p (k n) -> p k n", k=KT),
                                in_=s_w1_d[nm][h]).then_inc(
                                cx.sem[qk_loaded[si][0]], 16)
                    block.gpsimd(stream1)

                    def act_fn(a):
                        cx.w(a, "dS", aux_c)
                        cx.w(a, "dve", bprep_c)
                        for si in range(NP):
                            nm, h = wname(si)
                            cx.w(a, "pe", pe_proj[si])
                            ps = P1 if si % 2 == 0 else P0
                            if nm == "wk":
                                if si >= 2:
                                    a.wait_ge(cx.sem[hw2_plan[si - 2][0]],
                                              hw2_plan[si - 2][1])
                                a.activation(out=kst[:, (si % 2) * TPC:
                                                    (si % 2) * TPC + TPC],
                                             in_=ps[:, 0:TPC], func=Act.Identity,
                                             bias=bqt[:, H1 + h:H1 + h + 1],
                                             scale=1.0).then_inc(cx.sem["act"], 1)
                                a.drain()
                                if prev_norm[si] is not None:
                                    cx.w(a, "dve", dve_nrm[prev_norm[si]])
                                a.activation(out=sq32[:],
                                             in_=kst[:, (si % 2) * TPC:
                                                    (si % 2) * TPC + TPC],
                                             func=Act.Square
                                             ).then_inc(cx.sem["act"], 1)
                            elif nm == "wv":
                                if si - 2 >= H1:
                                    a.wait_ge(cx.sem[hw2_plan[si - 2][0]],
                                              hw2_plan[si - 2][1])
                                a.activation(out=vtt[:, (si % 2) * TPC:
                                                    (si % 2) * TPC + TPC],
                                             in_=ps[:, 0:TPC], func=Act.Identity,
                                             bias=bqt[:, 2 * H1 + h:
                                                      2 * H1 + h + 1],
                                             scale=1.0).then_inc(cx.sem["act"], 1)
                            else:
                                a.activation(out=qf[:, h * TPC:(h + 1) * TPC],
                                             in_=ps[:, 0:TPC], func=Act.Identity,
                                             bias=bqt[:, h:h + 1],
                                             scale=RS).then_inc(cx.sem["act"], 1)
                                a.drain()
                                if prev_norm[si] is not None:
                                    cx.w(a, "dve", dve_nrm[prev_norm[si]])
                                a.activation(out=sq32[:],
                                             in_=qf[:, h * TPC:(h + 1) * TPC],
                                             func=Act.Square
                                             ).then_inc(cx.sem["act"], 1)
                    cx.cnt["act"] = act_c0
                    block.scalar(act_fn)

                    def dvemix(v):
                        cx.w(v, "dve", xnT_dve)
                        for item in psched:
                            si = item[1]
                            nm, h = wname(si)
                            if item[0] == "dq":
                                v.wait_ge(cx.sem[qk_loaded[si][0]],
                                          qk_loaded[si][1])
                                if si >= 2:
                                    v.wait_ge(cx.sem["pe"], pe_proj[si - 2])
                                sl = (si % 2) * KT * 128
                                v.tensor_tensor(out=qkb[:, sl:sl + KT * 128],
                                                in0=qkb[:, sl:sl + KT * 128],
                                                in1=ssb[:, sl:sl + KT * 128],
                                                op=Alu.mult
                                                ).then_inc(cx.sem["dve"], 1)
                                cx.bump("dve")
                            elif nm in ("wk", "wq"):
                                v.wait_ge(cx.sem["pe"], pe_norm[si])
                                ro = (H1 if nm == "wq" else 0) + h
                                for ci in range(len(TC)):
                                    dst = (nrow[0:1, ro:ro + 1] if ci == 0
                                           else sq32[0:1, 0:1])
                                    cn = TC[ci][1]
                                    v.tensor_reduce(
                                        out=dst,
                                        in_=P0[0:1, 1024 + ci * 512:
                                               1024 + ci * 512 + cn],
                                        axis=AX, op=Alu.max
                                        ).then_inc(cx.sem["dve"], 1)
                                    cx.bump("dve")
                                if len(TC) > 1:
                                    v.wait_ge(cx.sem["dve"], cx.cnt["dve"])
                                    v.tensor_tensor(out=nrow[0:1, ro:ro + 1],
                                                    in0=nrow[0:1, ro:ro + 1],
                                                    in1=sq32[0:1, 0:1],
                                                    op=Alu.max
                                                    ).then_inc(cx.sem["dve"], 1)
                                    cx.bump("dve")
                            else:
                                base_pe = pe_vtr[si] - TT
                                for tt_ in range(TT):
                                    v.wait_ge(cx.sem["pe"], base_pe + tt_ + 1)
                                    v.tensor_copy(
                                        out=vtt[:, (si % 2) * TPC + tt_ * 128:
                                                (si % 2) * TPC + tt_ * 128 + 128],
                                        in_=PB2[:, (tt_ % 2) * 1024:
                                                (tt_ % 2) * 1024 + 128]
                                        ).then_inc(cx.sem["dve"], 1)
                                    cx.bump("dve")
                    block.vector(dvemix)
                    assert cx.cnt["dve"] == dve_c1, (cx.cnt["dve"], dve_c1)

                    def exp_sync(sync):
                        for si in range(2 * H1):
                            nm, h = wname(si)
                            if nm == "wk":
                                cx.w(sync, "act", act_evac[si])
                                sync.dma_start(
                                    out=cc_in1v[:, h * TPC:(h + 1) * TPC],
                                    in_=kst[:, (si % 2) * TPC:
                                            (si % 2) * TPC + TPC]
                                    ).then_inc(cx.sem[hw2_plan[si][0]], 16)
                            else:
                                cx.w(sync, "dve", dve_vtr[si])
                                sync.dma_start(
                                    out=cc_in1v[:, H1 * TPC + h * TPC:
                                                H1 * TPC + (h + 1) * TPC],
                                    in_=vtt[:, (si % 2) * TPC:
                                            (si % 2) * TPC + TPC]
                                    ).then_inc(cx.sem[hw2_plan[si][0]], 16)
                    block.sync(exp_sync)

                    def nb_pe(t):
                        cx.w(t, "dve", dve_c1)
                        t.matmul(P0[:, 1024:1024 + H1], ones32_t[:],
                                 nrow[0:1, 0:H1], start=True, stop=True)
                        ii = t.matmul(P0[:, 1536:1536 + H1], ones32_t[:],
                                      nrow[0:1, H1:2 * H1], start=True,
                                      stop=True)
                        ii.then_inc(cx.sem["pe"], 1)
                        cx.bump("pe")
                    block.tensor(nb_pe)
                    nb_c = cx.cnt["pe"]

                    def nb_v(v):
                        cx.w(v, "pe", nb_c)
                        v.tensor_copy(out=mkb[:], in_=P0[:, 1024:1024 + H1]
                                      ).then_inc(cx.sem["dve"], 1)
                        cx.bump("dve")
                        v.tensor_copy(out=mq[:], in_=P0[:, 1536:1536 + H1]
                                      ).then_inc(cx.sem["dve"], 1)
                        cx.bump("dve")
                    block.vector(nb_v)
                    nbv_c = cx.cnt["dve"]

                    def nexp(sync):
                        cx.w(sync, "dve", nbv_c)
                        sync.dma_start(out=cc_in1v[:, 2 * H1 * TPC:
                                               2 * H1 * TPC + H1],
                                       in_=mkb[:]).then_inc(cx.sem["hw2"], 16)
                    block.sync(nexp)

                def ag1(g):
                    for s_, c_ in hw2_all.items():
                        g.wait_ge(cx.sem[s_], c_)
                    g.collective_compute(
                        "AllGather", Alu.bypass,
                        replica_groups=[[0, 1, 2, 3], [4, 5, 6, 7]],
                        ins=[cc_in1], outs=[cc_out1],
                    ).then_inc(cx.sem["cc"], 1)
                    cx.bump("cc")
                block.gpsimd(ag1)
                cc1_done = cx.cnt["cc"]

                with (
                    nc.sbuf_tensor(f"s{sidx}_kbuf", [128, 2 * S1], BF16) as kbuf,
                    nc.sbuf_tensor(f"s{sidx}_vbuf", [128, 2 * S1], BF16) as vbuf,
                    nc.sbuf_tensor(f"s{sidx}_pT", [128, 2 * TPC], BF16) as pT,
                    nc.sbuf_tensor(f"s{sidx}_rl", [1, TPC], F32) as rl,
                    nc.sbuf_tensor(f"s{sidx}_rlb", [128, TPC], F32) as rlb,
                    nc.sbuf_tensor(f"s{sidx}_mk4", [128, 4 * H1], BF16) as mk4,
                ):
                    def mkld(sync):
                        cx.w(sync, "cc", cc1_done)
                        cx.w(sync, "pe", pe_proj_end)
                        for r in range(4):
                            sync.dma_start(
                                out=mk4[:, r * H1:(r + 1) * H1],
                                in_=ccoutv(r)[:, 2 * H1 * TPC:2 * H1 * TPC + H1]
                                ).then_inc(cx.sem["dA"], 16)
                            cx.bump("dA", 16)
                    block.sync(mkld)
                    mk_c = cx.cnt["dA"]

                    def negm_v(v):
                        cx.w(v, "dA", mk_c)
                        for r in range(1, 4):
                            v.tensor_tensor(out=mk4[:, 0:H1], in0=mk4[:, 0:H1],
                                            in1=mk4[:, r * H1:(r + 1) * H1],
                                            op=Alu.max).then_inc(cx.sem["dve"], 1)
                            cx.bump("dve")
                            v.drain()
                        v.tensor_copy(out=negM[:], in_=mk4[:, 0:H1]
                                      ).then_inc(cx.sem["dve"], 1)
                        cx.bump("dve")
                        v.drain()
                        v.tensor_tensor(out=negM[:], in0=negM[:], in1=mq[:],
                                        op=Alu.mult).then_inc(cx.sem["dve"], 1)
                        cx.bump("dve")
                        v.drain()
                    block.vector(negm_v)
                    c_nm1 = cx.cnt["dve"]

                    def negm_a(a):
                        cx.w(a, "dve", c_nm1)
                        a.activation(out=negM[:], in_=negM[:], func=Act.Sqrt,
                                     bias=0.0, scale=1.0
                                     ).then_inc(cx.sem["act"], 1)
                        cx.bump("act")
                    block.scalar(negm_a)
                    c_nm2 = cx.cnt["act"]

                    def negm_v2(v):
                        cx.w(v, "act", c_nm2)
                        v.tensor_scalar(out=negM[:], in0=negM[:], scalar1=-1.0,
                                        scalar2=None, op0=Alu.mult
                                        ).then_inc(cx.sem["dve"], 1)
                        cx.bump("dve")
                        v.drain()
                    block.vector(negm_v2)
                    negm_c = cx.cnt["dve"]

                    NJT = S1 // 128
                    peQ = {}
                    peL = {}
                    peAV = {}
                    peRB = {}
                    actE = {}
                    act_cB = cx.cnt["act"]
                    for h in range(H1):
                        for jt in range(NJT):
                            act_cB += 1
                            actE[(h, jt)] = act_cB
                    dve_tail = {}
                    dve_cT = cx.cnt["dve"]
                    for h in range(H1):
                        dve_cT += 3
                        dve_tail[h] = dve_cT
                    kv_plan = {}
                    tmpc = {0: cx.cnt["hw0"], 1: cx.cnt["hw1"]}
                    for h in range(H1):
                        tmpc[h % 2] += 128
                        kv_plan[h] = (f"hw{h % 2}", tmpc[h % 2])

                    def core_pe(t):
                        cx.w(t, "dve", negm_c)

                        def emit_qk(t, h, jt):
                            n_ = h * NJT + jt
                            if n_ >= 2:
                                pj = n_ - 2
                                t.wait_ge(cx.sem["act"],
                                          actE[(pj // NJT, pj % NJT)])
                            kap = kbuf[:, (h % 2) * S1 + jt * 128:
                                       (h % 2) * S1 + jt * 128 + 128]
                            t.ldweights(kap)
                            for ci, (co, cn) in enumerate(TC):
                                ii = t.matmul(
                                    P0[:, (jt % 2) * 1024 + ci * 512:
                                       (jt % 2) * 1024 + ci * 512 + cn],
                                    kap,
                                    qf[:, h * TPC + co:h * TPC + co + cn],
                                    start=True, stop=True)
                            ii.then_inc(cx.sem["pe"], 1)
                            cx.bump("pe")
                            peQ[(h, jt)] = cx.cnt["pe"]

                        def emit_lav(t, h, jt):
                            t.wait_ge(cx.sem["act"], actE[(h, jt)])
                            t.ldweights(onec_b_t[:])
                            for ci, (co, cn) in enumerate(TC):
                                ii = t.matmul(
                                    P2[0:1, ci * 512:ci * 512 + cn],
                                    onec_b_t[:],
                                    pT[:, (jt % 2) * TPC + co:
                                       (jt % 2) * TPC + co + cn],
                                    start=(jt == 0), stop=(jt == NJT - 1))
                            ii.then_inc(cx.sem["pe"], 1)
                            cx.bump("pe")
                            peL[(h, jt)] = cx.cnt["pe"]
                            vap = vbuf[:, (h % 2) * S1 + jt * 128:
                                       (h % 2) * S1 + jt * 128 + 128]
                            t.ldweights(vap)
                            for ci, (co, cn) in enumerate(TC):
                                ii = t.matmul(
                                    P1[:, co:co + cn], vap,
                                    pT[:, (jt % 2) * TPC + co:
                                       (jt % 2) * TPC + co + cn],
                                    start=(jt == 0), stop=(jt == NJT - 1))
                            ii.then_inc(cx.sem["pe"], 1)
                            cx.bump("pe")
                            peAV[(h, jt)] = cx.cnt["pe"]

                        for h in range(H1):
                            t.wait_ge(cx.sem[kv_plan[h][0]], kv_plan[h][1])
                            if h > 0:
                                t.wait_ge(cx.sem["dve"], dve_tail[h - 1])
                            for jt in range(NJT):
                                emit_qk(t, h, jt)
                                if jt >= 1:
                                    emit_lav(t, h, jt - 1)
                            emit_lav(t, h, NJT - 1)
                            t.wait_ge(cx.sem["dve"], dve_tail[h] - 2)
                            for ci, (co, cn) in enumerate(TC):
                                ii = t.matmul(P0[:, co:co + cn], ones32_t[:],
                                              rl[0:1, co:co + cn],
                                              start=True, stop=True)
                            ii.then_inc(cx.sem["pe"], 1)
                            cx.bump("pe")
                            peRB[h] = cx.cnt["pe"]
                    block.tensor(core_pe)
                    core_pe_end = cx.cnt["pe"]

                    def kvld(sync):
                        cx.w(sync, "pe", pe_proj_end)
                        for h in range(H1):
                            s_ = f"hw{h % 2}"
                            if h >= 2:
                                cx.w(sync, "pe", peAV[(h - 2, NJT - 1)])
                            cx.w(sync, "cc", cc1_done)
                            for r in range(4):
                                sync.dma_start(
                                    out=kbuf[:, (h % 2) * S1 + r * TPC:
                                            (h % 2) * S1 + (r + 1) * TPC],
                                    in_=ccoutv(r)[:, h * TPC:(h + 1) * TPC]
                                    ).then_inc(cx.sem[s_], 16)
                                sync.dma_start(
                                    out=vbuf[:, (h % 2) * S1 + r * TPC:
                                            (h % 2) * S1 + (r + 1) * TPC],
                                    in_=ccoutv(r)[:, H1 * TPC + h * TPC:
                                                  H1 * TPC + (h + 1) * TPC]
                                    ).then_inc(cx.sem[s_], 16)
                                cx.bump(s_, 32)
                            assert cx.cnt[s_] == kv_plan[h][1]
                    block.sync(kvld)

                    def core_act(a):
                        cx.w(a, "dve", negm_c)
                        for h in range(H1):
                            for jt in range(NJT):
                                n_ = h * NJT + jt
                                cx.w(a, "pe", peQ[(h, jt)])
                                if n_ >= 2:
                                    pj = n_ - 2
                                    cx.w(a, "pe", peAV[(pj // NJT, pj % NJT)])
                                a.activation(
                                    out=pT[:, (jt % 2) * TPC:(jt % 2) * TPC + TPC],
                                    in_=P0[:, (jt % 2) * 1024:
                                           (jt % 2) * 1024 + TPC],
                                    func=Act.Exp, bias=negM[:, h:h + 1],
                                    scale=1.0).then_inc(cx.sem["act"], 1)
                    cx.cnt["act"] = act_cB
                    block.scalar(core_act)

                    def core_dve(v):
                        for h in range(H1):
                            v.wait_ge(cx.sem["pe"], peL[(h, NJT - 1)])
                            v.tensor_copy(out=rl[0:1, 0:TPC],
                                          in_=P2[0:1, 0:TPC]
                                          ).then_inc(cx.sem["dve"], 1)
                            cx.bump("dve")
                            v.wait_ge(cx.sem["pe"], peRB[h])
                            v.reciprocal(out=rlb[:], in_=P0[:, 0:TPC]
                                         ).then_inc(cx.sem["dve"], 1)
                            cx.bump("dve")
                            v.wait_ge(cx.sem["dve"], cx.cnt["dve"])
                            v.wait_ge(cx.sem["pe"], peAV[(h, NJT - 1)])
                            v.tensor_tensor(out=oTn[:, h * TPC:(h + 1) * TPC],
                                            in0=P1[:, 0:TPC], in1=rlb[:],
                                            op=Alu.mult).then_inc(cx.sem["dve"], 1)
                            cx.bump("dve")
                            v.drain()
                    block.vector(core_dve)
                    assert cx.cnt["dve"] == dve_cT
                    core_dve_end = cx.cnt["dve"]
                    core_act_end = cx.cnt["act"]

                with (
                    nc.sbuf_tensor(f"s{sidx}_wob", [128, 3 * 4 * D], BF16) as wob,
                    nc.sbuf_tensor(f"s{sidx}_sso", [128, 2 * D], BF16) as sso,
                    nc.sbuf_tensor(f"s{sidx}_bo", [1, D], BF16) as bo_t,
                ):
                    EQ = [list(range(q * 4, min(H1, q * 4 + 4)))
                          for q in range((H1 + 3) // 4)]
                    bo_ld = ("dqB", cx.cnt["dqB"] + 16)
                    cx.cnt["dqB"] += 16
                    et_loaded = {}
                    for et in range(H1):
                        s_ = f"dq{et % 3}"
                        et_loaded[et] = (s_, cx.cnt[s_] + 32)
                        cx.cnt[s_] += 32

                    osched = []
                    for et in range(min(12, H1)):
                        osched.append(("dq", et))
                    for qi_ in range(len(EQ)):
                        for ts_ in range(TT):
                            osched.append(("ev", qi_, ts_, 0))
                            if D >= 1024:
                                osched.append(("ev", qi_, ts_, 1))
                        et2 = 12 + qi_ * 4
                        for e2 in range(et2, min(et2 + 4, H1)):
                            osched.append(("dq", e2))
                    dve_dqo = {}
                    dve_oev = {}
                    dve_cO = cx.cnt["dve"]
                    for item in osched:
                        dve_cO += 1
                        if item[0] == "dq":
                            dve_dqo[item[1]] = dve_cO
                        else:
                            dve_oev[(item[1], item[2], item[3])] = dve_cO

                    pe_o = {}

                    def opj_pe(t):
                        cx.w(t, "dve", core_dve_end)
                        cx.w(t, "act", core_act_end)
                        for qi_, eq in enumerate(EQ):
                            for e in eq:
                                t.wait_ge(cx.sem["dve"], dve_dqo[e])
                            if qi_ == 0:
                                t.wait_ge(cx.sem["dqB"], bo_ld[1])
                            HDC = (DC[:len(DC) // 2] if len(DC) > 1
                                   else list(DC))
                            HD2 = DC[len(DC) // 2:] if len(DC) > 1 else []
                            for ts_ in range(TT):
                                for hf, dcs in enumerate((HDC, HD2)):
                                    if not dcs:
                                        pe_o[(qi_, ts_, hf)] = cx.cnt["pe"]
                                        continue
                                    prev = ((qi_, ts_ - 1, hf) if ts_ > 0 else
                                            ((qi_ - 1, TT - 1, hf)
                                             if qi_ > 0 else None))
                                    if prev is not None:
                                        t.wait_ge(cx.sem["dve"],
                                                  dve_oev[prev])
                                    for gi, e in enumerate(eq):
                                        oap = oTn[:, e * TPC + ts_ * 128:
                                                  e * TPC + ts_ * 128 + 128]
                                        t.ldweights(oap)
                                        stop_ = (qi_ != 0) and \
                                            (gi == len(eq) - 1)
                                        for co, cn in dcs:
                                            ii = t.matmul(
                                                P0[:, co:co + cn], oap,
                                                wob[:, (qi_ % 3) * 4 * D +
                                                    gi * D + co:
                                                    (qi_ % 3) * 4 * D +
                                                    gi * D + co + cn],
                                                start=(gi == 0), stop=stop_)
                                    if qi_ == 0:
                                        t.ldweights(ones_t[:])
                                        for co, cn in dcs:
                                            ii = t.matmul(
                                                P0[:, co:co + cn],
                                                ones_t[:],
                                                bo_t[0:1, co:co + cn],
                                                start=False, stop=True)
                                    ii.then_inc(cx.sem["pe"], 1)
                                    cx.bump("pe")
                                    pe_o[(qi_, ts_, hf)] = cx.cnt["pe"]
                    block.tensor(opj_pe)
                    opj_pe_end = cx.cnt["pe"]

                    def opj_v(v):
                        cx.w(v, "dve", core_dve_end)
                        for item in osched:
                            if item[0] == "dq":
                                et = item[1]
                                v.wait_ge(cx.sem[et_loaded[et][0]],
                                          et_loaded[et][1])
                                if et >= 12:
                                    v.wait_ge(cx.sem["pe"],
                                              pe_o[((et - 12) // 4, TT - 1,
                                                    1 if D >= 1024 else 0)])
                                sl = ((et // 4) % 3) * 4 * D + (et % 4) * D
                                v.tensor_tensor(
                                    out=wob[:, sl:sl + D],
                                    in0=wob[:, sl:sl + D],
                                    in1=sso[:, (et % 2) * D:(et % 2) * D + D],
                                    op=Alu.mult).then_inc(cx.sem["dve"], 1)
                                cx.bump("dve")
                            else:
                                qi_, ts_, hf = item[1], item[2], item[3]
                                if len(DC) > 1:
                                    ho, hw = hf * (D // 2), D // 2
                                else:
                                    ho, hw = 0, D
                                cx.w(v, "pe", pe_o[(qi_, ts_, hf)])
                                xt = x_res[:, ts_ * D + ho:ts_ * D + ho + hw]
                                v.tensor_tensor(out=xt, in0=P0[:, ho:ho + hw],
                                                in1=xt, op=Alu.add
                                                ).then_inc(cx.sem["dve"], 1)
                                cx.bump("dve")
                    block.vector(opj_v)
                    assert cx.cnt["dve"] == dve_cO

                    def opj_st(g):
                        cx.w(g, "pe", core_pe_end)
                        cx.w(g, "dve", core_dve_end)
                        g.dma_start(out=bo_t[:], in_=b_w1_d["bo"]).then_inc(
                            cx.sem["dqB"], 16)
                        for et in range(H1):
                            if et >= 12:
                                g.wait_ge(cx.sem["pe"],
                                          pe_o[((et - 12) // 4, TT - 1,
                                                1 if D >= 1024 else 0)])
                            if et >= 2:
                                g.wait_ge(cx.sem["dve"], dve_dqo[et - 2])
                            sl = ((et // 4) % 3) * 4 * D + (et % 4) * D
                            g.dma_start(out=wob[:, sl:sl + D],
                                        in_=c_w1_d["wo"][et]).then_inc(
                                cx.sem[et_loaded[et][0]], 16)
                            g.dma_start(
                                out=sso[:, (et % 2) * D:(et % 2) * D + D],
                                in_=s_w1_d["wo"][et]).then_inc(
                                cx.sem[et_loaded[et][0]], 16)
                    block.gpsimd(opj_st)

            for ts_ in range(TT):
                x_ok[ts_] = ("dve", max(
                    dve_oev[(len(EQ) - 1, ts_, 0)],
                    dve_oev.get((len(EQ) - 1, ts_, 1), 0)))


        with nc.psum_tensor("P2", [128, 1024], F32) as P2:
            P2box[0] = P2
            for sidx, name in enumerate(subs):
                if name in ("ffn0a", "ffn0b"):
                    sublayer_ffn0(sidx)
                elif name == "attn0":
                    sublayer_attn0(sidx)
                elif name in ("ffn1a", "ffn1b"):
                    sublayer_ffn1(sidx, first=("wq_d" not in f1_state))
                elif name == "attn1":
                    sublayer_attn1(sidx)

            def store(sync):
                for tt in range(TT):
                    sem0, c0 = x_ok[tt]
                    cx.w(sync, sem0, c0)
                    sync.dma_start(out=y_out[tt],
                                   in_=x_res[:, tt * D:(tt + 1) * D]
                                   ).then_inc(cx.sem["dO"], 16)
                    cx.bump("dO", 16)
                cx.w(sync, "dO", cx.cnt["dO"])
            block.sync(store)

    return nc, taps


def kernel(**inputs):
    cfg = make_cfg(full=True)
    in_maps = host_prepare(cfg, inputs)
    nc, _ = build(cfg)
    res = run_bass_kernel_spmd(nc, in_maps, core_ids=list(range(NCORES)))
    B, S, D, TPC = cfg["B"], cfg["S"], cfg["D"], cfg["TPC"]
    out = np.empty((B * S, D), np.float32)
    for c in range(NCORES):
        out[c * TPC:(c + 1) * TPC] = res.results[c]["y_out"].reshape(TPC, D)
    return out.reshape(B, S, D)


# version 14 (final)
# speedup vs baseline: 1.4347x; 1.0238x over previous
import math
import numpy as np
import ml_dtypes
import concourse.bass as bass
import concourse.mybir as mybir
from concourse.bass_utils import run_bass_kernel_spmd

F32 = mybir.dt.float32
I32 = mybir.dt.int32
BF16 = mybir.dt.bfloat16
Alu = mybir.AluOpType
Act = mybir.ActivationFunctionType
AX = mybir.AxisListType.X
BF = ml_dtypes.bfloat16

NCORES = 8


def make_cfg(full=True):
    c = {}
    if full:
        c.update(S=4096, D=2048, H=16, HKV=4, INTER=5632)
    else:
        c.update(S=1024, D=512, H=4, HKV=1, INTER=1408)
    c["B"] = 2
    c["DH"] = 128
    c["CHUNK"] = 64
    c["TPC"] = c["B"] * c["S"] // NCORES
    c["TT"] = c["TPC"] // 128
    c["KT"] = c["D"] // 128
    c["IT"] = c["INTER"] // 128
    c["L1H"] = c["D"] // 128
    c["NCH"] = c["TPC"] // c["CHUNK"]
    c["sublayers"] = ["ffn0a", "attn0", "ffn0b", "ffn1a", "attn1", "ffn1b"]
    c["taps"] = []
    return c



def _lhsT_tiles(w_T):
    K, N = w_T.shape
    a = w_T.reshape(K // 128, 128, N // 128, 128)
    return np.ascontiguousarray(a.transpose(2, 1, 0, 3))


def _moving_tiles(w_T):
    K, N = w_T.shape
    return np.ascontiguousarray(w_T.reshape(K // 128, 128, N))


def _part_vec(v, tiles):
    return np.ascontiguousarray(np.asarray(v, np.float32).reshape(tiles, 128).T)


def host_prepare(cfg, I):
    B, S, D, H, HKV = cfg["B"], cfg["S"], cfg["D"], cfg["H"], cfg["HKV"]
    DH, CHUNK, TPC, TT, KT, IT = (cfg["DH"], cfg["CHUNK"], cfg["TPC"],
                                  cfg["TT"], cfg["KT"], cfg["IT"])
    NCH = cfg["NCH"]
    f32 = np.float32

    shared = {}
    gW = _lhsT_tiles(np.ascontiguousarray(np.asarray(I["L0_ffn_gate_w"], f32).T))
    uW = _lhsT_tiles(np.ascontiguousarray(np.asarray(I["L0_ffn_up_w"], f32).T))
    shared["w_gu"] = np.ascontiguousarray(np.stack([gW, uW], axis=2))
    shared["w_down"] = _moving_tiles(
        np.ascontiguousarray(np.asarray(I["L0_ffn_down_w"], f32).T))
    shared["gate_s"] = _part_vec(I["L0_ffn_gate_s"], IT)
    shared["up_s"] = _part_vec(I["L0_ffn_up_s"], IT)
    shared["down_s"] = np.asarray(I["L0_ffn_down_s"], f32).reshape(1, D)

    shared["w_q0"] = _lhsT_tiles(np.ascontiguousarray(np.asarray(I["L0_attn_wq"], f32).T))
    shared["w_k0"] = _lhsT_tiles(np.ascontiguousarray(np.asarray(I["L0_attn_wk"], f32).T))
    shared["w_v0"] = _lhsT_tiles(np.ascontiguousarray(np.asarray(I["L0_attn_wv"], f32).T))
    shared["w_o0"] = _moving_tiles(np.ascontiguousarray(np.asarray(I["L0_attn_wo"], f32).T))

    for nm in ("wq", "wk", "wv"):
        cd = np.asarray(I[f"L1_attn_{nm}_codes"], f32)
        shared[f"c_{nm}1"] = _lhsT_tiles(cd)
        sc = np.repeat(np.asarray(I[f"L1_attn_{nm}_scales"], f32), 32, axis=0)
        shared[f"s_{nm}1"] = _lhsT_tiles(sc)
    cd = np.asarray(I["L1_attn_wo_codes"], f32)
    shared["c_wo1"] = _moving_tiles(cd)
    sc = np.repeat(np.asarray(I["L1_attn_wo_scales"], f32), 32, axis=0)
    shared["s_wo1"] = _moving_tiles(sc)
    for nm in ("bq", "bk", "bv", "bo"):
        shared[f"{nm}1"] = np.asarray(I[f"L1_attn_{nm}"], f32).reshape(1, -1)

    shared["w_f1"] = np.ascontiguousarray(
        np.asarray(I["L1_ffn_weight"], f32).reshape(D // 128, 128, D))
    shared["b_f1"] = np.asarray(I["L1_ffn_bias"], f32).reshape(1, D)
    shared["ts_f1"] = np.asarray(I["L1_ffn_x_tensor_scale"], f32).reshape(1, 1)

    shared["ident"] = np.eye(128, dtype=BF)
    shared["ident32"] = np.eye(128, dtype=f32)
    shared["ones_row"] = np.ones((1, 128), dtype=BF)
    shared["ones32"] = np.ones((1, 128), f32)
    shared["onec_b"] = np.ones((128, 1), BF)
    shared["onec_f"] = np.ones((128, 1), f32)
    shared["cvals"] = np.broadcast_to(
        np.array([0.0, 1.0, 1e-5, math.log(2.0), -3.0 * math.log(2.0),
                  2.0 ** -3.5], f32), (128, 6)).copy()
    lam = (2.0 ** (-8.0 * np.arange(1, H + 1) / H)).astype(np.float64)
    r = np.arange(CHUNK, dtype=np.float64)
    diag = np.where(r[:, None] - r[None, :] >= 0,
                    np.exp(-lam[:, None, None] * (r[:, None] - r[None, :])), 0.0)
    dT = np.zeros((CHUNK, H * CHUNK), f32)
    for h in range(H):
        dT[:, h * CHUNK:(h + 1) * CHUNK] = diag[h].T
    shared["diagT"] = np.concatenate([dT, dT], axis=0)
    qdec = np.exp(-lam[:, None] * (r[None, :] + 1.0))
    kdec = np.exp(-lam[:, None] * (CHUNK - 1.0 - r[None, :]))
    shared["qdec"] = np.broadcast_to(
        qdec.reshape(1, H * CHUNK).astype(BF), (128, H * CHUNK)).copy()
    kd = np.zeros((128, H), f32)
    for h in range(H):
        kd[:, h] = np.tile(kdec[h], 2)
    shared["kdec"] = kd

    x = np.asarray(I["L0_attn_x"], f32).reshape(B * S, D)
    in_maps = []
    for core in range(NCORES):
        m = dict(shared)
        xs = x[core * TPC:(core + 1) * TPC]
        m["x_in"] = np.ascontiguousarray(xs.reshape(TT, 128, D))
        pos0 = (core % (NCORES // B)) * TPC
        t = np.arange(pos0, pos0 + TPC, dtype=np.float64)
        inv = 1.0 / 10000.0 ** (np.arange(0, DH, 2, dtype=np.float64) / DH)
        fr = t[None, :] * inv[:, None]
        m["ropec"] = np.concatenate([np.cos(fr), np.cos(fr)], 0).astype(BF)
        m["ropes"] = np.concatenate([-np.sin(fr), np.sin(fr)], 0).astype(BF)
        rank = core % 4
        blkspan = np.exp(-lam * CHUNK) ** NCH
        coef = np.zeros((4, H), np.float64)
        for j in range(rank):
            coef[j] = blkspan ** (rank - 1 - j)
        m["coef"] = np.broadcast_to(
            coef.reshape(1, 4 * H).astype(f32), (128, 4 * H)).copy()
        in_maps.append(m)
    return in_maps



class Ctx:

    def __init__(self, nc):
        self.nc = nc
        self.sem = {}
        self.cnt = {}
        self._cm = []

    def mksem(self, *names):
        for n in names:
            cm = self.nc.semaphore(n)
            self.sem[n] = cm.__enter__()
            self._cm.append(cm)
            self.cnt[n] = 0

    def inc(self, name, instr, n=1):
        instr.then_inc(self.sem[name], n)
        self.cnt[name] += n
        return self.cnt[name]

    def bump(self, name, n=1):
        self.cnt[name] += n
        return self.cnt[name]

    def w(self, eng, name, val):
        eng.wait_ge(self.sem[name], val)


def build(cfg):
    B, S, D, H, HKV = cfg["B"], cfg["S"], cfg["D"], cfg["H"], cfg["HKV"]
    DH, CHUNK, TPC, TT, KT, IT = (cfg["DH"], cfg["CHUNK"], cfg["TPC"],
                                  cfg["TT"], cfg["KT"], cfg["IT"])
    INTER, H1, NCH = cfg["INTER"], cfg["L1H"], cfg["NCH"]
    subs = cfg["sublayers"]
    taps = {}
    TC = [(o, min(512, TPC - o)) for o in range(0, TPC, 512)]
    NDC = D // 512 if D >= 512 else 1
    DC = [(o, min(512, D - o)) for o in range(0, D, 512)]

    nc = bass.Bass("TRN2", target_bir_lowering=False, debug=False)
    dt = nc.dram_tensor

    x_in = dt("x_in", [TT, 128, D], F32, kind="ExternalInput").ap()
    y_out = dt("y_out", [TT, 128, D], F32, kind="ExternalOutput").ap()
    w_gu_d = dt("w_gu", [IT, 128, 2, KT, 128], F32, kind="ExternalInput").ap()
    w_down_d = dt("w_down", [IT, 128, D], F32, kind="ExternalInput").ap()
    gate_s_d = dt("gate_s", [128, IT], F32, kind="ExternalInput").ap()
    up_s_d = dt("up_s", [128, IT], F32, kind="ExternalInput").ap()
    down_s_d = dt("down_s", [1, D], F32, kind="ExternalInput").ap()
    ident_d = dt("ident", [128, 128], BF16, kind="ExternalInput").ap()
    ident32_d = dt("ident32", [128, 128], F32, kind="ExternalInput").ap()
    ones_row_d = dt("ones_row", [1, 128], BF16, kind="ExternalInput").ap()
    cvals_d = dt("cvals", [128, 6], F32, kind="ExternalInput").ap()
    ones32_d = dt("ones32", [1, 128], F32, kind="ExternalInput").ap()
    onec_b_d = dt("onec_b", [128, 1], BF16, kind="ExternalInput").ap()
    onec_f_d = dt("onec_f", [128, 1], F32, kind="ExternalInput").ap()
    w_q0_d = dt("w_q0", [H, 128, KT, 128], F32, kind="ExternalInput").ap()
    w_k0_d = dt("w_k0", [HKV, 128, KT, 128], F32, kind="ExternalInput").ap()
    w_v0_d = dt("w_v0", [HKV, 128, KT, 128], F32, kind="ExternalInput").ap()
    w_o0_d = dt("w_o0", [H, 128, D], F32, kind="ExternalInput").ap()
    ropec_d = dt("ropec", [128, TPC], BF16, kind="ExternalInput").ap()
    ropes_d = dt("ropes", [128, TPC], BF16, kind="ExternalInput").ap()
    diagT_d = dt("diagT", [128, H * CHUNK], F32, kind="ExternalInput").ap()
    qdec_d = dt("qdec", [128, H * CHUNK], BF16, kind="ExternalInput").ap()
    kdec_d = dt("kdec", [128, H], F32, kind="ExternalInput").ap()
    coef_d = dt("coef", [128, 4 * H], F32, kind="ExternalInput").ap()
    c_w1_d = {nm: dt(f"c_{nm}1", [D // 128, 128, KT, 128], F32,
                     kind="ExternalInput").ap() for nm in ("wq", "wk", "wv")}
    c_w1_d["wo"] = dt("c_wo1", [KT, 128, D], F32, kind="ExternalInput").ap()
    s_w1_d = {nm: dt(f"s_{nm}1", [D // 128, 128, KT, 128], F32,
                     kind="ExternalInput").ap() for nm in ("wq", "wk", "wv")}
    s_w1_d["wo"] = dt("s_wo1", [KT, 128, D], F32, kind="ExternalInput").ap()
    b_w1_d = {nm: dt(f"{nm}1", [1, D], F32, kind="ExternalInput").ap()
              for nm in ("bq", "bk", "bv", "bo")}
    w_f1_d = dt("w_f1", [KT, 128, D], F32, kind="ExternalInput").ap()
    b_f1_d = dt("b_f1", [1, D], F32, kind="ExternalInput").ap()
    ts_f1_d = dt("ts_f1", [1, 1], F32, kind="ExternalInput").ap()

    cx = Ctx(nc)
    cx.mksem("dA", "dS", "pe", "dve", "act", "gp", "cc", "dO")
    cx.mksem("gu0", "gu1", "dq0", "dq1", "dq2", "sw0", "sw1")
    cx.mksem("hw0", "hw1", "hw2", "hwA", "hwB", "hwC", "hwW0", "hwW1", "dqB")

    def tap(name, shape, dtype=F32):
        if name in cfg["taps"]:
            taps[name] = dt("tap_" + name, list(shape), dtype,
                            kind="ExternalOutput").ap()
            return taps[name]
        return None

    with (
        nc.sbuf_tensor("x_res", [128, TT * D], F32) as x_res,
        nc.sbuf_tensor("ident_t", [128, 128], BF16) as ident_t,
        nc.sbuf_tensor("ident32_t", [128, 128], F32) as ident32_t,
        nc.sbuf_tensor("ones_t", [1, 128], BF16) as ones_t,
        nc.sbuf_tensor("cvals_t", [128, 6], F32) as cvals_t,
        nc.sbuf_tensor("ones32_t", [1, 128], F32) as ones32_t,
        nc.sbuf_tensor("f1_scb", [128, 8], F32) as f1_scb,
        nc.sbuf_tensor("onec_b_t", [128, 1], BF16) as onec_b_t,
        nc.sbuf_tensor("onec_f_t", [128, 1], F32) as onec_f_t,
        nc.sbuf_tensor("lnsc", [128, 36], F32) as lnsc,
        nc.sbuf_tensor("ln_sc", [128, 8 * TT], F32) as ln_sc,
        nc.psum_tensor("P0", [128, 2048], F32) as P0,
        nc.psum_tensor("P1", [128, 1024], F32) as P1,
        nc.Block() as block,
    ):
        x_ok = [("dA", 128)] * TT
        for _ci, _cv in enumerate([0.0, 1.0, 1e-5, math.log(2.0),
                                   -3.0 * math.log(2.0), 2.0 ** -3.5]):
            nc.const_aps.register(F32, _cv, cvals_t[:, _ci:_ci + 1])
        P2box = [None]

        def _(sync):
            sync.dma_start(
                out=x_res[:].rearrange("p (t d) -> p t d", t=TT),
                in_=x_in.rearrange("t p d -> p t d")).then_inc(cx.sem["dA"], 16)
            sync.dma_start(out=ident_t[:], in_=ident_d).then_inc(cx.sem["dA"], 16)
            sync.dma_start(out=ident32_t[:], in_=ident32_d).then_inc(cx.sem["dA"], 16)
            sync.dma_start(out=ones_t[:], in_=ones_row_d).then_inc(cx.sem["dA"], 16)
            sync.dma_start(out=cvals_t[:], in_=cvals_d).then_inc(cx.sem["dA"], 16)
            sync.dma_start(out=ones32_t[:], in_=ones32_d).then_inc(cx.sem["dA"], 16)
            sync.dma_start(out=onec_b_t[:], in_=onec_b_d).then_inc(cx.sem["dA"], 16)
            sync.dma_start(out=onec_f_t[:], in_=onec_f_d).then_inc(cx.sem["dA"], 16)
            cx.bump("dA", 128)
        block.sync(_)

        def emit_ln(tt, dst_bf, dst_f32, pe_guard=None):
            xt = x_res[:, tt * D:(tt + 1) * D]
            nch = max(1, D // 512)
            st = ln_sc[:, tt * 8:tt * 8 + 8]
            sem0, c0 = x_ok[tt]

            def dve1(v):
                cx.w(v, sem0, c0)
                v.drain()
                for a_ in range(nch):
                    v.bn_stats(out=lnsc[:, 6 * a_:6 * a_ + 6],
                               in_=xt[:, 512 * a_:512 * a_ + min(512, D)])
                v.drain()
                cx.inc("dve", v.bn_aggr(out=st[:, 0:2], in_=lnsc[:, 0:6 * nch]))
            block.vector(dve1)
            c_dve1 = cx.cnt["dve"]

            def act1(a):
                cx.w(a, "dve", c_dve1)
                cx.inc("act", a.activation(out=st[:, 2:3], in_=st[:, 1:2],
                                           func=Act.Sqrt, bias=1e-5, scale=1.0))
            block.scalar(act1)
            c_act1 = cx.cnt["act"]

            def dve2(v):
                cx.w(v, "act", c_act1)
                v.reciprocal(out=st[:, 3:4], in_=st[:, 2:3])
                v.drain()
                v.tensor_tensor(out=st[:, 4:5], in0=st[:, 0:1], in1=st[:, 3:4],
                                op=Alu.mult)
                v.drain()
                cx.inc("dve", v.tensor_scalar(
                    out=st[:, 4:5], in0=st[:, 4:5], scalar1=-1.0, scalar2=None,
                    op0=Alu.mult))
            block.vector(dve2)
            c_dve2 = cx.cnt["dve"]

            def act2(a):
                cx.w(a, "dve", c_dve2)
                if pe_guard is not None:
                    cx.w(a, "pe", pe_guard)
                if dst_bf is not None:
                    a.activation(out=dst_bf, in_=xt, func=Act.Identity,
                                 bias=st[:, 4:5], scale=st[:, 3:4])
                    a.drain()
                if dst_f32 is not None:
                    a.activation(out=dst_f32, in_=xt, func=Act.Identity,
                                 bias=st[:, 4:5], scale=st[:, 3:4])
                cx.inc("act", a.drain())
            block.scalar(act2)
            return cx.cnt["act"]

        def emit_transposes(n_tiles, src, dst, ready, extra=None):
            slot = lambda i: (i % 4) * 512
            pe_base = cx.cnt["pe"]
            dve_base = cx.cnt["dve"]
            act_base = cx.cnt["act"]

            def pe(t):
                cx.w(t, "dA", 128)
                cx.w(t, ready[0], ready[1])
                if extra is not None:
                    cx.w(t, extra[0], extra[1])
                for i in range(n_tiles):
                    if i >= 4:
                        j = i - 4
                        if j % 2 == 0:
                            t.wait_ge(cx.sem["dve"], dve_base + j // 2 + 1)
                        else:
                            t.wait_ge(cx.sem["act"], act_base + j // 2 + 1)
                    t.transpose(P0[:, slot(i):slot(i) + 128], src(i),
                                ident32_t[:]).then_inc(cx.sem["pe"], 1)
            cx.bump("pe", n_tiles)
            block.tensor(pe)

            def dve(v):
                for i in range(0, n_tiles, 2):
                    v.wait_ge(cx.sem["pe"], pe_base + i + 1)
                    v.tensor_copy(out=dst(i), in_=P0[:, slot(i):slot(i) + 128]
                                  ).then_inc(cx.sem["dve"], 1)
            cx.bump("dve", (n_tiles + 1) // 2)
            block.vector(dve)

            def act(a):
                for i in range(1, n_tiles, 2):
                    a.wait_ge(cx.sem["pe"], pe_base + i + 1)
                    a.copy(out=dst(i), in_=P0[:, slot(i):slot(i) + 128]
                           ).then_inc(cx.sem["act"], 1)
            cx.bump("act", n_tiles // 2)
            block.scalar(act)
            return cx.cnt["dve"], cx.cnt["act"]

        def sublayer_ffn0(sidx):
            ent = {k: cx.cnt[k] for k in ("pe", "dve", "act")}
            n_oct = (IT + 7) // 8
            octs = [list(range(o * 8, min(IT, o * 8 + 8))) for o in range(n_oct)]
            if cfg.get("ffn0_octs"):
                octs = octs[:cfg["ffn0_octs"]]
                n_oct = len(octs)
            RING = cfg.get("ffn0_ring", 4)
            quads = [list(range(q * 2, min(IT, q * 2 + 2)))
                     for q in range((IT + 1) // 2)]
            NQ = len(quads)

            P2 = P2box[0]
            with (
                nc.sbuf_tensor(f"s{sidx}_xnT", [128, KT * TPC], BF16) as xnT,
                nc.sbuf_tensor(f"s{sidx}_xn32", [128, D], F32) as xn32,
                nc.sbuf_tensor(f"s{sidx}_gu", [128, 2 * 2 * KT * 128], BF16) as gub,
                nc.sbuf_tensor(f"s{sidx}_dwn", [128, RING * 2 * D], BF16) as dwn,
                nc.sbuf_tensor(f"s{sidx}_h", [128, 2 * 8 * TPC], BF16) as hsl,
                nc.sbuf_tensor(f"s{sidx}_gs", [128, IT], F32) as gs_t,
                nc.sbuf_tensor(f"s{sidx}_us", [128, IT], F32) as us_t,
                nc.sbuf_tensor(f"s{sidx}_dsb", [128, D], BF16) as dsb,
                nc.sbuf_tensor(f"s{sidx}_dsr", [1, D], F32) as dsr,
                nc.sbuf_tensor(f"s{sidx}_gt", [128, TPC], BF16) as gtmp,
                nc.sbuf_tensor(f"s{sidx}_ut", [128, TPC], BF16) as utmp,
            ):
                _gu2 = cfg.get("ffn0_phase") == "gu2"
                def aux(sc):
                    cx.w(sc, "act", ent["act"])
                    cx.w(sc, "dve", ent["dve"])
                    sc.dma_start(out=gs_t[:], in_=gate_s_d).then_inc(cx.sem["dS"], 16)
                    sc.dma_start(out=us_t[:], in_=up_s_d).then_inc(cx.sem["dS"], 16)
                    sc.dma_start(out=dsr[:], in_=down_s_d).then_inc(cx.sem["dS"], 16)
                    cx.bump("dS", 48)
                block.scalar(aux)
                aux_c = cx.cnt["dS"]

                dsrb = hsl[0:1, 0:D]

                def prep(v):
                    cx.w(v, "dS", aux_c)
                    v.tensor_scalar(out=us_t[:], in0=us_t[:], scalar1=0.5,
                                    scalar2=None, op0=Alu.mult)
                    v.drain()
                    cx.inc("dve", v.tensor_copy(out=dsrb, in_=dsr[:]))
                block.vector(prep)
                c0 = cx.cnt["dve"]

                def peb(t):
                    cx.w(t, "dve", max(c0, ent["dve"]))
                    cx.w(t, "dA", 128)
                    for co, cn in DC:
                        t.matmul(P0[:, co:co + cn], ones_t[:],
                                 dsrb[:, co:co + cn], start=True, stop=True)
                    cx.inc("pe", t.drain())
                block.tensor(peb)
                c1 = cx.cnt["pe"]

                def ev(v):
                    cx.w(v, "pe", c1)
                    cx.inc("dve", v.tensor_copy(out=dsb[:], in_=P0[:, 0:D]))
                block.vector(ev)
                dsb_c = cx.cnt["dve"]
                if _gu2:
                    dsb_c = cx.cnt["dve"]

                pe_g = None
                for tt in range(TT):
                    a_rdy = emit_ln(tt, None, xn32[:], pe_guard=pe_g)
                    emit_transposes(
                        KT, lambda kt: xn32[:, kt * 128:(kt + 1) * 128],
                        lambda kt, tt=tt: xnT[:, kt * TPC + tt * 128:
                                              kt * TPC + (tt + 1) * 128],
                        ("act", a_rdy), extra=("dve", dsb_c))
                    pe_g = cx.cnt["pe"]
                xnT_dve, xnT_act = cx.cnt["dve"], cx.cnt["act"]
                if cfg.get("ffn0_phase") == "ln":
                    return

                stream_order = []
                for oi, oct in enumerate(octs):
                    for i in oct:
                        stream_order.append(("gu", i))
                    if not _gu2:
                        for q in range(oi * 4, min(NQ, oi * 4 + 4)):
                            stream_order.append(("dq", q))
                gu_loaded = {i: (f"gu{i % 2}", cx.cnt[f"gu{i % 2}"] + 16 * (i // 2 + 1))
                             for i in range(IT)}
                _dqsems = ["dq0", "dq1", "dq2", "dqB"]
                dq_loaded = {q: (_dqsems[q % 4],
                                 cx.cnt[_dqsems[q % 4]] + 16 * (q // 4 + 1))
                             for q in range(NQ)}
                for i in range(IT):
                    cx.cnt[f"gu{i % 2}"] = gu_loaded[i][1]
                for q in range(NQ):
                    cx.cnt[_dqsems[q % 4]] = dq_loaded[q][1]

                incl = [i for oct_ in octs for i in oct_]
                act_g_done = {}
                act_c = cx.cnt["act"]
                for i in incl:
                    act_c += 1; act_g_done[i] = act_c
                _gu_only = cfg.get("ffn0_phase") in ("gu", "gu2", "gu3")
                _gu3 = cfg.get("ffn0_phase") == "gu3"
                ev_sched = []
                for oi, oct in enumerate(octs):
                    for i in oct:
                        ev_sched.append(("u", i))
                        ev_sched.append(("h", i))
                    if not (_gu2 or _gu3):
                        for q2 in range(oi * 4, min(NQ, oi * 4 + 4,
                                                    4 * len(octs))):
                            ev_sched.append(("scale", q2))
                    if not _gu_only:
                        for ts in range(TT):
                            ev_sched.append(("evac", oi, ts, 0))
                            if D >= 1024:
                                ev_sched.append(("evac", oi, ts, 1))
                dve_scale_done, dve_h_done, dve_evac_done = {}, {}, {}
                dve_u_done = {}
                dve_c = cx.cnt["dve"]
                for item in ev_sched:
                    dve_c += 1
                    if item[0] == "scale":
                        dve_scale_done[item[1]] = dve_c
                    elif item[0] == "u":
                        dve_u_done[item[1]] = dve_c
                    elif item[0] == "h":
                        dve_h_done[item[1]] = dve_c
                    else:
                        dve_evac_done[(item[1], item[2], item[3])] = dve_c

                pe_gu_done = {}
                pe_down_done = {}

                def pe_mm(t):
                    cx.w(t, "dve", max(xnT_dve, dsb_c))
                    cx.w(t, "act", xnT_act)
                    for oi, oct in enumerate(octs):
                        for i in oct:
                            t.wait_ge(cx.sem[gu_loaded[i][0]], gu_loaded[i][1])
                            for w in range(2):
                                ps = P1 if w == 0 else P2
                                if i > 0:
                                    if w == 0:
                                        t.wait_ge(cx.sem["act"],
                                                  act_g_done[i - 1])
                                    else:
                                        t.wait_ge(cx.sem["dve"],
                                                  dve_u_done[i - 1])
                                base = (i % 2) * (2 * KT * 128) + w * KT * 128
                                for kt in range(KT):
                                    wap = gub[:, base + kt * 128:base + kt * 128 + 128]
                                    t.ldweights(wap)
                                    for tco, tcn in TC:
                                        ii = t.matmul(ps[:, tco:tco + tcn], wap,
                                                      xnT[:, kt * TPC + tco:
                                                          kt * TPC + tco + tcn],
                                                      start=(kt == 0),
                                                      stop=(kt == KT - 1))
                                ii.then_inc(cx.sem["pe"], 1)
                                cx.bump("pe")
                            pe_gu_done[i] = cx.cnt["pe"]
                        HDC = DC[:len(DC) // 2] if len(DC) > 1 else list(DC)
                        HD2 = DC[len(DC) // 2:] if len(DC) > 1 else []
                        for ts in ([] if _gu_only else range(TT)):
                            for hf, dcs in enumerate((HDC, HD2)):
                                if not dcs:
                                    pe_down_done[(oi, ts, hf)] = cx.cnt["pe"]
                                    continue
                                prev = ((oi, ts - 1, hf) if ts > 0 else
                                        (oi - 1, TT - 1, hf) if oi > 0
                                        else None)
                                if prev is not None:
                                    t.wait_ge(cx.sem["dve"],
                                              dve_evac_done[prev])
                                for ii, i in enumerate(oct):
                                    q = i // 2
                                    if hf == 0:
                                        t.wait_ge(cx.sem["dve"],
                                                  dve_scale_done[q])
                                        t.wait_ge(cx.sem["dve"],
                                                  dve_h_done[i])
                                    hb = (((i // 8) % 2) * 8 * TPC +
                                          (i % 8) * TPC)
                                    hap = hsl[:, hb + ts * 128:
                                              hb + ts * 128 + 128]
                                    t.ldweights(hap)
                                    for co, cn in dcs:
                                        slab = q % RING
                                        wap = dwn[:, slab * 2 * D +
                                                  (i % 2) * D + co:
                                                  slab * 2 * D +
                                                  (i % 2) * D + co + cn]
                                        li = t.matmul(P0[:, co:co + cn],
                                                      hap, wap,
                                                      start=(ii == 0),
                                                      stop=(ii ==
                                                            len(oct) - 1))
                                li.then_inc(cx.sem["pe"], 1)
                                cx.bump("pe")
                                pe_down_done[(oi, ts, hf)] = cx.cnt["pe"]
                block.tensor(pe_mm)
                pe_dw_need = {}
                for q in range(NQ):
                    oi = quads[q][0] // 8
                    pe_dw_need[q] = pe_down_done.get(
                        (oi, TT - 1, 1),
                        pe_down_done.get((oi, TT - 1, 0),
                                         max(pe_gu_done.values())))

                def act_ep(a):
                    cx.w(a, "dS", aux_c)
                    for i in incl:
                        cx.w(a, "pe", pe_gu_done[i])
                        if i >= 1:
                            cx.w(a, "dve", dve_h_done[i - 1])
                        a.activation(out=gtmp[:], in_=P1[:, 0:TPC],
                                     func=Act.Silu, scale=gs_t[:, i:i + 1]
                                     ).then_inc(cx.sem["act"], 1)
                cx.cnt["act"] = act_c
                block.scalar(act_ep)

                def dve_ep(v):
                    for item in ev_sched:
                        if item[0] == "scale":
                            q = item[1]
                            nq = len(quads[q])
                            v.wait_ge(cx.sem[dq_loaded[q][0]], dq_loaded[q][1])
                            cx.w(v, "dve", dsb_c)
                            if q >= RING:
                                cx.w(v, "pe", pe_dw_need[q - RING])
                            slab = q % RING
                            ap3 = dwn[:, slab * 2 * D:slab * 2 * D + nq * D] \
                                .rearrange("p (i d) -> p i d", i=nq)
                            v.tensor_tensor(
                                out=ap3, in0=ap3,
                                in1=dsb[:].unsqueeze(1).broadcast_to([128, nq, D]),
                                op=Alu.mult).then_inc(cx.sem["dve"], 1)
                        elif item[0] == "u":
                            i = item[1]
                            cx.w(v, "pe", pe_gu_done[i])
                            cx.w(v, "dS", aux_c)
                            if i >= 1:
                                cx.w(v, "dve", dve_h_done[i - 1])
                            if i // 8 >= 2 and not _gu_only:
                                cx.w(v, "pe", pe_down_done[
                                    (i // 8 - 2, TT - 1, len(DC) > 1 and 1 or 0)])
                            v.tensor_scalar(out=utmp[:], in0=P2[:, 0:TPC],
                                            scalar1=us_t[:, i:i + 1],
                                            scalar2=None, op0=Alu.mult
                                            ).then_inc(cx.sem["dve"], 1)
                        elif item[0] == "h":
                            i = item[1]
                            cx.w(v, "act", act_g_done[i])
                            cx.w(v, "dve", dve_u_done[i])
                            if i // 8 >= 2 and not _gu_only:
                                cx.w(v, "pe", pe_down_done[
                                    (i // 8 - 2, TT - 1, len(DC) > 1 and 1 or 0)])
                            hb = ((i // 8) % 2) * 8 * TPC + (i % 8) * TPC
                            v.tensor_tensor(out=hsl[:, hb:hb + TPC],
                                            in0=gtmp[:],
                                            in1=utmp[:],
                                            op=Alu.mult).then_inc(cx.sem["dve"], 1)
                        else:
                            oi, ts, hf = item[1], item[2], item[3]
                            if len(DC) > 1:
                                ho = hf * (D // 2)
                                hw = D // 2
                            else:
                                ho, hw = 0, D
                            cx.w(v, "pe", pe_down_done[(oi, ts, hf)])
                            xt = x_res[:, ts * D + ho:ts * D + ho + hw]
                            v.tensor_tensor(out=xt, in0=P0[:, ho:ho + hw],
                                            in1=xt,
                                            op=Alu.add).then_inc(cx.sem["dve"], 1)
                cx.cnt["dve"] = dve_c
                block.vector(dve_ep)

                def stream(g):
                    cx.w(g, "pe", ent["pe"])
                    cx.w(g, "dve", ent["dve"])
                    for kind, idx in stream_order:
                        if kind == "gu":
                            i = idx
                            if i >= 2:
                                g.wait_ge(cx.sem["pe"], pe_gu_done[i - 2])
                            g.dma_start(
                                out=gub[:, (i % 2) * 2 * KT * 128:
                                        ((i % 2) + 1) * 2 * KT * 128]
                                .rearrange("p (w k n) -> p w k n", w=2, k=KT),
                                in_=w_gu_d[i]).then_inc(cx.sem[f"gu{i % 2}"], 16)
                        else:
                            q = idx
                            nq = len(quads[q])
                            if q >= RING:
                                if not _gu3:
                                    g.wait_ge(cx.sem["dve"], dve_scale_done[q - RING])
                                g.wait_ge(cx.sem["pe"], pe_dw_need[q - RING])
                            slab = q % RING
                            g.dma_start(
                                out=dwn[:, slab * 2 * D:slab * 2 * D + nq * D]
                                .rearrange("p (i d) -> p i d", i=nq),
                                in_=w_down_d[quads[q][0]:quads[q][0] + nq]
                                .rearrange("i p d -> p i d")).then_inc(
                                cx.sem[dq_loaded[q][0]], 16)
                block.gpsimd(stream)

            if not _gu_only:
                for ts in range(TT):
                    x_ok[ts] = ("dve", max(
                        dve_evac_done[(n_oct - 1, ts, 0)],
                        dve_evac_done.get((n_oct - 1, ts, 1), 0)))


        def sublayer_attn0(sidx):
            ent = {k: cx.cnt[k] for k in ("pe", "dve", "act")}
            G = H // HKV
            lam = [2.0 ** (-8.0 * (h + 1) / H) for h in range(H)]
            blkd = [math.exp(-lam[h] * CHUNK) for h in range(H)]
            NPH = 2 * HKV + H

            with (
                nc.sbuf_tensor(f"s{sidx}_xoT", [128, max(KT, H) * TPC], BF16) as xoT,
                nc.sbuf_tensor(f"s{sidx}_ropec", [128, TPC], BF16) as ropec_t,
                nc.sbuf_tensor(f"s{sidx}_ropes", [128, TPC], BF16) as ropes_t,
                nc.sbuf_tensor(f"s{sidx}_diagT", [128, H * CHUNK], F32) as diagT_t,
                nc.sbuf_tensor(f"s{sidx}_qdec", [128, H * CHUNK], BF16) as qdec_t,
                nc.sbuf_tensor(f"s{sidx}_kdect", [128, H], F32) as kdec_t,
                nc.sbuf_tensor(f"s{sidx}_coef", [128, 4 * H], F32) as coef_t,
            ):
                xnT = xoT[:, 0:KT * TPC]
                oT = xoT[:, 0:H * TPC]

                def aux(sc):
                    cx.w(sc, "act", ent["act"])
                    cx.w(sc, "dve", ent["dve"])
                    for dst_, src_ in ((ropec_t, ropec_d), (ropes_t, ropes_d),
                                       (diagT_t, diagT_d), (qdec_t, qdec_d),
                                       (kdec_t, kdec_d), (coef_t, coef_d)):
                        sc.dma_start(out=dst_[:], in_=src_).then_inc(cx.sem["dS"], 16)
                        cx.bump("dS", 16)
                block.scalar(aux)
                aux_c = cx.cnt["dS"]

                with (
                    nc.sbuf_tensor(f"s{sidx}_qf", [128, H * TPC], BF16) as qf,
                    nc.sbuf_tensor(f"s{sidx}_kf", [128, HKV * TPC], BF16) as kf,
                    nc.sbuf_tensor(f"s{sidx}_vf", [128, HKV * TPC], BF16) as vf,
                    nc.sbuf_tensor(f"s{sidx}_vtok", [128, TT * HKV * DH], BF16) as vtok,
                ):
                    act_proj_done = {}
                    with (
                        nc.sbuf_tensor(f"s{sidx}_xn32", [128, D], F32) as xn32,
                        nc.sbuf_tensor(f"s{sidx}_qkb", [128, 2 * KT * 128], BF16) as qkb,
                    ):
                        pe_g = None
                        for tt in range(TT):
                            a_rdy = emit_ln(tt, None, xn32[:], pe_guard=pe_g)
                            emit_transposes(
                                KT, lambda kt: xn32[:, kt * 128:(kt + 1) * 128],
                                lambda kt, tt=tt: xnT[:, kt * TPC + tt * 128:
                                                      kt * TPC + (tt + 1) * 128],
                                ("act", a_rdy), extra=("dve", ent["dve"]))
                            pe_g = cx.cnt["pe"]
                        xnT_dve, xnT_act = cx.cnt["dve"], cx.cnt["act"]

                        qk_loaded = {}
                        for si in range(NPH):
                            s_ = f"gu{si % 2}"
                            qk_loaded[si] = (s_, cx.cnt[s_] + 16)
                            cx.cnt[s_] += 16

                        act_c = cx.cnt["act"]
                        for si in range(NPH):
                            act_c += 1
                            act_proj_done[si] = act_c

                        def proj_dst(si):
                            if si < HKV:
                                return kf[:, si * TPC:(si + 1) * TPC]
                            if si < 2 * HKV:
                                return vf[:, (si - HKV) * TPC:(si - HKV + 1) * TPC]
                            return qf[:, (si - 2 * HKV) * TPC:
                                      (si - 2 * HKV + 1) * TPC]

                        pe_proj_done = {}

                        def pe_proj(t):
                            cx.w(t, "dve", xnT_dve)
                            cx.w(t, "act", xnT_act)
                            for si in range(NPH):
                                t.wait_ge(cx.sem[qk_loaded[si][0]], qk_loaded[si][1])
                                if si >= 2:
                                    t.wait_ge(cx.sem["act"], act_proj_done[si - 2])
                                ps = P1 if si % 2 == 0 else P0
                                base = (si % 2) * KT * 128
                                for kt in range(KT):
                                    wap = qkb[:, base + kt * 128:base + kt * 128 + 128]
                                    t.ldweights(wap)
                                    for tco, tcn in TC:
                                        ii = t.matmul(ps[:, tco:tco + tcn], wap,
                                                      xnT[:, kt * TPC + tco:
                                                          kt * TPC + tco + tcn],
                                                      start=(kt == 0),
                                                      stop=(kt == KT - 1))
                                ii.then_inc(cx.sem["pe"], 1)
                                cx.bump("pe")
                                pe_proj_done[si] = cx.cnt["pe"]
                        block.tensor(pe_proj)

                        def act_proj(a):
                            for si in range(NPH):
                                cx.w(a, "pe", pe_proj_done[si])
                                ps = P1 if si % 2 == 0 else P0
                                a.activation(out=proj_dst(si), in_=ps[:, 0:TPC],
                                             func=Act.Copy
                                             ).then_inc(cx.sem["act"], 1)
                        cx.cnt["act"] = act_c
                        block.scalar(act_proj)

                        def stream1(g):
                            cx.w(g, "pe", ent["pe"])
                            cx.w(g, "dve", ent["dve"])
                            for si in range(NPH):
                                if si >= 2:
                                    g.wait_ge(cx.sem["pe"], pe_proj_done[si - 2])
                                if si < HKV:
                                    w_src = w_k0_d[si]
                                elif si < 2 * HKV:
                                    w_src = w_v0_d[si - HKV]
                                else:
                                    w_src = w_q0_d[si - 2 * HKV]
                                g.dma_start(out=qkb[:, (si % 2) * KT * 128:
                                                   (si % 2 + 1) * KT * 128]
                                            .rearrange("p (k n) -> p k n", k=KT),
                                            in_=w_src).then_inc(
                                    cx.sem[qk_loaded[si][0]], 16)
                        block.gpsimd(stream1)

                    with (
                        nc.sbuf_tensor(f"s{sidx}_swp", [128, 2 * TPC], BF16) as swp,
                        nc.sbuf_tensor(f"s{sidx}_ktok", [128, HKV * TT * 128], BF16) as ktok,
                        nc.sbuf_tensor(f"s{sidx}_kv", [128, H * 128], F32) as kvst,
                        nc.sbuf_tensor(f"s{sidx}_kvbf", [128, 2 * 128], BF16) as kvbf,
                        nc.sbuf_tensor(f"s{sidx}_sj", [128, 4 * 128], F32) as sjb,
                        nc.sbuf_tensor(f"s{sidx}_ploc", [128, 2 * CHUNK], BF16) as ploc,
                        nc.sbuf_tensor(f"s{sidx}_qdq", [128, 2 * CHUNK], BF16) as qdqs,
                        nc.sbuf_tensor(f"s{sidx}_kdk", [128, 2 * 128], BF16) as kdks,
                    ):
                        rope_heads = list(range(HKV)) + \
                                     list(range(2 * HKV, NPH))
                        dve_rope_done = {}
                        dve_rope_m2 = {}
                        _r0 = cx.cnt["dve"]
                        R_ = len(rope_heads)
                        for ri in range(R_):
                            dve_rope_m2[ri] = _r0 + (2 if ri == 0
                                                     else 2 + 3 * ri)
                            dve_rope_done[ri] = _r0 + (5 + 3 * ri
                                                       if ri < R_ - 1
                                                       else 3 * R_)
                        dve_c2 = _r0 + 3 * R_
                        dA_swp = {}

                        def head_ap(si):
                            if si < HKV:
                                return kf, si * TPC
                            return qf, (si - 2 * HKV) * TPC

                        def swap_all(sync):
                            cx.w(sync, "pe", pe_proj_done[NPH - 1])
                            for ri, si in enumerate(rope_heads):
                                base, off = head_ap(si)
                                sl = (ri % 2) * TPC
                                sw = f"sw{ri % 2}"
                                cx.w(sync, "act", act_proj_done[si])
                                cx.w(sync, "dS", aux_c)
                                if ri >= 2:
                                    cx.w(sync, "dve", dve_rope_done[ri - 2])
                                sync.dma_start(out=swp[0:64, sl:sl + TPC],
                                               in_=base[64:128, off:off + TPC]
                                               ).then_inc(cx.sem[sw], 16)
                                sync.dma_start(out=swp[64:128, sl:sl + TPC],
                                               in_=base[0:64, off:off + TPC]
                                               ).then_inc(cx.sem[sw], 16)
                                cx.bump(sw, 32)
                                dA_swp[ri] = (sw, cx.cnt[sw])
                        block.sync(swap_all)

                        def rope_v(v):
                            def m12(ri, si):
                                base, off = head_ap(si)
                                sl = (ri % 2) * TPC
                                v.wait_ge(cx.sem[dA_swp[ri][0]], dA_swp[ri][1])
                                cx.w(v, "dS", aux_c)
                                v.tensor_tensor(out=swp[:, sl:sl + TPC],
                                                in0=swp[:, sl:sl + TPC],
                                                in1=ropes_t[:], op=Alu.mult
                                                ).then_inc(cx.sem["dve"], 1)
                                v.tensor_tensor(out=base[:, off:off + TPC],
                                                in0=base[:, off:off + TPC],
                                                in1=ropec_t[:], op=Alu.mult
                                                ).then_inc(cx.sem["dve"], 1)

                            def a3(ri, si):
                                base, off = head_ap(si)
                                sl = (ri % 2) * TPC
                                v.wait_ge(cx.sem["dve"], dve_rope_m2[ri])
                                v.tensor_tensor(out=base[:, off:off + TPC],
                                                in0=base[:, off:off + TPC],
                                                in1=swp[:, sl:sl + TPC],
                                                op=Alu.add
                                                ).then_inc(cx.sem["dve"], 1)

                            for ri, si in enumerate(rope_heads):
                                m12(ri, si)
                                if ri >= 1:
                                    a3(ri - 1, rope_heads[ri - 1])
                            a3(R_ - 1, rope_heads[R_ - 1])
                            cx.cnt["dve"] = dve_c2
                        block.vector(rope_v)
                        dve_rope_k = dve_rope_done[HKV - 1]
                        dve_rope_all = dve_c2

                        PB = P0.bitcast(BF16)
                        tr_list = []
                        for kv_ in range(HKV):
                            for tt_ in range(TT):
                                tr_list.append(
                                    (kf[:, kv_ * TPC + tt_ * 128:
                                        kv_ * TPC + tt_ * 128 + 128],
                                     ktok[:, (kv_ * TT + tt_) * 128:
                                          (kv_ * TT + tt_) * 128 + 128]))
                        for kv_ in range(HKV):
                            for tt_ in range(TT):
                                tr_list.append(
                                    (vf[:, kv_ * TPC + tt_ * 128:
                                        kv_ * TPC + tt_ * 128 + 128],
                                     vtok[:, (tt_ * HKV + kv_) * 128:
                                          (tt_ * HKV + kv_) * 128 + 128]))
                        ntr = len(tr_list)
                        ktr_pe_base = cx.cnt["pe"]
                        ktr_dve_base = cx.cnt["dve"]

                        def ktr_pe(t):
                            cx.w(t, "dve", dve_rope_k)
                            cx.w(t, "act", act_proj_done[2 * HKV - 1])
                            for i in range(ntr):
                                if i >= 2:
                                    t.wait_ge(cx.sem["dve"], ktr_dve_base + i - 1)
                                t.transpose(PB[:, 2048 + (i % 2) * 1024:
                                               2048 + (i % 2) * 1024 + 128],
                                            tr_list[i][0], ident_t[:]
                                            ).then_inc(cx.sem["pe"], 1)
                        cx.bump("pe", ntr)
                        block.tensor(ktr_pe)

                        def ktr_ev(v):
                            for i in range(ntr):
                                v.wait_ge(cx.sem["pe"], ktr_pe_base + i + 1)
                                v.tensor_copy(out=tr_list[i][1],
                                              in_=PB[:, 2048 + (i % 2) * 1024:
                                                     2048 + (i % 2) * 1024 + 128]
                                              ).then_inc(cx.sem["dve"], 1)
                        cx.bump("dve", ntr)
                        dve_ktok_done = cx.cnt["dve"]
                        block.vector(ktr_ev)

                        def vtok_chunk(h, c):
                            kv_ = h // G
                            pb = (c % 2) * 64
                            return vtok[pb:pb + 64,
                                        ((c // 2) * HKV + kv_) * 128:
                                        ((c // 2) * HKV + kv_) * 128 + 128]

                        def ktok_chunk(h, c):
                            kv_ = h // G
                            pb = (c % 2) * 64
                            i = kv_ * TT + c // 2
                            return ktok[pb:pb + 64, i * 128:i * 128 + 128]

                        def kf_chunk(h, c):
                            kv_ = h // G
                            return kf[:, kv_ * TPC + c * CHUNK:
                                      kv_ * TPC + (c + 1) * CHUNK]

                        def qf_chunk(h, c):
                            return qf[:, h * TPC + c * CHUNK:
                                      h * TPC + (c + 1) * CHUNK]

                        def mset(v):
                            v.memset(kvst[:], 0.0)
                            v.drain().then_inc(cx.sem["dve"], 1)
                        cx.bump("dve")
                        mset_c = cx.cnt["dve"]
                        block.vector(mset)

                        NSEQ = NCH * H
                        dveA_kdk = {}
                        dveA_upd = {}
                        dve_cA = cx.cnt["dve"]
                        for n_ in range(NSEQ):
                            dve_cA += 1
                            dveA_kdk[n_] = dve_cA
                            dve_cA += 1
                            dveA_upd[n_] = dve_cA

                        peA = {}

                        def phA_pe(t):
                            cx.w(t, "dve", max(dve_ktok_done, mset_c))
                            for c in range(NCH):
                                for h in range(H):
                                    n_ = c * H + h
                                    t.wait_ge(cx.sem["dve"], dveA_kdk[n_])
                                    if n_ >= 2:
                                        t.wait_ge(cx.sem["dve"], dveA_upd[n_ - 2])
                                    pb = (c % 2) * 64
                                    kap = kdks[pb:pb + 64,
                                               (n_ % 2) * 128:(n_ % 2) * 128 + 128]
                                    t.ldweights(kap, tile_position=(pb, 0))
                                    t.matmul(P2[:, (n_ % 2) * 512:
                                                (n_ % 2) * 512 + 128],
                                             kap, vtok_chunk(h, c),
                                             start=True, stop=True,
                                             tile_position=(pb, 0)
                                             ).then_inc(cx.sem["pe"], 1)
                                    cx.bump("pe")
                                    peA[n_] = cx.cnt["pe"]
                        block.tensor(phA_pe)

                        def phA_dve(v):
                            cx.w(v, "dve", max(dve_ktok_done, mset_c))
                            for c in range(NCH):
                                for h in range(H):
                                    n_ = c * H + h
                                    pb = (c % 2) * 64
                                    if n_ >= 2:
                                        v.wait_ge(cx.sem["pe"], peA[n_ - 2])
                                    v.tensor_scalar(
                                        out=kdks[pb:pb + 64,
                                                 (n_ % 2) * 128:(n_ % 2) * 128 + 128],
                                        in0=ktok_chunk(h, c),
                                        scalar1=kdec_t[pb:pb + 64, h:h + 1],
                                        scalar2=None, op0=Alu.mult
                                        ).then_inc(cx.sem["dve"], 1)
                                    v.wait_ge(cx.sem["pe"], peA[n_])
                                    if n_ >= H:
                                        v.wait_ge(cx.sem["dve"],
                                                  dveA_upd[n_ - H])
                                    v.scalar_tensor_tensor(
                                        out=kvst[:, h * 128:(h + 1) * 128],
                                        in0=kvst[:, h * 128:(h + 1) * 128],
                                        scalar=blkd[h],
                                        in1=P2[:, (n_ % 2) * 512:
                                               (n_ % 2) * 512 + 128],
                                        op0=Alu.mult, op1=Alu.add
                                        ).then_inc(cx.sem["dve"], 1)
                            cx.cnt["dve"] = dve_cA
                        block.vector(phA_dve)

                        Nag = 128 * H * 128
                        cc_in0 = dt(f"s{sidx}_ccin", [1, Nag], F32,
                                    kind="Internal").ap()
                        cc_out0 = dt(f"s{sidx}_ccout", [4, Nag], F32,
                                     kind="Internal").ap()

                        def s_exp(sync):
                            cx.w(sync, "dve", dveA_upd[NSEQ - 1])
                            sync.dma_start(
                                out=cc_in0.rearrange("o (p f) -> (o p) f", p=128),
                                in_=kvst[:]).then_inc(cx.sem["dA"], 16)
                            cx.bump("dA", 16)
                        block.sync(s_exp)
                        dA_sexp = cx.cnt["dA"]

                        def agather(g):
                            g.wait_ge(cx.sem["dA"], dA_sexp)
                            g.collective_compute(
                                "AllGather", Alu.bypass,
                                replica_groups=[[0, 1, 2, 3], [4, 5, 6, 7]],
                                ins=[cc_in0], outs=[cc_out0],
                            ).then_inc(cx.sem["cc"], 1)
                            cx.bump("cc")
                        block.gpsimd(agather)
                        cc_done = cx.cnt["cc"]

                        NQH = max(1, H // 4)
                        comb_dve = {}
                        dve_cB = cx.cnt["dve"]
                        for m_ in range(4 * NQH):
                            j, hq = divmod(m_, NQH)
                            nheads = min(4, H - hq * 4)
                            dve_cB += nheads
                            comb_dve[m_] = dve_cB
                        dA_comb = {}

                        def comb_dma(sync):
                            cx.w(sync, "cc", cc_done)
                            for m_ in range(4 * NQH):
                                j, hq = divmod(m_, NQH)
                                nheads = min(4, H - hq * 4)
                                if m_ >= 1:
                                    cx.w(sync, "dve", comb_dve[m_ - 1])
                                sync.dma_start(
                                    out=sjb[:, 0:nheads * 128],
                                    in_=cc_out0[j:j + 1]
                                    .rearrange("o (p f) -> (o p) f", p=128)
                                    [:, hq * 512:hq * 512 + nheads * 128]
                                    ).then_inc(cx.sem["dA"], 16)
                                cx.bump("dA", 16)
                                dA_comb[m_] = cx.cnt["dA"]
                        block.sync(comb_dma)

                        def comb_v(v):
                            for m_ in range(4 * NQH):
                                j, hq = divmod(m_, NQH)
                                nheads = min(4, H - hq * 4)
                                cx.w(v, "dA", dA_comb[m_])
                                for hi in range(nheads):
                                    h = hq * 4 + hi
                                    if j == 0:
                                        v.tensor_scalar(
                                            out=kvst[:, h * 128:(h + 1) * 128],
                                            in0=sjb[:, hi * 128:(hi + 1) * 128],
                                            scalar1=coef_t[:, j * H + h:
                                                           j * H + h + 1],
                                            scalar2=None, op0=Alu.mult
                                            ).then_inc(cx.sem["dve"], 1)
                                    else:
                                        v.scalar_tensor_tensor(
                                            out=kvst[:, h * 128:(h + 1) * 128],
                                            in0=sjb[:, hi * 128:(hi + 1) * 128],
                                            scalar=coef_t[:, j * H + h:
                                                          j * H + h + 1],
                                            in1=kvst[:, h * 128:(h + 1) * 128],
                                            op0=Alu.mult, op1=Alu.add
                                            ).then_inc(cx.sem["dve"], 1)
                                    v.drain()
                            cx.cnt["dve"] = dve_cB
                        block.vector(comb_v)

                        NSEQB = NCH * H
                        pe_seq = []
                        dve_seq = []
                        for n_ in range(NSEQB):
                            pe_seq.append(("sc", n_))
                            if n_ >= 1:
                                pe_seq.append(("o", n_ - 1))
                                pe_seq.append(("kvU", n_ - 1))
                            dve_seq.append(("prep", n_))
                            if n_ >= 1:
                                dve_seq.append(("fin", n_ - 1))
                        pe_seq.append(("o", NSEQB - 1))
                        pe_seq.append(("kvU", NSEQB - 1))
                        dve_seq.append(("fin", NSEQB - 1))
                        peB_sc = {}
                        peB_o = {}
                        peB_kvU = {}
                        _pc = cx.cnt["pe"]
                        for kind, n_ in pe_seq:
                            _pc += 1
                            (peB_sc if kind == "sc" else
                             peB_o if kind == "o" else peB_kvU)[n_] = _pc
                        dveB_ploc = {}
                        dveB_qdq = {}
                        dveB_kdk = {}
                        dveB_kvbf = {}
                        dveB_evac = {}
                        dveB_upd = {}
                        _dc = cx.cnt["dve"]
                        for kind, n_ in dve_seq:
                            if kind == "prep":
                                _dc += 1; dveB_ploc[n_] = _dc
                                _dc += 1; dveB_qdq[n_] = _dc
                                _dc += 1; dveB_kdk[n_] = _dc
                                _dc += 1; dveB_kvbf[n_] = _dc
                            else:
                                _dc += 1; dveB_evac[n_] = _dc
                                _dc += 1; dveB_upd[n_] = _dc
                        dve_cC = _dc
                        pe_cC = _pc

                        def phB_pe(t):
                            cx.w(t, "dve", max(dve_rope_all, dve_cB))
                            cx.w(t, "dS", aux_c)
                            for kind, n_ in pe_seq:
                                c, h = divmod(n_, H)
                                pb = (c % 2) * 64
                                if kind == "sc":
                                    if n_ >= 2:
                                        t.wait_ge(cx.sem["dve"], dveB_ploc[n_ - 2])
                                    kap = kf_chunk(h, c)
                                    t.ldweights(kap)
                                    t.matmul(P0[pb:pb + 64,
                                                (n_ % 2) * 512:(n_ % 2) * 512 + 64],
                                             kap, qf_chunk(h, c),
                                             start=True, stop=True
                                             ).then_inc(cx.sem["pe"], 1)
                                    cx.bump("pe")
                                elif kind == "o":
                                    t.wait_ge(cx.sem["dve"], dveB_kvbf[n_])
                                    if n_ >= 2:
                                        t.wait_ge(cx.sem["dve"], dveB_evac[n_ - 2])
                                    osl = P0[:, 1024 + (n_ % 2) * 512:
                                             1024 + (n_ % 2) * 512 + 64]
                                    vap = vtok_chunk(h, c)
                                    t.ldweights(vap, tile_position=(pb, 0))
                                    t.matmul(osl, vap,
                                             ploc[pb:pb + 64, (n_ % 2) * CHUNK:
                                                  (n_ % 2) * CHUNK + CHUNK],
                                             start=True, stop=False,
                                             tile_position=(pb, 0))
                                    kvap = kvbf[:, (n_ % 2) * 128:
                                                (n_ % 2) * 128 + 128]
                                    t.ldweights(kvap)
                                    t.matmul(osl, kvap,
                                             qdqs[:, (n_ % 2) * CHUNK:
                                                  (n_ % 2) * CHUNK + CHUNK],
                                             start=False, stop=True
                                             ).then_inc(cx.sem["pe"], 1)
                                    cx.bump("pe")
                                else:
                                    t.wait_ge(cx.sem["dve"], dveB_kdk[n_])
                                    if n_ >= 2:
                                        t.wait_ge(cx.sem["dve"], dveB_upd[n_ - 2])
                                    kdap = kdks[pb:pb + 64,
                                                (n_ % 2) * 128:(n_ % 2) * 128 + 128]
                                    t.ldweights(kdap, tile_position=(pb, 0))
                                    t.matmul(P2[:, (n_ % 2) * 512:
                                                (n_ % 2) * 512 + 128],
                                             kdap, vtok_chunk(h, c),
                                             start=True, stop=True,
                                             tile_position=(pb, 0)
                                             ).then_inc(cx.sem["pe"], 1)
                                    cx.bump("pe")
                        assert cx.cnt["pe"] + len(pe_seq) - len(pe_seq) >= 0
                        block.tensor(phB_pe)
                        assert cx.cnt["pe"] == pe_cC

                        def phB_dve(v):
                            cx.w(v, "dve", dve_cB)
                            for kind, n_ in dve_seq:
                                c, h = divmod(n_, H)
                                pb = (c % 2) * 64
                                if kind == "prep":
                                    v.wait_ge(cx.sem["pe"], peB_sc[n_])
                                    if n_ >= 2:
                                        v.wait_ge(cx.sem["pe"], peB_o[n_ - 2])
                                    v.tensor_tensor(
                                        out=ploc[pb:pb + 64, (n_ % 2) * CHUNK:
                                                 (n_ % 2) * CHUNK + CHUNK],
                                        in0=P0[pb:pb + 64,
                                               (n_ % 2) * 512:(n_ % 2) * 512 + 64],
                                        in1=diagT_t[pb:pb + 64,
                                                    h * CHUNK:(h + 1) * CHUNK],
                                        op=Alu.mult).then_inc(cx.sem["dve"], 1)
                                    cx.bump("dve")
                                    v.tensor_tensor(
                                        out=qdqs[:, (n_ % 2) * CHUNK:
                                                 (n_ % 2) * CHUNK + CHUNK],
                                        in0=qf_chunk(h, c),
                                        in1=qdec_t[:, h * CHUNK:(h + 1) * CHUNK],
                                        op=Alu.mult).then_inc(cx.sem["dve"], 1)
                                    cx.bump("dve")
                                    if n_ >= 2:
                                        v.wait_ge(cx.sem["pe"], peB_kvU[n_ - 2])
                                    v.tensor_scalar(
                                        out=kdks[pb:pb + 64,
                                                 (n_ % 2) * 128:(n_ % 2) * 128 + 128],
                                        in0=ktok_chunk(h, c),
                                        scalar1=kdec_t[pb:pb + 64, h:h + 1],
                                        scalar2=None, op0=Alu.mult
                                        ).then_inc(cx.sem["dve"], 1)
                                    cx.bump("dve")
                                    if n_ >= H:
                                        v.wait_ge(cx.sem["dve"],
                                                  dveB_upd[n_ - H])
                                    v.tensor_copy(
                                        out=kvbf[:, (n_ % 2) * 128:
                                                 (n_ % 2) * 128 + 128],
                                        in_=kvst[:, h * 128:(h + 1) * 128]
                                        ).then_inc(cx.sem["dve"], 1)
                                    cx.bump("dve")
                                else:
                                    v.wait_ge(cx.sem["pe"], peB_o[n_])
                                    v.tensor_copy(
                                        out=oT[:, h * TPC + c * CHUNK:
                                               h * TPC + (c + 1) * CHUNK],
                                        in_=P0[:, 1024 + (n_ % 2) * 512:
                                               1024 + (n_ % 2) * 512 + 64]
                                        ).then_inc(cx.sem["dve"], 1)
                                    cx.bump("dve")
                                    v.wait_ge(cx.sem["pe"], peB_kvU[n_])
                                    if n_ >= H:
                                        v.wait_ge(cx.sem["dve"],
                                                  dveB_upd[n_ - H])
                                    v.scalar_tensor_tensor(
                                        out=kvst[:, h * 128:(h + 1) * 128],
                                        in0=kvst[:, h * 128:(h + 1) * 128],
                                        scalar=blkd[h],
                                        in1=P2[:, (n_ % 2) * 512:
                                               (n_ % 2) * 512 + 128],
                                        op0=Alu.mult, op1=Alu.add
                                        ).then_inc(cx.sem["dve"], 1)
                                    cx.bump("dve")
                        block.vector(phB_dve)
                        assert cx.cnt["dve"] == dve_cC
                    phB_dve_end = cx.cnt["dve"]
                    phB_pe_end = cx.cnt["pe"]

                with nc.sbuf_tensor(f"s{sidx}_wob", [128, 2 * 8 * D], BF16) as wob:
                    halves = ([list(range(0, 8)), list(range(8, H))]
                              if H > 8 else [list(range(H))])
                    wo_loaded = {}
                    for hi in range(len(halves)):
                        s_ = f"gu{hi % 2}"
                        wo_loaded[hi] = (s_, cx.cnt[s_] + 16)
                        cx.cnt[s_] += 16

                    dve_wo_evac = {}
                    dve_cD = cx.cnt["dve"]
                    for hi in range(len(halves)):
                        for ts_ in range(TT):
                            dve_cD += 1
                            dve_wo_evac[(hi, ts_)] = dve_cD

                    pe_wo = {}

                    def wo_pe(t):
                        cx.w(t, "dve", phB_dve_end)
                        for hi, hl in enumerate(halves):
                            t.wait_ge(cx.sem[wo_loaded[hi][0]], wo_loaded[hi][1])
                            for ts_ in range(TT):
                                prev = ((hi, ts_ - 1) if ts_ > 0 else
                                        ((hi - 1, TT - 1) if hi > 0 else None))
                                if prev is not None:
                                    t.wait_ge(cx.sem["dve"], dve_wo_evac[prev])
                                for gi, h in enumerate(hl):
                                    oap = oT[:, h * TPC + ts_ * 128:
                                             h * TPC + ts_ * 128 + 128]
                                    t.ldweights(oap)
                                    for co, cn in DC:
                                        ii = t.matmul(P0[:, co:co + cn], oap,
                                                      wob[:, (hi % 2) * 8 * D +
                                                          gi * D + co:
                                                          (hi % 2) * 8 * D +
                                                          gi * D + co + cn],
                                                      start=(gi == 0),
                                                      stop=(gi == len(hl) - 1))
                                ii.then_inc(cx.sem["pe"], 1)
                                cx.bump("pe")
                                pe_wo[(hi, ts_)] = cx.cnt["pe"]
                    block.tensor(wo_pe)

                    def wo_ev(v):
                        for hi in range(len(halves)):
                            for ts_ in range(TT):
                                cx.w(v, "pe", pe_wo[(hi, ts_)])
                                xt = x_res[:, ts_ * D:(ts_ + 1) * D]
                                v.tensor_tensor(out=xt, in0=P0[:, 0:D], in1=xt,
                                                op=Alu.add
                                                ).then_inc(cx.sem["dve"], 1)
                                v.drain()
                        cx.cnt["dve"] = dve_cD
                    block.vector(wo_ev)

                    def wo_stream(g):
                        cx.w(g, "pe", phB_pe_end)
                        cx.w(g, "dve", phB_dve_end)
                        for hi, hl in enumerate(halves):
                            g.dma_start(
                                out=wob[:, (hi % 2) * 8 * D:
                                        (hi % 2) * 8 * D + len(hl) * D]
                                .rearrange("p (i d) -> p i d", i=len(hl)),
                                in_=w_o0_d[hl[0]:hl[0] + len(hl)]
                                .rearrange("i p d -> p i d")
                                ).then_inc(cx.sem[wo_loaded[hi][0]], 16)
                    block.gpsimd(wo_stream)

                for ts_ in range(TT):
                    x_ok[ts_] = ("dve", dve_wo_evac[(len(halves) - 1, ts_)])


        f1_state = {}

        def sublayer_ffn1(sidx, first):
            ent = {k: cx.cnt[k] for k in ("pe", "dve", "act")}
            NT = D // 128
            LN2 = math.log(2.0)
            PB1 = P1.bitcast(BF16)
            NBK = 2 if D >= 2048 else 1
            HB_ = D // NBK

            def q_part1(src32, sc, r6_ap, ts_ap, W, parity=0, b0sel=0,
                        pre=None, hazard=()):
                HW = W // 2
                nbh = HW // 16
                amax, bs, q, t, tmp = (sc["amax"], sc["bs"], sc["q"],
                                       sc["t"], sc["tmp"])
                qi, tI, qI = sc["qi"], sc["t"].bitcast(I32), \
                    sc["q"].bitcast(I32)
                co0 = parity * 2 * nbh

                def S(buf, h):
                    return buf[:, co0 + h * nbh:co0 + (h + 1) * nbh]

                A3 = [src32[:, h * HW:(h + 1) * HW]
                      .rearrange("p (a b) -> p a b", b=16) for h in range(2)]
                B0 = [sc["big0"][:, b0sel * W + h * HW:
                                 b0sel * W + (h + 1) * HW] for h in range(2)]
                last = [None, None]

                def op(v, h, instr_fn):
                    if last[h] is not None:
                        v.wait_ge(cx.sem["dve"], last[h])
                    instr_fn().then_inc(cx.sem["dve"], 1)
                    cx.bump("dve")
                    last[h] = cx.cnt["dve"]

                def v1(v):
                    if pre is not None:
                        pre(v)
                    for h in range(2):
                        op(v, h, lambda h=h: v.tensor_reduce(
                            out=S(amax, h), in_=A3[h], axis=AX, op=Alu.max,
                            apply_absolute_value=True))
                    for h in range(2):
                        op(v, h, lambda h=h: v.tensor_scalar(
                            out=S(bs, h), in0=S(amax, h), scalar1=r6_ap,
                            scalar2=None, op0=Alu.mult))
                    for h in range(2):
                        op(v, h, lambda h=h: v.tensor_scalar(
                            out=S(bs, h), in0=S(bs, h), scalar1=448.0,
                            scalar2=None, op0=Alu.min))
                    for h in range(2):
                        op(v, h, lambda h=h: v.tensor_scalar(
                            out=S(t, h), in0=S(bs, h), scalar1=2.0 ** -30,
                            scalar2=None, op0=Alu.max))
                    for h in range(2):
                        op(v, h, lambda h=h: v.tensor_scalar(
                            out=S(qi, h), in0=S(tI, h), scalar1=23,
                            scalar2=None, op0=Alu.logical_shift_right))
                    for h in range(2):
                        op(v, h, lambda h=h: v.tensor_scalar(
                            out=S(qi, h), in0=S(qi, h), scalar1=121,
                            scalar2=-3, op0=Alu.max, op1=Alu.add))
                    for h in range(2):
                        op(v, h, lambda h=h: v.tensor_scalar(
                            out=S(qI, h), in0=S(qi, h), scalar1=23,
                            scalar2=None, op0=Alu.logical_shift_left))
                    for hz in hazard:
                        cx.w(v, hz[0], hz[1])
                    for h in range(2):
                        op(v, h, lambda h=h: v.reciprocal(
                            out=S(tmp, h), in_=S(q, h)))
                    for h in range(2):
                        op(v, h, lambda h=h: v.tensor_tensor(
                            out=S(t, h), in0=S(bs, h), in1=S(tmp, h),
                            op=Alu.mult))
                    for h in range(2):
                        op(v, h, lambda h=h: v.tensor_copy(
                            out=S(qi, h), in_=S(t, h)))
                    for h in range(2):
                        op(v, h, lambda h=h: v.tensor_tensor(
                            out=S(bs, h), in0=S(qi, h), in1=S(q, h),
                            op=Alu.mult))
                    for h in range(2):
                        op(v, h, lambda h=h: v.tensor_scalar(
                            out=S(tmp, h), in0=S(bs, h), scalar1=0.0,
                            scalar2=None, op0=Alu.is_equal))
                    for h in range(2):
                        op(v, h, lambda h=h: v.tensor_tensor(
                            out=S(tmp, h), in0=S(bs, h), in1=S(tmp, h),
                            op=Alu.add))
                    for h in range(2):
                        op(v, h, lambda h=h: v.tensor_scalar(
                            out=S(tmp, h), in0=S(tmp, h), scalar1=ts_ap,
                            scalar2=None, op0=Alu.mult))
                    for h in range(2):
                        op(v, h, lambda h=h: v.reciprocal(
                            out=S(q, h), in_=S(tmp, h)))
                    for h in range(2):
                        qe = S(q, h).unsqueeze(2).broadcast_to([128, nbh, 16])
                        op(v, h, lambda h=h, qe=qe: v.tensor_tensor(
                            out=B0[h].rearrange("p (a b) -> p a b", b=16),
                            in0=A3[h], in1=qe, op=Alu.mult))
                block.vector(v1)
                return {"last": last, "S": S, "B0": B0, "nbh": nbh,
                        "bs": bs, "co0": co0, "W": W, "HW": HW}

            def q_part2(st, dst_bf, sc, hazard=(), act_prev=None):
                HW = st["HW"]
                nbh = st["nbh"]
                B0 = st["B0"]
                S = st["S"]
                bs = st["bs"]
                B1 = [sc["big1"][:, h * HW:(h + 1) * HW] for h in range(2)]
                B2 = [sc["big2"][:, h * HW:(h + 1) * HW] for h in range(2)]
                DST = [dst_bf[:, h * HW:(h + 1) * HW] for h in range(2)]
                D3 = [d.rearrange("p (a b) -> p a b", b=16) for d in DST]
                last = st["last"]
                v3e = list(last)

                def op(v, h, instr_fn):
                    if last[h] is not None:
                        v.wait_ge(cx.sem["dve"], last[h])
                    instr_fn().then_inc(cx.sem["dve"], 1)
                    cx.bump("dve")
                    last[h] = cx.cnt["dve"]

                def a_sa(a):
                    if act_prev is not None:
                        cx.w(a, "act", act_prev)
                    for hz in hazard:
                        cx.w(a, hz[0], hz[1])
                    for h in range(2):
                        a.wait_ge(cx.sem["dve"], v3e[h])
                        a.activation(out=DST[h], in_=B0[h], func=Act.Sign
                                     ).then_inc(cx.sem["act"], 1)
                        cx.bump("act")
                    for h in range(2):
                        a.activation(out=B1[h], in_=B0[h], func=Act.Abs
                                     ).then_inc(cx.sem["act"], 1)
                        cx.bump("act")
                block.scalar(a_sa)
                c_abs = [cx.cnt["act"] - 1, cx.cnt["act"]]

                def v4(v):
                    for h in range(2):
                        v.wait_ge(cx.sem["act"], c_abs[h])
                        op(v, h, lambda h=h: v.tensor_scalar(
                            out=B2[h], in0=B1[h], scalar1=0.25, scalar2=None,
                            op0=Alu.is_ge))
                    for thr in (0.75, 1.25, 1.75):
                        for h in range(2):
                            op(v, h, lambda h=h, thr=thr:
                               v.scalar_tensor_tensor(
                                   out=B2[h], in0=B1[h], scalar=thr,
                                   in1=B2[h], op0=Alu.is_ge, op1=Alu.add))
                    for h in range(2):
                        op(v, h, lambda h=h: v.tensor_scalar(
                            out=B0[h], in0=B1[h], scalar1=2.5, scalar2=None,
                            op0=Alu.is_ge))
                    for h in range(2):
                        op(v, h, lambda h=h: v.scalar_tensor_tensor(
                            out=B0[h], in0=B1[h], scalar=3.5, in1=B0[h],
                            op0=Alu.is_ge, op1=Alu.add))
                    for h in range(2):
                        op(v, h, lambda h=h: v.scalar_tensor_tensor(
                            out=B0[h], in0=B2[h], scalar=0.5, in1=B0[h],
                            op0=Alu.mult, op1=Alu.add))
                    for h in range(2):
                        op(v, h, lambda h=h: v.tensor_scalar(
                            out=B2[h], in0=B1[h], scalar1=5.0, scalar2=None,
                            op0=Alu.is_ge))
                    for h in range(2):
                        op(v, h, lambda h=h: v.scalar_tensor_tensor(
                            out=B0[h], in0=B2[h], scalar=2.0, in1=B0[h],
                            op0=Alu.mult, op1=Alu.add))
                    for h in range(2):
                        op(v, h, lambda h=h: v.tensor_tensor(
                            out=B0[h], in0=B0[h], in1=DST[h], op=Alu.mult))
                    for h in range(2):
                        bse = S(bs, h).unsqueeze(2).broadcast_to([128, nbh, 16])
                        op(v, h, lambda h=h, bse=bse: v.tensor_tensor(
                            out=D3[h],
                            in0=B0[h].rearrange("p (a b) -> p a b", b=16),
                            in1=bse, op=Alu.mult))
                block.vector(v4)
                return list(last)

            scb = f1_scb
            with (
                nc.sbuf_tensor(f"s{sidx}_sc1", [1, 8], F32) as sc1,
                nc.sbuf_tensor(f"s{sidx}_tsx", [1, 1], F32) as tsx_t,
                nc.sbuf_tensor(f"s{sidx}_rsw", [1, 132], F32) as rsw,
            ):
                def mk_scratch(stk, tag, b0depth=1):
                    d = {}
                    for k_, nm in (("amax", "qa0"), ("bs", "qa1"),
                                   ("q", "qa3"),
                                   ("t", "qa4"), ("tmp", "qa6")):
                        d[k_] = stk.enter_context(
                            nc.sbuf_tensor(f"s{sidx}{tag}_{nm}", [128, 256],
                                           F32))
                    d["qi"] = stk.enter_context(
                        nc.sbuf_tensor(f"s{sidx}{tag}_qai", [128, 256], I32))
                    d["big0"] = stk.enter_context(
                        nc.sbuf_tensor(f"s{sidx}{tag}_big0",
                                       [128, b0depth * D], F32))
                    for nm in ("big1", "big2"):
                        d[nm] = stk.enter_context(
                            nc.sbuf_tensor(f"s{sidx}{tag}_{nm}", [128, D],
                                           F32))
                    return d

                import contextlib

                if first:
                    def tsxl(sc_):
                        cx.w(sc_, "act", ent["act"])
                        cx.w(sc_, "dve", ent["dve"])
                        sc_.dma_start(out=tsx_t[:], in_=ts_f1_d).then_inc(
                            cx.sem["dS"], 16)
                        cx.bump("dS", 16)
                    block.scalar(tsxl)

                    wq_d = dt("wq_stash", [KT, NT, 128, 128], BF16,
                              kind="Internal").ap()
                    f1_state["wq_d"] = wq_d
                    with contextlib.ExitStack() as stk:
                        sc = mk_scratch(stk, "w", b0depth=2)
                        wbuf = stk.enter_context(
                            nc.sbuf_tensor(f"s{sidx}_wb", [128, 2 * D], F32))
                        gm = stk.enter_context(
                            nc.sbuf_tensor(f"s{sidx}_gm", [128, NT], F32))
                        wdq = stk.enter_context(
                            nc.sbuf_tensor(f"s{sidx}_wdq", [128, D], BF16))
                        tpc_ = stk.enter_context(
                            nc.sbuf_tensor(f"s{sidx}_tp", [128, 2 * D], BF16))
                        wl = {}
                        for nt in range(NT):
                            s_ = f"hw{nt % 2}"
                            wl[nt] = (s_, cx.cnt[s_] + 16)
                            cx.cnt[s_] += 16
                        gm_red = {}
                        dve_cW = cx.cnt["dve"]
                        for nt in range(NT):
                            dve_cW += 1
                            gm_red[nt] = dve_cW

                        def gml(sync):
                            cx.w(sync, "dve", ent["dve"])
                            cx.w(sync, "act", ent["act"])
                            for nt in range(NT):
                                if nt >= 2:
                                    cx.w(sync, "dve", gm_red[nt - 2])
                                sync.dma_start(out=wbuf[:, (nt % 2) * D:
                                                       (nt % 2) * D + D],
                                               in_=w_f1_d[nt]).then_inc(
                                    cx.sem[wl[nt][0]], 16)
                        block.sync(gml)

                        def gmr(v):
                            for nt in range(NT):
                                v.wait_ge(cx.sem[wl[nt][0]], wl[nt][1])
                                v.tensor_reduce(out=gm[:, nt:nt + 1],
                                                in_=wbuf[:, (nt % 2) * D:
                                                         (nt % 2) * D + D],
                                                axis=AX, op=Alu.max,
                                                apply_absolute_value=True
                                                ).then_inc(cx.sem["dve"], 1)
                            v.wait_ge(cx.sem["dve"], dve_cW)
                            v.tensor_reduce(out=gm[:, 0:1], in_=gm[:, 0:NT],
                                            axis=AX, op=Alu.max)
                            v.drain().then_inc(cx.sem["dve"], 1)
                        cx.cnt["dve"] = dve_cW + 1
                        block.vector(gmr)
                        gm_c = cx.cnt["dve"]

                        def gmt(t):
                            cx.w(t, "dve", gm_c)
                            cx.w(t, "dA", 128)
                            t.transpose(P0[0:1, 0:128], gm[:, 0:1], ident32_t[:]
                                        ).then_inc(cx.sem["pe"], 1)
                        cx.bump("pe")
                        block.tensor(gmt)
                        gmt_c = cx.cnt["pe"]

                        def vv(v, instr):
                            instr.then_inc(cx.sem["dve"], 1)
                            cx.bump("dve")
                            v.drain()

                        def gms(v):
                            cx.w(v, "pe", gmt_c)
                            cx.w(v, "dS", cx.cnt["dS"])
                            vv(v, v.tensor_reduce(out=sc1[0:1, 0:1],
                                                  in_=P0[0:1, 0:128], axis=AX,
                                                  op=Alu.max,
                                                  apply_absolute_value=True))
                            vv(v, v.tensor_scalar(out=sc1[0:1, 2:3],
                                                  in0=sc1[0:1, 0:1],
                                                  scalar1=1.0 / 2688.0,
                                                  scalar2=None, op0=Alu.mult))
                            vv(v, v.tensor_scalar(out=sc1[0:1, 1:2],
                                                  in0=sc1[0:1, 0:1],
                                                  scalar1=6.0 / 2688.0,
                                                  scalar2=None, op0=Alu.mult))
                            vv(v, v.reciprocal(out=sc1[0:1, 1:2],
                                               in_=sc1[0:1, 1:2]))
                            vv(v, v.tensor_copy(out=sc1[0:1, 4:5],
                                                in_=tsx_t[0:1, 0:1]))
                            vv(v, v.tensor_scalar(out=sc1[0:1, 3:4],
                                                  in0=sc1[0:1, 4:5],
                                                  scalar1=6.0, scalar2=None,
                                                  op0=Alu.mult))
                            vv(v, v.reciprocal(out=sc1[0:1, 3:4],
                                               in_=sc1[0:1, 3:4]))
                            vv(v, v.tensor_tensor(out=sc1[0:1, 5:6],
                                                  in0=sc1[0:1, 4:5],
                                                  in1=sc1[0:1, 2:3],
                                                  op=Alu.mult))
                            vv(v, v.tensor_scalar(out=sc1[0:1, 5:6],
                                                  in0=sc1[0:1, 5:6],
                                                  scalar1=0.5, scalar2=None,
                                                  op0=Alu.mult))
                        block.vector(gms)
                        sc1_c = cx.cnt["dve"]

                        def bct(t):
                            cx.w(t, "dve", sc1_c)
                            cx.w(t, "dA", 128)
                            t.matmul(P0[:, 0:6], ones32_t[:], sc1[0:1, 0:6],
                                     start=True, stop=True
                                     ).then_inc(cx.sem["pe"], 1)
                        cx.bump("pe")
                        block.tensor(bct)
                        bct_c = cx.cnt["pe"]

                        def bcv(v):
                            cx.w(v, "pe", bct_c)
                            vv(v, v.tensor_copy(out=scb[:, 0:6], in_=P0[:, 0:6]))
                        block.vector(bcv)
                        scb_c = cx.cnt["dve"]
                        f1_state["scb_c"] = scb_c

                        wtr_done = {}
                        states = {}
                        qe = {}

                        def emit_wload(nt):
                            s_ = f"hw{nt % 2}"
                            ld_c = cx.cnt[s_] + 16
                            cx.cnt[s_] += 16
                            prev = max(qe[nt - 2]) if nt - 2 in qe else scb_c

                            def wload(sync, nt=nt, s_=s_, prev=prev):
                                cx.w(sync, "dve", prev)
                                sync.dma_start(out=wbuf[:, (nt % 2) * D:
                                                       (nt % 2) * D + D],
                                               in_=w_f1_d[nt]).then_inc(
                                    cx.sem[s_], 16)
                            block.sync(wload)
                            return s_, ld_c

                        def emit_p1(nt, s_, ld_c):
                            def pre(v, s_=s_, ld_c=ld_c):
                                v.wait_ge(cx.sem[s_], ld_c)
                                cx.w(v, "dve", scb_c)
                            states[nt] = q_part1(
                                wbuf[:, (nt % 2) * D:(nt % 2) * D + D], sc,
                                scb[:, 1:2], scb[:, 2:3], D,
                                parity=nt % 2, b0sel=nt % 2, pre=pre)

                        ap_cell = {"a": ent["act"]}

                        def emit_p2_tr(nt):
                            hz = ([("dve", wtr_done[nt - 1])]
                                  if nt >= 1 else [])
                            if nt - 1 in qe:
                                hz.append(("dve", max(qe[nt - 1])))
                            qe[nt] = q_part2(states[nt], wdq[:], sc, hazard=hz,
                                             act_prev=ap_cell["a"])
                            ap_cell["a"] = cx.cnt["act"]
                            qd = max(qe[nt])
                            f1_state["wq_done"] = qd
                            tp_pe = cx.cnt["pe"]

                            def wtr(t, nt=nt, qd=qd):
                                cx.w(t, "dve", qd)
                                if nt >= 1:
                                    cx.w(t, "dve", wtr_done[nt - 1])
                                if nt == 0:
                                    cx.w(t, "dA", 128)
                                for kt in range(KT):
                                    t.transpose(PB1[:, kt * 128:kt * 128 + 128],
                                                wdq[:, kt * 128:kt * 128 + 128],
                                                ident_t[:]).then_inc(
                                        cx.sem["pe"], 1)
                            cx.bump("pe", KT)
                            block.tensor(wtr)

                            hwW_c = cx.cnt[f"hwW{nt % 2}"]

                            def wtre(v, nt=nt, tp_pe=tp_pe, hwW_c=hwW_c):
                                for b_ in range(NBK):
                                    v.wait_ge(cx.sem["pe"],
                                              tp_pe + (KT // NBK) * (b_ + 1))
                                    if b_ == 0 and nt >= 2:
                                        v.wait_ge(cx.sem[f"hwW{nt % 2}"], hwW_c)
                                    v.tensor_copy(
                                        out=tpc_[:, (nt % 2) * D + b_ * HB_:
                                                (nt % 2) * D + (b_ + 1) * HB_],
                                        in_=PB1[:, b_ * HB_:(b_ + 1) * HB_]
                                        ).then_inc(cx.sem["dve"], 1)
                            cx.bump("dve", NBK)
                            wtr_done[nt] = cx.cnt["dve"]
                            block.vector(wtre)
                            tre_c = cx.cnt["dve"]

                            def wst(sync, nt=nt, tre_c=tre_c):
                                cx.w(sync, "dve", tre_c)
                                sync.dma_start(
                                    out=wq_d[:, nt].rearrange("k p m -> p k m"),
                                    in_=tpc_[:, (nt % 2) * D:(nt % 2) * D + D]
                                    .rearrange("p (k m) -> p k m", k=KT)
                                    ).then_inc(cx.sem[f"hwW{nt % 2}"], 16)
                                cx.bump(f"hwW{nt % 2}", 16)
                            block.sync(wst)

                        for nt in range(NT):
                            s_, ld_c = emit_wload(nt)
                            emit_p1(nt, s_, ld_c)
                            if nt >= 1:
                                emit_p2_tr(nt - 1)
                        emit_p2_tr(NT - 1)
                        f1_state["wq_w_end"] = (cx.cnt["hwW0"], cx.cnt["hwW1"])
                        f1_state["prep_dve"] = cx.cnt["dve"]
                else:
                    wq_d = f1_state["wq_d"]

                with contextlib.ExitStack() as stk:
                    sc = mk_scratch(stk, "x")
                    xn32 = stk.enter_context(
                        nc.sbuf_tensor(f"s{sidx}_xn32", [128, D], F32))
                    xdq = stk.enter_context(
                        nc.sbuf_tensor(f"s{sidx}_xdq", [128, D], BF16))
                    xdqT = stk.enter_context(
                        nc.sbuf_tensor(f"s{sidx}_xdqT", [128, 2 * D], BF16))
                    wqs = stk.enter_context(
                        nc.sbuf_tensor(f"s{sidx}_wqs", [128, KT * D], BF16))
                    br = stk.enter_context(
                        nc.sbuf_tensor(f"s{sidx}_br", [1, D], F32))
                    scb_c0 = (max(f1_state["scb_c"], f1_state["prep_dve"])
                              if first else f1_state["scb_c"])

                    def wql(sync):
                        if first:
                            sync.wait_ge(cx.sem["hwW0"], f1_state["wq_w_end"][0])
                            sync.wait_ge(cx.sem["hwW1"], f1_state["wq_w_end"][1])
                            cx.w(sync, "dve", f1_state["prep_dve"])
                        else:
                            cx.w(sync, "dve", ent["dve"])
                            cx.w(sync, "act", ent["act"])
                        for kt in range(KT):
                            sync.dma_start(
                                out=wqs[:, kt * D:(kt + 1) * D]
                                .rearrange("p (n m) -> p n m", n=NT),
                                in_=wq_d[kt].rearrange("n p m -> p n m")
                                ).then_inc(cx.sem["hw0"], 16)
                            cx.bump("hw0", 16)
                    block.sync(wql)
                    wq_all = cx.cnt["hw0"]

                    def bld(sc_):
                        cx.w(sc_, "act", ent["act"])
                        cx.w(sc_, "dve", ent["dve"])
                        if first:
                            sc_.wait_ge(cx.sem["hwW0"], f1_state["wq_w_end"][0])
                            sc_.wait_ge(cx.sem["hwW1"], f1_state["wq_w_end"][1])
                            cx.w(sc_, "dve", f1_state["prep_dve"])
                        sc_.dma_start(out=br[:], in_=b_f1_d).then_inc(
                            cx.sem["dS"], 16)
                        cx.bump("dS", 16)
                    block.scalar(bld)
                    bld_c = cx.cnt["dS"]

                    def brs(v):
                        cx.w(v, "dve", scb_c0)
                        v.reciprocal(out=rsw[0:1, 128:129], in_=scb[0:1, 5:6]
                                     ).then_inc(cx.sem["dve"], 1)
                        cx.bump("dve")
                        v.wait_ge(cx.sem["dve"], cx.cnt["dve"])
                        v.tensor_scalar(out=rsw[0:1, 0:128],
                                        in0=ones32_t[0:1, 0:128],
                                        scalar1=rsw[0:1, 128:129],
                                        scalar2=0.5, op0=Alu.mult,
                                        op1=Alu.mult).then_inc(
                            cx.sem["dve"], 1)
                        cx.bump("dve")
                    block.vector(brs)
                    rsw_c = cx.cnt["dve"]

                    te_done = {}
                    mm_done = {}
                    mmev_done = {}

                    def emit_mmev(ts_):
                        def f(v, ts_=ts_):
                            cx.w(v, "pe", mm_done[ts_])
                            xt_ = x_res[:, ts_ * D:(ts_ + 1) * D]
                            v.scalar_tensor_tensor(
                                out=xt_, in0=P0[:, 0:D],
                                scalar=scb[:, 5:6], in1=xt_,
                                op0=Alu.mult, op1=Alu.add).then_inc(
                                cx.sem["dve"], 1)
                            cx.bump("dve")
                        block.vector(f)
                        mmev_done[ts_] = cx.cnt["dve"]
                        x_ok[ts_] = ("dve", cx.cnt["dve"])

                    qe_prev = None
                    xap_cell = {"a": ent["act"]}
                    for tt in range(TT):
                        a_rdy = emit_ln(tt, None, xn32[:], pe_guard=None)

                        def pre(v, a_rdy=a_rdy):
                            cx.w(v, "act", a_rdy)
                            cx.w(v, "dve", scb_c0)
                        st_q = q_part1(xn32[:], sc, scb[:, 3:4], scb[:, 4:5],
                                       D, parity=0, b0sel=0, pre=pre)

                        if tt >= 1:
                            emit_mmev(tt - 1)

                        hz = ([("dve", te_done[tt - 1])] if tt >= 1 else [])
                        if qe_prev is not None:
                            hz.append(("dve", max(qe_prev)))
                        qe_ = q_part2(st_q, xdq[:], sc, hazard=hz,
                                      act_prev=xap_cell["a"])
                        xap_cell["a"] = cx.cnt["act"]
                        qe_prev = qe_
                        qd = max(qe_)

                        tp_pe = cx.cnt["pe"]

                        def xtr(t, tt=tt, qd=qd):
                            cx.w(t, "dve", qd)
                            if tt >= 1:
                                cx.w(t, "dve", te_done[tt - 1])
                            if tt == 0:
                                cx.w(t, "dA", 128)
                            for kt in range(KT):
                                t.transpose(PB1[:, kt * 128:kt * 128 + 128],
                                            xdq[:, kt * 128:kt * 128 + 128],
                                            ident_t[:]).then_inc(cx.sem["pe"], 1)
                        cx.bump("pe", KT)
                        block.tensor(xtr)

                        slot = (tt % 2) * D

                        def xte(v, tt=tt, tp_pe=tp_pe, slot=slot):
                            if tt == 0:
                                cx.w(v, "act", ent["act"])
                            for b in range(NBK):
                                v.wait_ge(cx.sem["pe"],
                                          tp_pe + (KT // NBK) * (b + 1))
                                if tt >= 2 and b == 0:
                                    v.wait_ge(cx.sem["pe"], mm_done[tt - 2])
                                v.tensor_copy(
                                    out=xdqT[:, slot + b * HB_:
                                            slot + (b + 1) * HB_],
                                    in_=PB1[:, b * HB_:(b + 1) * HB_]
                                    ).then_inc(cx.sem["dve"], 1)
                        cx.bump("dve", NBK)
                        te_done[tt] = cx.cnt["dve"]
                        block.vector(xte)

                        def fmm(t, tt=tt, slot=slot):
                            cx.w(t, "dve", te_done[tt])
                            if tt == 0:
                                t.wait_ge(cx.sem["hw0"], wq_all)
                                cx.w(t, "dve", rsw_c)
                                cx.w(t, "dS", bld_c)
                                cx.w(t, "act", ent["act"])
                            else:
                                cx.w(t, "dve", mmev_done[tt - 1])
                            for co, cn in DC:
                                t.matmul(P0[:, co:co + cn], rsw[0:1, 0:128],
                                         br[0:1, co:co + cn], start=True,
                                         stop=False)
                            for kt in range(KT):
                                xap = xdqT[:, slot + kt * 128:
                                           slot + kt * 128 + 128]
                                t.ldweights(xap)
                                for co, cn in DC:
                                    ii = t.matmul(P0[:, co:co + cn], xap,
                                                  wqs[:, kt * D + co:
                                                      kt * D + co + cn],
                                                  start=False,
                                                  stop=(kt == KT - 1))
                            ii.then_inc(cx.sem["pe"], 1)
                        cx.bump("pe", 1)
                        mm_done[tt] = cx.cnt["pe"]
                        block.tensor(fmm)
                    emit_mmev(TT - 1)


        def sublayer_attn1(sidx):
            ent = {k: cx.cnt[k] for k in ("pe", "dve", "act")}
            H1 = D // 128
            S1 = 4 * TPC
            RS = 2.0 ** -3.5
            NP = 3 * H1
            nrm_ev = len(TC) + (1 if len(TC) > 1 else 0)

            with (
                nc.sbuf_tensor(f"s{sidx}_xoT", [128, KT * TPC], BF16) as xoT,
                nc.sbuf_tensor(f"s{sidx}_negM", [128, H1], F32) as negM,
                nc.sbuf_tensor(f"s{sidx}_mq", [128, H1], F32) as mq,
                nc.sbuf_tensor(f"s{sidx}_mkb", [128, H1], BF16) as mkb,
                nc.sbuf_tensor(f"s{sidx}_bqt", [128, 3 * H1], F32) as bqt,
                nc.sbuf_tensor(f"s{sidx}_nrow", [1, 2 * H1], F32) as nrow,
                nc.sbuf_tensor(f"s{sidx}_qf", [128, H1 * TPC], BF16) as qf,
            ):
                xnT = xoT[:, 0:KT * TPC]
                oTn = xoT[:, 0:H1 * TPC]
                NHG = (H1 + 3) // 4
                NG = (H1 + NHG - 1) // NHG
                cc_ing = [dt(f"s{sidx}_ccin{g}", [1, 128 * 2 * NHG * TPC],
                             BF16, kind="Internal").ap() for g in range(NG)]
                cc_outg = [dt(f"s{sidx}_ccout{g}", [4, 128 * 2 * NHG * TPC],
                              BF16, kind="Internal").ap() for g in range(NG)]
                cc_inn = dt(f"s{sidx}_ccinn", [1, 128 * H1], BF16,
                            kind="Internal").ap()
                cc_outn = dt(f"s{sidx}_ccoutn", [4, 128 * H1], BF16,
                             kind="Internal").ap()

                def cig(g):
                    return cc_ing[g].rearrange("o (p f) -> (o p) f", p=128)

                def cogv(g, r):
                    return cc_outg[g][r:r + 1].rearrange("o (p f) -> (o p) f",
                                                         p=128)

                cinnv = cc_inn.rearrange("o (p f) -> (o p) f", p=128)

                def connv(r):
                    return cc_outn[r:r + 1].rearrange("o (p f) -> (o p) f",
                                                      p=128)

                def aux(sc_):
                    cx.w(sc_, "act", ent["act"])
                    cx.w(sc_, "dve", ent["dve"])
                    for i_, nm in enumerate(("bq", "bk", "bv")):
                        sc_.dma_start(
                            out=bqt[:, i_ * H1:(i_ + 1) * H1],
                            in_=b_w1_d[nm].rearrange("o (h p) -> (o p) h", p=128),
                            allow_slow_non_contiguous=True
                            ).then_inc(cx.sem["dS"], 16)
                        cx.bump("dS", 16)
                block.scalar(aux)
                aux_c = cx.cnt["dS"]

                def bprep(v):
                    cx.w(v, "dS", aux_c)
                    v.tensor_scalar(out=bqt[:, 0:H1], in0=bqt[:, 0:H1],
                                    scalar1=RS, scalar2=None, op0=Alu.mult
                                    ).then_inc(cx.sem["dve"], 1)
                    cx.bump("dve")
                    v.drain()
                block.vector(bprep)
                bprep_c = cx.cnt["dve"]

                with (
                    nc.sbuf_tensor(f"s{sidx}_kst", [128, 4 * TPC], BF16) as kst,
                    nc.sbuf_tensor(f"s{sidx}_vtt", [128, 4 * TPC], BF16) as vtt,
                    nc.sbuf_tensor(f"s{sidx}_xn32", [128, D], F32) as xn32,
                    nc.sbuf_tensor(f"s{sidx}_qkb", [128, 2 * KT * 128], BF16) as qkb,
                    nc.sbuf_tensor(f"s{sidx}_ssb", [128, 2 * KT * 128], BF16) as ssb,
                    nc.sbuf_tensor(f"s{sidx}_sq32", [128, TPC], F32) as sq32,
                ):
                    pe_g = None
                    for tt in range(TT):
                        a_rdy = emit_ln(tt, None, xn32[:], pe_guard=pe_g)
                        emit_transposes(
                            KT, lambda kt: xn32[:, kt * 128:(kt + 1) * 128],
                            lambda kt, tt=tt: xnT[:, kt * TPC + tt * 128:
                                                  kt * TPC + (tt + 1) * 128],
                            ("act", a_rdy), extra=("dve", ent["dve"]))
                        pe_g = cx.cnt["pe"]
                    xnT_dve, xnT_act = cx.cnt["dve"], cx.cnt["act"]

                    qk_loaded = {}
                    for si in range(NP):
                        s_ = f"gu{si % 2}"
                        qk_loaded[si] = (s_, cx.cnt[s_] + 32)
                        cx.cnt[s_] += 32

                    def wname(si):
                        return ("wk", si) if si < H1 else \
                               (("wv", si - H1) if si < 2 * H1 else
                                ("wq", si - 2 * H1))

                    psched = [("dq", 0)]
                    if NP > 1:
                        psched.append(("dq", 1))
                    for si in range(2, NP):
                        psched.append(("x", si - 2))
                        psched.append(("dq", si))
                    if NP > 1:
                        psched.append(("x", NP - 2))
                    psched.append(("x", NP - 1))
                    dve_dq = {}
                    dve_nrm = {}
                    dve_vtr = {}
                    dve_c1 = cx.cnt["dve"]
                    for item in psched:
                        si = item[1]
                        nm, _h = wname(si)
                        if item[0] == "dq":
                            dve_c1 += 1
                            dve_dq[si] = dve_c1
                        elif nm in ("wk", "wq"):
                            dve_c1 += nrm_ev
                            dve_nrm[si] = dve_c1
                        else:
                            dve_c1 += TT
                            dve_vtr[si] = dve_c1
                    act_evac = {}
                    act_sq = {}
                    act_c0 = cx.cnt["act"]
                    for si in range(NP):
                        act_c0 += 1
                        act_evac[si] = act_c0
                        if si < H1 or si >= 2 * H1:
                            act_c0 += 1
                            act_sq[si] = act_c0
                    GSEM = ["hw2", "hwA", "hwB", "hwC"]

                    def exp_sem(si):
                        h_ = si if si < H1 else si - H1
                        return GSEM[(h_ // NHG) % 4]
                    hw2_plan = {}
                    _ec = {s: cx.cnt[s] for s in ("hw2", "hwA", "hwB", "hwC")}
                    _base = dict(_ec)
                    for si in range(2 * H1):
                        s_ = exp_sem(si)
                        _ec[s_] += 16
                        hw2_plan[si] = (s_, _ec[s_])
                    grp_kfull = {g: _base[GSEM[g % 4]] + 16 * NHG
                                 for g in range(NG)}
                    grp_full = {g: _base[GSEM[g % 4]] + 32 * NHG
                                for g in range(NG)}
                    nrm_exp = ("dA", cx.cnt["dA"] + 16)
                    cx.cnt["dA"] += 16
                    hw2_all = dict(_ec)
                    for s_ in _ec:
                        cx.cnt[s_] = _ec[s_]

                    pe_proj = {}
                    pe_norm = {}
                    pe_vtr = {}
                    prev_norm = {}
                    lastn = None
                    for si in range(NP):
                        nm, _h = wname(si)
                        if nm in ("wk", "wq"):
                            prev_norm[si] = lastn
                            lastn = si

                    PB2 = P2.bitcast(BF16)

                    def pe_proj_fn(t):
                        cx.w(t, "dve", max(xnT_dve, bprep_c))
                        cx.w(t, "act", xnT_act)
                        for si in range(NP):
                            nm, h = wname(si)
                            t.wait_ge(cx.sem["dve"], dve_dq[si])
                            if si >= 2:
                                t.wait_ge(cx.sem["act"], act_evac[si - 2])
                            ps = P1 if si % 2 == 0 else P0
                            sl = (si % 2) * KT * 128
                            for kt in range(KT):
                                wap = qkb[:, sl + kt * 128:sl + kt * 128 + 128]
                                t.ldweights(wap)
                                for tco, tcn in TC:
                                    ii = t.matmul(ps[:, tco:tco + tcn], wap,
                                                  xnT[:, kt * TPC + tco:
                                                      kt * TPC + tco + tcn],
                                                  start=(kt == 0),
                                                  stop=(kt == KT - 1))
                            ii.then_inc(cx.sem["pe"], 1)
                            cx.bump("pe")
                            pe_proj[si] = cx.cnt["pe"]
                            if nm in ("wk", "wq"):
                                t.wait_ge(cx.sem["act"], act_sq[si])
                                if prev_norm[si] is not None:
                                    t.wait_ge(cx.sem["dve"],
                                              dve_nrm[prev_norm[si]])
                                for ci, (co, cn) in enumerate(TC):
                                    ii = t.matmul(P0[0:1, 1024 + ci * 512:
                                                     1024 + ci * 512 + cn],
                                                  onec_f_t[:],
                                                  sq32[:, co:co + cn],
                                                  start=True, stop=True)
                                ii.then_inc(cx.sem["pe"], 1)
                                cx.bump("pe")
                                pe_norm[si] = cx.cnt["pe"]
                            else:
                                t.wait_ge(cx.sem["act"], act_evac[si])
                                for tt_ in range(TT):
                                    if tt_ >= 2:
                                        t.wait_ge(cx.sem["dve"],
                                                  dve_vtr[si] - TT + tt_ - 1)
                                    elif si - 1 >= H1:
                                        t.wait_ge(cx.sem["dve"],
                                                  dve_vtr[si - 1] + tt_ - 1)
                                    t.transpose(
                                        PB2[:, (tt_ % 2) * 1024:
                                            (tt_ % 2) * 1024 + 128],
                                        vtt[:, (si % 4) * TPC + tt_ * 128:
                                            (si % 4) * TPC + tt_ * 128 + 128],
                                        ident_t[:]).then_inc(cx.sem["pe"], 1)
                                    cx.bump("pe")
                                pe_vtr[si] = cx.cnt["pe"]
                    block.tensor(pe_proj_fn)
                    pe_proj_end = cx.cnt["pe"]

                    def stream1(g):
                        cx.w(g, "pe", ent["pe"])
                        cx.w(g, "dve", ent["dve"])
                        for si in range(NP):
                            nm, h = wname(si)
                            if si >= 2:
                                g.wait_ge(cx.sem["pe"], pe_proj[si - 2])
                            sl = (si % 2) * KT * 128
                            g.dma_start(out=qkb[:, sl:sl + KT * 128]
                                        .rearrange("p (k n) -> p k n", k=KT),
                                        in_=c_w1_d[nm][h]).then_inc(
                                cx.sem[qk_loaded[si][0]], 16)
                            g.dma_start(
                                out=ssb[:, sl:sl + KT * 128]
                                .rearrange("p (k n) -> p k n", k=KT),
                                in_=s_w1_d[nm][h]).then_inc(
                                cx.sem[qk_loaded[si][0]], 16)
                    block.gpsimd(stream1)

                    def act_fn(a):
                        cx.w(a, "dS", aux_c)
                        cx.w(a, "dve", bprep_c)
                        for si in range(NP):
                            nm, h = wname(si)
                            cx.w(a, "pe", pe_proj[si])
                            ps = P1 if si % 2 == 0 else P0
                            if nm == "wk":
                                if si >= 4:
                                    g4 = (h - 4) // NHG
                                    a.wait_ge(cx.sem[GSEM[g4 % 4]],
                                              grp_kfull[g4])
                                a.activation(out=kst[:, (si % 4) * TPC:
                                                    (si % 4) * TPC + TPC],
                                             in_=ps[:, 0:TPC], func=Act.Identity,
                                             bias=bqt[:, H1 + h:H1 + h + 1],
                                             scale=1.0).then_inc(cx.sem["act"], 1)
                                a.drain()
                                if prev_norm[si] is not None:
                                    cx.w(a, "dve", dve_nrm[prev_norm[si]])
                                a.activation(out=sq32[:],
                                             in_=kst[:, (si % 4) * TPC:
                                                    (si % 4) * TPC + TPC],
                                             func=Act.Square
                                             ).then_inc(cx.sem["act"], 1)
                            elif nm == "wv":
                                if si - 4 >= H1:
                                    g4 = (h - 4) // NHG
                                    a.wait_ge(cx.sem[GSEM[g4 % 4]],
                                              grp_full[g4])
                                a.activation(out=vtt[:, (si % 4) * TPC:
                                                    (si % 4) * TPC + TPC],
                                             in_=ps[:, 0:TPC], func=Act.Identity,
                                             bias=bqt[:, 2 * H1 + h:
                                                      2 * H1 + h + 1],
                                             scale=1.0).then_inc(cx.sem["act"], 1)
                            else:
                                a.activation(out=qf[:, h * TPC:(h + 1) * TPC],
                                             in_=ps[:, 0:TPC], func=Act.Identity,
                                             bias=bqt[:, h:h + 1],
                                             scale=RS).then_inc(cx.sem["act"], 1)
                                a.drain()
                                if prev_norm[si] is not None:
                                    cx.w(a, "dve", dve_nrm[prev_norm[si]])
                                a.activation(out=sq32[:],
                                             in_=qf[:, h * TPC:(h + 1) * TPC],
                                             func=Act.Square
                                             ).then_inc(cx.sem["act"], 1)
                    cx.cnt["act"] = act_c0
                    block.scalar(act_fn)

                    def dvemix(v):
                        cx.w(v, "dve", xnT_dve)
                        for item in psched:
                            si = item[1]
                            nm, h = wname(si)
                            if item[0] == "dq":
                                v.wait_ge(cx.sem[qk_loaded[si][0]],
                                          qk_loaded[si][1])
                                if si >= 2:
                                    v.wait_ge(cx.sem["pe"], pe_proj[si - 2])
                                sl = (si % 2) * KT * 128
                                v.tensor_tensor(out=qkb[:, sl:sl + KT * 128],
                                                in0=qkb[:, sl:sl + KT * 128],
                                                in1=ssb[:, sl:sl + KT * 128],
                                                op=Alu.mult
                                                ).then_inc(cx.sem["dve"], 1)
                                cx.bump("dve")
                            elif nm in ("wk", "wq"):
                                v.wait_ge(cx.sem["pe"], pe_norm[si])
                                ro = (H1 if nm == "wq" else 0) + h
                                for ci in range(len(TC)):
                                    dst = (nrow[0:1, ro:ro + 1] if ci == 0
                                           else sq32[0:1, 0:1])
                                    cn = TC[ci][1]
                                    v.tensor_reduce(
                                        out=dst,
                                        in_=P0[0:1, 1024 + ci * 512:
                                               1024 + ci * 512 + cn],
                                        axis=AX, op=Alu.max
                                        ).then_inc(cx.sem["dve"], 1)
                                    cx.bump("dve")
                                if len(TC) > 1:
                                    v.wait_ge(cx.sem["dve"], cx.cnt["dve"])
                                    v.tensor_tensor(out=nrow[0:1, ro:ro + 1],
                                                    in0=nrow[0:1, ro:ro + 1],
                                                    in1=sq32[0:1, 0:1],
                                                    op=Alu.max
                                                    ).then_inc(cx.sem["dve"], 1)
                                    cx.bump("dve")
                            else:
                                base_pe = pe_vtr[si] - TT
                                for tt_ in range(TT):
                                    v.wait_ge(cx.sem["pe"], base_pe + tt_ + 1)
                                    v.tensor_copy(
                                        out=vtt[:, (si % 4) * TPC + tt_ * 128:
                                                (si % 4) * TPC + tt_ * 128 + 128],
                                        in_=PB2[:, (tt_ % 2) * 1024:
                                                (tt_ % 2) * 1024 + 128]
                                        ).then_inc(cx.sem["dve"], 1)
                                    cx.bump("dve")
                    block.vector(dvemix)
                    assert cx.cnt["dve"] == dve_c1, (cx.cnt["dve"], dve_c1)

                    def exp_sync(sync):
                        for si in range(2 * H1):
                            nm, h = wname(si)
                            g_ = h // NHG
                            hj = h % NHG
                            if nm == "wk":
                                cx.w(sync, "act", act_evac[si])
                                sync.dma_start(
                                    out=cig(g_)[:, hj * TPC:(hj + 1) * TPC],
                                    in_=kst[:, (si % 4) * TPC:
                                            (si % 4) * TPC + TPC]
                                    ).then_inc(cx.sem[hw2_plan[si][0]], 16)
                            else:
                                cx.w(sync, "dve", dve_vtr[si])
                                sync.dma_start(
                                    out=cig(g_)[:, (NHG + hj) * TPC:
                                                (NHG + hj + 1) * TPC],
                                    in_=vtt[:, (si % 4) * TPC:
                                            (si % 4) * TPC + TPC]
                                    ).then_inc(cx.sem[hw2_plan[si][0]], 16)
                    block.sync(exp_sync)

                    def nb_pe(t):
                        cx.w(t, "dve", max(dve_nrm.values()))
                        t.matmul(P0[:, 1024:1024 + H1], ones32_t[:],
                                 nrow[0:1, 0:H1], start=True, stop=True)
                        ii = t.matmul(P0[:, 1536:1536 + H1], ones32_t[:],
                                      nrow[0:1, H1:2 * H1], start=True,
                                      stop=True)
                        ii.then_inc(cx.sem["pe"], 1)
                        cx.bump("pe")
                    block.tensor(nb_pe)
                    nb_c = cx.cnt["pe"]

                    def nb_v(v):
                        cx.w(v, "pe", nb_c)
                        v.tensor_copy(out=mkb[:], in_=P0[:, 1024:1024 + H1]
                                      ).then_inc(cx.sem["dve"], 1)
                        cx.bump("dve")
                        v.tensor_copy(out=mq[:], in_=P0[:, 1536:1536 + H1]
                                      ).then_inc(cx.sem["dve"], 1)
                        cx.bump("dve")
                    block.vector(nb_v)
                    nbv_c = cx.cnt["dve"]

                    def nexp(sync):
                        cx.w(sync, "dve", nbv_c)
                        sync.dma_start(out=cinnv[:, 0:H1],
                                       in_=mkb[:]).then_inc(cx.sem["dA"], 16)
                    block.sync(nexp)

                def ag1(g):
                    g.wait_ge(cx.sem["dA"], nrm_exp[1])
                    g.collective_compute(
                        "AllGather", Alu.bypass,
                        replica_groups=[[0, 1, 2, 3], [4, 5, 6, 7]],
                        ins=[cc_inn], outs=[cc_outn],
                    ).then_inc(cx.sem["cc"], 1)
                    cx.bump("cc")
                    for g_ in range(NG):
                        g.wait_ge(cx.sem[GSEM[g_ % 4]], grp_full[g_])
                        g.collective_compute(
                            "AllGather", Alu.bypass,
                            replica_groups=[[0, 1, 2, 3], [4, 5, 6, 7]],
                            ins=[cc_ing[g_]], outs=[cc_outg[g_]],
                        ).then_inc(cx.sem["cc"], 1)
                        cx.bump("cc")
                block.gpsimd(ag1)
                ccn_done = cx.cnt["cc"] - NG
                ccg_done = [ccn_done + 1 + g_ for g_ in range(NG)]
                cc1_done = cx.cnt["cc"]

                with (
                    nc.sbuf_tensor(f"s{sidx}_kbuf", [128, 2 * S1], BF16) as kbuf,
                    nc.sbuf_tensor(f"s{sidx}_vbuf", [128, 2 * S1], BF16) as vbuf,
                    nc.sbuf_tensor(f"s{sidx}_pT", [128, 2 * TPC], BF16) as pT,
                    nc.sbuf_tensor(f"s{sidx}_rl", [1, TPC], F32) as rl,
                    nc.sbuf_tensor(f"s{sidx}_rlb", [128, TPC], F32) as rlb,
                    nc.sbuf_tensor(f"s{sidx}_mk4", [128, 4 * H1], BF16) as mk4,
                ):
                    def mkld(sync):
                        cx.w(sync, "cc", ccn_done)
                        cx.w(sync, "pe", pe_proj_end)
                        for r in range(4):
                            sync.dma_start(
                                out=mk4[:, r * H1:(r + 1) * H1],
                                in_=connv(r)[:, 0:H1]
                                ).then_inc(cx.sem["dA"], 16)
                            cx.bump("dA", 16)
                    block.sync(mkld)
                    mk_c = cx.cnt["dA"]

                    def negm_v(v):
                        cx.w(v, "dA", mk_c)
                        for r in range(1, 4):
                            v.tensor_tensor(out=mk4[:, 0:H1], in0=mk4[:, 0:H1],
                                            in1=mk4[:, r * H1:(r + 1) * H1],
                                            op=Alu.max).then_inc(cx.sem["dve"], 1)
                            cx.bump("dve")
                            v.drain()
                        v.tensor_copy(out=negM[:], in_=mk4[:, 0:H1]
                                      ).then_inc(cx.sem["dve"], 1)
                        cx.bump("dve")
                        v.drain()
                        v.tensor_tensor(out=negM[:], in0=negM[:], in1=mq[:],
                                        op=Alu.mult).then_inc(cx.sem["dve"], 1)
                        cx.bump("dve")
                        v.drain()
                    block.vector(negm_v)
                    c_nm1 = cx.cnt["dve"]

                    def negm_a(a):
                        cx.w(a, "dve", c_nm1)
                        a.activation(out=negM[:], in_=negM[:], func=Act.Sqrt,
                                     bias=0.0, scale=1.0
                                     ).then_inc(cx.sem["act"], 1)
                        cx.bump("act")
                    block.scalar(negm_a)
                    c_nm2 = cx.cnt["act"]

                    def negm_v2(v):
                        cx.w(v, "act", c_nm2)
                        v.tensor_scalar(out=negM[:], in0=negM[:], scalar1=-1.0,
                                        scalar2=None, op0=Alu.mult
                                        ).then_inc(cx.sem["dve"], 1)
                        cx.bump("dve")
                        v.drain()
                    block.vector(negm_v2)
                    negm_c = cx.cnt["dve"]

                    NJT = S1 // 128
                    peQ = {}
                    peL = {}
                    peAV = {}
                    peRB = {}
                    actE = {}
                    act_cB = cx.cnt["act"]
                    for h in range(H1):
                        for jt in range(NJT):
                            act_cB += 1
                            actE[(h, jt)] = act_cB
                    dve_tail = {}
                    dve_cT = cx.cnt["dve"]
                    for h in range(H1):
                        dve_cT += 3
                        dve_tail[h] = dve_cT
                    kv_plan = {}
                    tmpc = {0: cx.cnt["hw0"], 1: cx.cnt["hw1"]}
                    for h in range(H1):
                        tmpc[h % 2] += 128
                        kv_plan[h] = (f"hw{h % 2}", tmpc[h % 2])

                    def core_pe(t):
                        cx.w(t, "dve", negm_c)

                        def emit_qk(t, h, jt):
                            n_ = h * NJT + jt
                            if n_ >= 2:
                                pj = n_ - 2
                                t.wait_ge(cx.sem["act"],
                                          actE[(pj // NJT, pj % NJT)])
                            kap = kbuf[:, (h % 2) * S1 + jt * 128:
                                       (h % 2) * S1 + jt * 128 + 128]
                            t.ldweights(kap)
                            for ci, (co, cn) in enumerate(TC):
                                ii = t.matmul(
                                    P0[:, (jt % 2) * 1024 + ci * 512:
                                       (jt % 2) * 1024 + ci * 512 + cn],
                                    kap,
                                    qf[:, h * TPC + co:h * TPC + co + cn],
                                    start=True, stop=True)
                            ii.then_inc(cx.sem["pe"], 1)
                            cx.bump("pe")
                            peQ[(h, jt)] = cx.cnt["pe"]

                        def emit_lav(t, h, jt):
                            t.wait_ge(cx.sem["act"], actE[(h, jt)])
                            t.ldweights(onec_b_t[:])
                            for ci, (co, cn) in enumerate(TC):
                                ii = t.matmul(
                                    P2[0:1, ci * 512:ci * 512 + cn],
                                    onec_b_t[:],
                                    pT[:, (jt % 2) * TPC + co:
                                       (jt % 2) * TPC + co + cn],
                                    start=(jt == 0), stop=(jt == NJT - 1))
                            ii.then_inc(cx.sem["pe"], 1)
                            cx.bump("pe")
                            peL[(h, jt)] = cx.cnt["pe"]
                            vap = vbuf[:, (h % 2) * S1 + jt * 128:
                                       (h % 2) * S1 + jt * 128 + 128]
                            t.ldweights(vap)
                            for ci, (co, cn) in enumerate(TC):
                                ii = t.matmul(
                                    P1[:, co:co + cn], vap,
                                    pT[:, (jt % 2) * TPC + co:
                                       (jt % 2) * TPC + co + cn],
                                    start=(jt == 0), stop=(jt == NJT - 1))
                            ii.then_inc(cx.sem["pe"], 1)
                            cx.bump("pe")
                            peAV[(h, jt)] = cx.cnt["pe"]

                        for h in range(H1):
                            t.wait_ge(cx.sem[kv_plan[h][0]], kv_plan[h][1])
                            if h > 0:
                                t.wait_ge(cx.sem["dve"], dve_tail[h - 1])
                            for jt in range(NJT):
                                emit_qk(t, h, jt)
                                if jt >= 1:
                                    emit_lav(t, h, jt - 1)
                            emit_lav(t, h, NJT - 1)
                            t.wait_ge(cx.sem["dve"], dve_tail[h] - 2)
                            for ci, (co, cn) in enumerate(TC):
                                ii = t.matmul(P0[:, co:co + cn], ones32_t[:],
                                              rl[0:1, co:co + cn],
                                              start=True, stop=True)
                            ii.then_inc(cx.sem["pe"], 1)
                            cx.bump("pe")
                            peRB[h] = cx.cnt["pe"]
                    block.tensor(core_pe)
                    core_pe_end = cx.cnt["pe"]

                    def kvld(sync):
                        cx.w(sync, "pe", pe_proj_end)
                        for s_, c_ in hw2_all.items():
                            cx.w(sync, s_, c_)
                        cx.w(sync, "dA", nrm_exp[1])
                        for h in range(H1):
                            s_ = f"hw{h % 2}"
                            g_ = h // NHG
                            hj = h % NHG
                            if h >= 2:
                                cx.w(sync, "pe", peAV[(h - 2, NJT - 1)])
                            cx.w(sync, "cc", ccg_done[g_])
                            for r in range(4):
                                sync.dma_start(
                                    out=kbuf[:, (h % 2) * S1 + r * TPC:
                                            (h % 2) * S1 + (r + 1) * TPC],
                                    in_=cogv(g_, r)[:, hj * TPC:(hj + 1) * TPC]
                                    ).then_inc(cx.sem[s_], 16)
                                sync.dma_start(
                                    out=vbuf[:, (h % 2) * S1 + r * TPC:
                                            (h % 2) * S1 + (r + 1) * TPC],
                                    in_=cogv(g_, r)[:, (NHG + hj) * TPC:
                                                    (NHG + hj + 1) * TPC]
                                    ).then_inc(cx.sem[s_], 16)
                                cx.bump(s_, 32)
                            assert cx.cnt[s_] == kv_plan[h][1]
                    block.sync(kvld)

                    def core_act(a):
                        cx.w(a, "dve", negm_c)
                        for h in range(H1):
                            for jt in range(NJT):
                                n_ = h * NJT + jt
                                cx.w(a, "pe", peQ[(h, jt)])
                                if n_ >= 2:
                                    pj = n_ - 2
                                    cx.w(a, "pe", peAV[(pj // NJT, pj % NJT)])
                                a.activation(
                                    out=pT[:, (jt % 2) * TPC:(jt % 2) * TPC + TPC],
                                    in_=P0[:, (jt % 2) * 1024:
                                           (jt % 2) * 1024 + TPC],
                                    func=Act.Exp, bias=negM[:, h:h + 1],
                                    scale=1.0).then_inc(cx.sem["act"], 1)
                    cx.cnt["act"] = act_cB
                    block.scalar(core_act)

                    def core_dve(v):
                        for h in range(H1):
                            v.wait_ge(cx.sem["pe"], peL[(h, NJT - 1)])
                            v.tensor_copy(out=rl[0:1, 0:TPC],
                                          in_=P2[0:1, 0:TPC]
                                          ).then_inc(cx.sem["dve"], 1)
                            cx.bump("dve")
                            v.wait_ge(cx.sem["pe"], peRB[h])
                            v.reciprocal(out=rlb[:], in_=P0[:, 0:TPC]
                                         ).then_inc(cx.sem["dve"], 1)
                            cx.bump("dve")
                            v.wait_ge(cx.sem["dve"], cx.cnt["dve"])
                            v.wait_ge(cx.sem["pe"], peAV[(h, NJT - 1)])
                            v.tensor_tensor(out=oTn[:, h * TPC:(h + 1) * TPC],
                                            in0=P1[:, 0:TPC], in1=rlb[:],
                                            op=Alu.mult).then_inc(cx.sem["dve"], 1)
                            cx.bump("dve")
                            v.drain()
                    block.vector(core_dve)
                    assert cx.cnt["dve"] == dve_cT
                    core_dve_end = cx.cnt["dve"]
                    core_act_end = cx.cnt["act"]

                with (
                    nc.sbuf_tensor(f"s{sidx}_wob", [128, 3 * 4 * D], BF16) as wob,
                    nc.sbuf_tensor(f"s{sidx}_sso", [128, 2 * D], BF16) as sso,
                    nc.sbuf_tensor(f"s{sidx}_bo", [1, D], BF16) as bo_t,
                ):
                    EQ = [list(range(q * 4, min(H1, q * 4 + 4)))
                          for q in range((H1 + 3) // 4)]
                    bo_ld = ("dqB", cx.cnt["dqB"] + 16)
                    cx.cnt["dqB"] += 16
                    et_loaded = {}
                    for et in range(H1):
                        s_ = f"dq{et % 3}"
                        et_loaded[et] = (s_, cx.cnt[s_] + 32)
                        cx.cnt[s_] += 32

                    osched = []
                    for et in range(min(12, H1)):
                        osched.append(("dq", et))
                    for qi_ in range(len(EQ)):
                        for ts_ in range(TT):
                            osched.append(("ev", qi_, ts_))
                        et2 = 12 + qi_ * 4
                        for e2 in range(et2, min(et2 + 4, H1)):
                            osched.append(("dq", e2))
                    dve_dqo = {}
                    dve_oev = {}
                    dve_cO = cx.cnt["dve"]
                    for item in osched:
                        dve_cO += 1
                        if item[0] == "dq":
                            dve_dqo[item[1]] = dve_cO
                        else:
                            dve_oev[(item[1], item[2])] = dve_cO

                    pe_o = {}

                    def opj_pe(t):
                        cx.w(t, "dve", core_dve_end)
                        cx.w(t, "act", core_act_end)
                        for qi_, eq in enumerate(EQ):
                            for e in eq:
                                t.wait_ge(cx.sem["dve"], dve_dqo[e])
                            if qi_ == 0:
                                t.wait_ge(cx.sem["dqB"], bo_ld[1])
                            for ts_ in range(TT):
                                prev = ((qi_, ts_ - 1) if ts_ > 0 else
                                        ((qi_ - 1, TT - 1) if qi_ > 0 else None))
                                if prev is not None:
                                    t.wait_ge(cx.sem["dve"], dve_oev[prev])
                                for gi, e in enumerate(eq):
                                    oap = oTn[:, e * TPC + ts_ * 128:
                                              e * TPC + ts_ * 128 + 128]
                                    t.ldweights(oap)
                                    stop_ = (qi_ != 0) and (gi == len(eq) - 1)
                                    for co, cn in DC:
                                        ii = t.matmul(
                                            P0[:, co:co + cn], oap,
                                            wob[:, (qi_ % 3) * 4 * D +
                                                gi * D + co:
                                                (qi_ % 3) * 4 * D +
                                                gi * D + co + cn],
                                            start=(gi == 0), stop=stop_)
                                if qi_ == 0:
                                    t.ldweights(ones_t[:])
                                    for co, cn in DC:
                                        ii = t.matmul(P0[:, co:co + cn],
                                                      ones_t[:],
                                                      bo_t[0:1, co:co + cn],
                                                      start=False, stop=True)
                                ii.then_inc(cx.sem["pe"], 1)
                                cx.bump("pe")
                                pe_o[(qi_, ts_)] = cx.cnt["pe"]
                    block.tensor(opj_pe)
                    opj_pe_end = cx.cnt["pe"]

                    def opj_v(v):
                        cx.w(v, "dve", core_dve_end)
                        for item in osched:
                            if item[0] == "dq":
                                et = item[1]
                                v.wait_ge(cx.sem[et_loaded[et][0]],
                                          et_loaded[et][1])
                                if et >= 12:
                                    v.wait_ge(cx.sem["pe"],
                                              pe_o[((et - 12) // 4, TT - 1)])
                                sl = ((et // 4) % 3) * 4 * D + (et % 4) * D
                                v.tensor_tensor(
                                    out=wob[:, sl:sl + D],
                                    in0=wob[:, sl:sl + D],
                                    in1=sso[:, (et % 2) * D:(et % 2) * D + D],
                                    op=Alu.mult).then_inc(cx.sem["dve"], 1)
                                cx.bump("dve")
                            else:
                                qi_, ts_ = item[1], item[2]
                                cx.w(v, "pe", pe_o[(qi_, ts_)])
                                xt = x_res[:, ts_ * D:(ts_ + 1) * D]
                                v.tensor_tensor(out=xt, in0=P0[:, 0:D], in1=xt,
                                                op=Alu.add
                                                ).then_inc(cx.sem["dve"], 1)
                                cx.bump("dve")
                    block.vector(opj_v)
                    assert cx.cnt["dve"] == dve_cO

                    def opj_st(g):
                        cx.w(g, "pe", core_pe_end)
                        cx.w(g, "dve", core_dve_end)
                        g.dma_start(out=bo_t[:], in_=b_w1_d["bo"]).then_inc(
                            cx.sem["dqB"], 16)
                        for et in range(H1):
                            if et >= 12:
                                g.wait_ge(cx.sem["pe"],
                                          pe_o[((et - 12) // 4, TT - 1)])
                            if et >= 2:
                                g.wait_ge(cx.sem["dve"], dve_dqo[et - 2])
                            sl = ((et // 4) % 3) * 4 * D + (et % 4) * D
                            g.dma_start(out=wob[:, sl:sl + D],
                                        in_=c_w1_d["wo"][et]).then_inc(
                                cx.sem[et_loaded[et][0]], 16)
                            g.dma_start(
                                out=sso[:, (et % 2) * D:(et % 2) * D + D],
                                in_=s_w1_d["wo"][et]).then_inc(
                                cx.sem[et_loaded[et][0]], 16)
                    block.gpsimd(opj_st)

            for ts_ in range(TT):
                x_ok[ts_] = ("dve", dve_oev[(len(EQ) - 1, ts_)])


        with nc.psum_tensor("P2", [128, 1024], F32) as P2:
            P2box[0] = P2
            for sidx, name in enumerate(subs):
                if name in ("ffn0a", "ffn0b"):
                    sublayer_ffn0(sidx)
                elif name == "attn0":
                    sublayer_attn0(sidx)
                elif name in ("ffn1a", "ffn1b"):
                    sublayer_ffn1(sidx, first=("wq_d" not in f1_state))
                elif name == "attn1":
                    sublayer_attn1(sidx)

            def store(sync):
                for tt in range(TT):
                    sem0, c0 = x_ok[tt]
                    cx.w(sync, sem0, c0)
                    sync.dma_start(out=y_out[tt],
                                   in_=x_res[:, tt * D:(tt + 1) * D]
                                   ).then_inc(cx.sem["dO"], 16)
                    cx.bump("dO", 16)
                cx.w(sync, "dO", cx.cnt["dO"])
            block.sync(store)

    return nc, taps


def kernel(**inputs):
    cfg = make_cfg(full=True)
    in_maps = host_prepare(cfg, inputs)
    nc, _ = build(cfg)
    res = run_bass_kernel_spmd(nc, in_maps, core_ids=list(range(NCORES)))
    B, S, D, TPC = cfg["B"], cfg["S"], cfg["D"], cfg["TPC"]
    out = np.empty((B * S, D), np.float32)
    for c in range(NCORES):
        out[c * TPC:(c + 1) * TPC] = res.results[c]["y_out"].reshape(TPC, D)
    return out.reshape(B, S, D)
